# version 08
# speedup vs baseline: 1.1279x; 1.0059x over previous
import numpy as np
from contextlib import ExitStack

import concourse.bass as bass
import concourse.mybir as mybir
from concourse.bass_utils import run_bass_kernel_spmd
from concourse.aot_env import allow_ici_all_to_all

F32 = mybir.dt.float32
BF16 = mybir.dt.bfloat16
AF = mybir.ActivationFunctionType
ALU = mybir.AluOpType

T = 2048
D = 2048
DIN = 4096
NH = 64
HD = 64
NST = 128
CONVD = DIN + 2 * NST
DPROJ = 2 * DIN + 2 * NST + NH
CH = 128
NCH = T // CH
RMS_EPS = 1e-5
LN_EPS = 1e-5
NCORES = 8

ENGINES = ("sync", "tensor", "vector", "scalar", "gpsimd")
DMA_SEMS = {"LD", "LG", "LW", "LZ", "LV", "LU", "LX", "LY",
             "SS", "SZ", "SG"}


class Prog:
    def __init__(self, nc):
        self.nc = nc
        self.q = {e: [] for e in ENGINES}
        self.stack = ExitStack()
        self.sems = {}
        self.semv = {}

    def sem(self, name):
        if name not in self.sems:
            self.sems[name] = self.stack.enter_context(self.nc.semaphore(name))
            self.semv[name] = 0
        return self.sems[name]

    def op(self, eng, fn, inc=None, waits=()):
        h = self.sems
        wl = []
        for w in waits:
            if w is None:
                continue
            sn, v = w
            if sn in DMA_SEMS:
                v = self.semv[sn]
            wl.append((sn, v))
        seen = {}
        for sn, v in wl:
            seen[sn] = max(v, seen.get(sn, -1))
        wl = sorted(seen.items())

        def play(e, fn=fn, wl=wl, inc=inc):
            for (sn, v) in wl:
                e.wait_ge(h[sn], v)
            inst = fn(e)
            if inc is not None:
                inst = inst.then_inc(h[inc[0]], inc[1])
            return inst

        self.q[eng].append(play)
        if inc is not None:
            self.semv[inc[0]] += inc[1]
            return (inc[0], self.semv[inc[0]])
        return None

    def wait(self, eng, *toks):
        h = self.sems
        tl = []
        for t in toks:
            if t is None:
                continue
            sn, v = t
            if sn in DMA_SEMS:
                v = self.semv[sn]
            tl.append((sn, v))
        seen = {}
        for sn, v in tl:
            seen[sn] = max(v, seen.get(sn, -1))
        tl = sorted(seen.items())
        if not tl:
            return

        def play(e, toks=tl):
            for (sn, v) in toks:
                e.wait_ge(h[sn], v)

        self.q[eng].append(play)

    def dma(self, eng, out, in_, semname, waits=(), slow=False):
        waits = tuple(waits) + ((semname, self.semv[semname]),)
        return self.op(eng,
                       lambda e: e.dma_start(out=out, in_=in_,
                                             allow_slow_non_contiguous=slow),
                       inc=(semname, 16), waits=waits)

    def light_barrier(self):
        self.wait("vector", ("VE", self.semv["VE"]))
        self.wait("scalar", ("SC", self.semv["SC"]))
        self.wait("tensor", ("TE", self.semv["TE"]))
        self.wait("gpsimd", ("GE", self.semv["GE"]))
        for e in ENGINES:
            self.op(e, lambda en: en.nop(), inc=("PH", 1))
        v = self.semv["PH"]
        for e in ENGINES:
            self.wait(e, ("PH", v))

    def barrier(self):
        self.wait("sync", ("SS", self.semv["SS"]), ("SZ", self.semv["SZ"]),
                  ("LD", self.semv["LD"]))
        self.wait("gpsimd", ("SG", self.semv["SG"]), ("LG", self.semv["LG"]),
                  ("LW", self.semv["LW"]), ("LZ", self.semv["LZ"]),
                  ("LV", self.semv["LV"]), ("LU", self.semv["LU"]))
        self.wait("vector", ("VE", self.semv["VE"]))
        self.wait("scalar", ("SC", self.semv["SC"]))
        self.wait("tensor", ("TE", self.semv["TE"]))
        self.wait("gpsimd", ("GE", self.semv["GE"]))
        for e in ENGINES:
            self.op(e, lambda en: en.nop(), inc=("PH", 1))
        v = self.semv["PH"]
        for e in ENGINES:
            self.wait(e, ("PH", v))

    def playback(self):
        with self.nc.Block() as block:
            for e in ENGINES:
                ops = self.q[e]
                getattr(block, e)(lambda eng, ops=ops: [f(eng) for f in ops])


class Arena:
    def __init__(self, nc, name, kb):
        self.n = kb * 1024 // 2
        self.t = nc.alloc_sbuf_tensor(name, [128, self.n], BF16)
        self.off = 0
        self.marks = []

    def push(self):
        self.marks.append(self.off)

    def pop(self):
        self.off = self.marks.pop()

    def bf16(self, cols):
        c = (cols + 63) & ~63
        assert self.off + c <= self.n, f"arena overflow {self.off}+{c}>{self.n}"
        ap = self.t.ap()[:, self.off:self.off + cols]
        self.off += c
        return ap

    def f32(self, cols):
        return self.bf16(2 * cols).bitcast(F32)


def rr(ap_, spec, **kw):
    return ap_.rearrange(spec, **kw)


def build_program(taps=()):
    taps = set(taps)
    nc = bass.Bass("TRN2", target_bir_lowering=False)
    p = Prog(nc)
    for name in ("LD", "LG", "TE", "VE", "SC", "GE", "SS", "SZ", "SG", "CC",
                 "PH", "LW", "LZ", "LV", "LU", "LX", "LY"):
        p.sem(name)

    def ext(name, shape, dtype=F32):
        return nc.dram_tensor(name, list(shape), dtype, kind="ExternalInput").ap()

    u = ext("u", [T, D])
    w_in = ext("w_in", [D, DPROJ])
    conv_wt = ext("conv_wt", [CONVD, 4])
    conv_bc = ext("conv_bc", [CONVD, 1])
    dtb_c = ext("dtb_c", [NH, 1])
    alog_c = ext("alog_c", [NH, 1])
    d_row = ext("d_row", [1, NH])
    ns_row = ext("ns_row", [1, DIN])
    w_out0 = ext("w_out0", [DIN, D])
    a_all = ext("a_all", [D, 512])
    b_all = ext("b_all", [512, D])
    wr0 = ext("wr0", [D, 8])
    w_qkv_t = ext("w_qkv_t", [D, 3 * D])
    w_out1_t = ext("w_out1_t", [D, D])
    onesw1 = ext("onesw1", [D, 2])
    wq_my = ext("wq_my", [D, 256])
    wk_my = ext("wk_my", [D, 128])
    wv_my = ext("wv_my", [D, 128])
    bq_my = ext("bq_my", [1, 256])
    bk_my = ext("bk_my", [1, 128])
    bv_my = ext("bv_my", [1, 128])
    wo1 = ext("wo1", [D, D])
    w_gate = ext("w_gate", [D, 8192])
    w_up = ext("w_up", [D, 8192])
    w_down = ext("w_down", [8192, D])
    ln1_c = ext("ln1_c", [D, 1])
    ln2_c = ext("ln2_c", [D, 1])
    ident_f = ext("ident_f", [128, 128])
    ident_b = ext("ident_b", [128, 128], BF16)
    ident_t = ext("ident_t", [128, 512], BF16)
    negmask = ext("negmask", [128, 128], BF16)
    tril_in = ext("tril_in", [128, 128], BF16)
    iota_r = ext("iota_r", [1, 256])
    ind64 = ext("ind64", [NH, NH * CH], BF16)
    ropetab = ext("ropetab", [T, 128], BF16)
    ones_rb = ext("ones_rb", [1, 128], BF16)
    ones_cb = ext("ones_cb", [128, 1], BF16)

    out = nc.dram_tensor("out", [T, D], F32, kind="ExternalOutput").ap()
    tap_out = {}
    tap_specs = {"xfm1": [D, T], "xfm2": [D, T], "xfm3": [D, T],
                 "zx": [CONVD, T], "dt": [NH, T], "stats": [2, T],
                 "yhat": [NCH, 128, DIN], "zz": [T, DIN]}
    for name in taps:
        tap_out[name] = nc.dram_tensor("tap_" + name, tap_specs[name], F32,
                                       kind="ExternalOutput").ap()

    xfm_h = nc.dram_tensor("xfm_h", [D, T], F32).ap()
    xbc_h = nc.dram_tensor("xbc_h", [CONVD, T], F32).ap()
    z_h = nc.dram_tensor("z_h", [T, DIN], BF16).ap()
    yh_h = nc.dram_tensor("yh_h", [NCH, 128, DIN], BF16).ap()
    at_h = nc.dram_tensor("at_h", [D, T], BF16).ap()
    xsf_h = nc.dram_tensor("xsf_h", [D, T], BF16).ap()
    xh4_h = nc.dram_tensor("xh4_h", [T, D], BF16).ap()
    rowf_h = nc.dram_tensor("rowf_h", [1, T], F32).ap()
    rowb_h = nc.dram_tensor("rowb_h", [1, T], BF16).ap()
    ag1_i = nc.dram_tensor("ag1_i", [256, 2176], BF16).ap()
    ag1_o = nc.dram_tensor("ag1_o", [2048, 2176], BF16).ap()
    a2a_i = nc.dram_tensor("a2a_i", [2048, 256], BF16).ap()
    a2a_o = nc.dram_tensor("a2a_o", [2048, 256], BF16).ap()

    sb = nc.alloc_sbuf_tensor
    c_idf = sb("c_idf", [128, 128], F32)
    c_idb = sb("c_idb", [128, 128], BF16)
    c_idt = sb("c_idt", [128, 512], BF16)
    c_ngm = sb("c_ngm", [128, 128], BF16)
    c_tri = sb("c_tri", [128, 128], BF16)
    c_orb = sb("c_orb", [1, 128], BF16)
    c_ocb = sb("c_ocb", [128, 1], BF16)
    c_dbc = sb("c_dbc", [128, NH], BF16)
    c_nsb = sb("c_nsb", [128, DIN], BF16)
    c_cw = sb("c_cw", [128, 34 * 4], F32)
    c_cb = sb("c_cb", [128, 34], F32)
    c_sm = sb("c_sm", [128, 32], F32)
    drhs = sb("drhs", [65, NH * CH], BF16)
    dlhs_hi = sb("dlhs_hi", [65, CH], BF16)
    dlhs_lo = sb("dlhs_lo", [65, CH], BF16)
    xin = sb("xin", [128, 2048], F32)
    wst = [sb(f"wst{i}", [128, 8192], BF16) for i in range(2)]
    dtw = sb("dtw", [NH, T], F32)
    sgi_r = sb("sgi_r", [1, T], BF16)
    msg_r = sb("msg_r", [1, T], BF16)
    sx_r = sb("sx_r", [1, T], F32)
    sxx_r = sb("sxx_r", [1, T], F32)
    sco_r = sb("sco_r", [1, T], F32)
    sxw_r = sb("sxw_r", [1, T], F32)
    scr_r = sco_r
    srow = sb("srow", [1, 512], F32)
    scol = sb("scol", [128, 32], F32)
    AR = Arena(nc, "AR", 84)

    PS = [nc.alloc_psum_tensor(f"ps{i}", [128, 512], F32) for i in range(8)]

    def ps(i):
        return PS[i].ap()

    def psb(i):
        return PS[i].ap().bitcast(BF16)

    def mm(pst, lhsT, rhs, first, last, waits=(), inc="dflt"):
        if inc == "dflt":
            inc = ("TE", 1) if last else None
        return p.op("tensor",
                    lambda e: e.matmul(pst, lhsT, rhs, start=first, stop=last),
                    inc=inc, waits=waits)

    def tr(pst, src, idap, waits=(), inc=None):
        return p.op("tensor", lambda e: e.transpose(pst, src, idap),
                    inc=inc, waits=waits)

    def vop(fn, waits=(), inc=("VE", 1)):
        return p.op("vector", fn, inc=inc, waits=waits)

    def sop(fn, waits=(), inc=("SC", 1)):
        return p.op("scalar", fn, inc=inc, waits=waits)

    def vcopy(outap, inap, waits=()):
        return vop(lambda e: e.tensor_copy(outap, inap), waits=waits)

    def vtt(outap, a, b, alu, waits=()):
        return vop(lambda e: e.tensor_tensor(outap, a, b, op=alu), waits=waits)

    def vts(outap, inap, s1, s2, op0, op1=None, waits=()):
        if op1 is None:
            return vop(lambda e: e.tensor_scalar(outap, inap, s1, None,
                                                 op0=op0), waits=waits)
        return vop(lambda e: e.tensor_scalar(outap, inap, s1, s2, op0=op0,
                                             op1=op1), waits=waits)

    def vstt(outap, in0, scalar, in1, op0, op1, waits=()):
        return vop(lambda e: e.scalar_tensor_tensor(outap, in0, scalar, in1,
                                                    op0=op0, op1=op1),
                   waits=waits)

    def sact(outap, inap, func, bias=None, scale=None, accum=None, waits=()):
        kw = {}
        if bias is not None:
            kw["bias"] = bias
        if scale is not None:
            kw["scale"] = scale
        if accum is not None:
            kw["accum_out"] = accum
        return sop(lambda e: e.activation(outap, inap, func, **kw),
                   waits=waits)

    def xfm_v(tc, h=xfm_h):
        return rr(h, "(c p) t -> p c t", p=128)[:, :, tc * 128:(tc + 1) * 128]

    def sb3(ap_, c=16):
        return rr(ap_, "p (c q) -> p c q", c=c)

    def bc_in1(psum_slice, reps):
        return psum_slice.unsqueeze(1).to_broadcast(
            [128, reps, psum_slice.shape[1]])

    for dst, src in [(c_idf, ident_f), (c_idb, ident_b), (c_idt, ident_t),
                     (c_ngm, negmask), (c_tri, tril_in), (c_orb, ones_rb),
                     (c_ocb, ones_cb)]:
        p.dma("sync", dst.ap(), src, "LD")
    p.dma("sync", drhs.ap()[1:65, :], ind64, "LD")
    p.dma("sync", dlhs_hi.ap()[0:1, :], ones_rb, "LD")
    p.dma("sync", dlhs_lo.ap()[0:1, :], ones_rb, "LD")
    p.dma("sync", rr(c_cw.ap(), "p (c k) -> p c k", c=34),
          rr(conv_wt, "(c p) k -> p c k", p=128), "LD")
    p.dma("sync", c_cb.ap().unsqueeze(2),
          rr(conv_bc, "(c p) k -> p c k", p=128), "LD", slow=True)
    p.dma("sync", c_sm.ap()[0:NH, 0:1], dtb_c, "LD")
    tld_sm = p.dma("sync", c_sm.ap()[0:NH, 1:2], alog_c, "LD")
    p.dma("gpsimd", c_dbc.ap(), d_row.broadcast_to([128, NH]), "LG")
    p.dma("gpsimd", c_nsb.ap(), ns_row.broadcast_to([128, DIN]), "LG")
    tk = sact(c_sm.ap()[0:NH, 2:3], c_sm.ap()[0:NH, 1:2], AF.Exp,
              waits=(tld_sm,))
    tok_negA = vts(c_sm.ap()[0:NH, 1:2], c_sm.ap()[0:NH, 2:3], -1.0, None,
                   ALU.mult, waits=(tk,))

    def stats_chunk(src_ap, tc, xb, sq, ready=()):
        cast = vcopy(xb, src_ap, waits=ready)
        sqt = sact(sq, src_ap, AF.Square, waits=ready)
        mt = None
        for dc in range(16):
            mm(ps(4)[0:1, 0:128], c_ocb.ap(), xb[:, dc * 128:(dc + 1) * 128],
               dc == 0, dc == 15, waits=(cast,) if dc == 0 else (), inc=None)
            mt = mm(ps(6)[0:1, 0:128], c_ocb.ap(),
                    sq[:, dc * 128:(dc + 1) * 128], dc == 0, dc == 15,
                    waits=(sqt,) if dc == 0 else ())
        tcs = slice(tc * 128, (tc + 1) * 128)
        v1 = vcopy(sx_r.ap()[0:1, tcs], ps(4)[0:1, 0:128], waits=(mt,))
        v2 = vcopy(sxx_r.ap()[0:1, tcs], ps(6)[0:1, 0:128], waits=(mt,))
        return (v1, v2)

    def finalize_stats(ready=()):
        v = vts(sx_r.ap(), sx_r.ap(), 1.0 / D, None, ALU.mult, waits=ready)
        v2 = vts(sxx_r.ap(), sxx_r.ap(), 1.0 / D, None, ALU.mult,
                 waits=ready)
        sq = sact(scr_r.ap(), sx_r.ap(), AF.Square, waits=(v,))
        v3 = vtt(sxx_r.ap(), sxx_r.ap(), scr_r.ap(), ALU.subtract,
                 waits=(v2, sq))
        v3 = vts(sxx_r.ap(), sxx_r.ap(), LN_EPS, None, ALU.add, waits=(v3,))
        v3 = vop(lambda e: e.reciprocal(sxx_r.ap(), sxx_r.ap()), waits=(v3,))
        s1 = sact(sgi_r.ap(), sxx_r.ap(), AF.Sqrt, waits=(v3,))
        v4 = vtt(msg_r.ap(), sx_r.ap(), sgi_r.ap(), ALU.mult, waits=(s1,))
        return (s1, v4)

    AR.push()
    a_xb = AR.bf16(2048)
    a_sq = AR.bf16(2048)
    a_xo = AR.f32(2048)
    last_tt = None
    prev_toks = ()
    for tc in range(16):
        tl = p.dma("sync", xin.ap(), u[tc * 128:(tc + 1) * 128, :], "LD",
                   waits=(last_tt,))
        evs = []
        for g in range(4):
            tt = None
            for j in range(4):
                dc = g * 4 + j
                w = list(prev_toks) + [tl] if j == 0 else []
                tt = tr(ps(g)[:, j * 128:(j + 1) * 128],
                        xin.ap()[:, dc * 128:(dc + 1) * 128], c_idf.ap(),
                        waits=w, inc=("TE", 1) if j == 3 else None)
            evs.append(vcopy(a_xo[:, g * 512:(g + 1) * 512], ps(g),
                             waits=(tt,)))
        last_tt = tt
        p.wait("sync", *evs)
        st = p.dma("sync", xfm_v(tc), sb3(a_xo),
                   "SS" if tc % 2 == 0 else "SZ")
        sc = stats_chunk(a_xo, tc, a_xb, a_sq, ready=tuple(evs))
        prev_toks = (st, sc[0], sc[1])
    AR.pop()
    tok_stats = finalize_stats(ready=prev_toks)
    p.barrier()

    AR.push()
    xh1 = AR.bf16(16 * 2048)

    def build_xhat(xh_dst, extra=()):
        prev = None
        toks = []
        for tc in range(16):
            tl = p.dma("sync", xin.ap(), xfm_v(tc), "LD", waits=(prev,))
            b1 = mm(ps(5)[:, 0:128], c_orb.ap(),
                    sgi_r.ap()[0:1, tc * 128:(tc + 1) * 128], True, True,
                    waits=(prev,) + tuple(extra), inc=None)
            b2 = mm(ps(5)[:, 128:256], c_orb.ap(),
                    msg_r.ap()[0:1, tc * 128:(tc + 1) * 128], True, True)
            dst = rr(xh_dst, "p (c q) -> p c q", c=16)[:, :,
                                                       tc * 128:(tc + 1) * 128]
            v1 = vtt(dst, sb3(xin.ap()), bc_in1(ps(5)[:, 0:128], 16),
                     ALU.mult, waits=(tl, b2))
            prev = vtt(dst, dst, bc_in1(ps(5)[:, 128:256], 16),
                       ALU.subtract, waits=(v1,))
            toks.append(prev)
        return toks

    xh1_done = build_xhat(xh1, extra=tok_stats)
    tok_xh1 = xh1_done[-1]

    a_ev = AR.bf16(2 * 2048)
    a_cv = AR.bf16(2 * 2048)

    WSEMS = ("LW", "LZ", "LV", "LU")

    def wld(buf, m0, wdt, wsrc, waits=(), sem=None):
        return p.dma("gpsimd",
                     rr(wst[buf].ap()[:, 0:16 * wdt], "p (c m) -> p c m", c=16),
                     rr(wsrc, "(c p) m -> p c m", p=128)[:, :, m0:m0 + wdt],
                     sem if sem else ("LW" if buf == 0 else "LZ"),
                     waits=waits)

    wtoks = [None, None]
    wlast = [None, None]
    conv_done = None
    prev_cv = [None, None]
    prev_stripe_evs = []
    for si in range(17):
        m0 = DIN + si * 256
        buf = si % 2
        wtoks[buf] = wld(buf, m0, 256, w_in, waits=(wtoks[buf], wlast[buf]),
                         sem=WSEMS[si % 4])
        ev_toks = []
        for ms in range(2):
            for tq in range(4):
                j = ms * 4 + tq
                bank = j % 6
                lt = None
                for dc in range(16):
                    w = [wtoks[buf], tok_xh1] if dc == 0 else []
                    if dc == 0 and j >= 6:
                        w.append(ev_toks[j - 6])
                    if dc == 0 and j < 6 and prev_stripe_evs:
                        w.append(prev_stripe_evs[j + 6 if j < 2 else j])
                    lt = mm(ps(bank)[:, 0:512],
                            wst[buf].ap()[:, dc * 256 + ms * 128:
                                          dc * 256 + ms * 128 + 128],
                            xh1[:, dc * 2048 + tq * 512:
                                dc * 2048 + tq * 512 + 512],
                            dc == 0, dc == 15, waits=w)
                wlast[buf] = lt
                ev_toks.append(vcopy(
                    a_ev[:, ms * 2048 + tq * 512:ms * 2048 + tq * 512 + 512],
                    ps(bank)[:, 0:512], waits=(lt, prev_cv[ms])))
        st_toks = []
        ncv = [None, None]
        for ms in range(2):
            pc = si * 2 + ms
            src = a_ev[:, ms * 2048:(ms + 1) * 2048]
            dstc = a_cv[:, ms * 2048:(ms + 1) * 2048]
            w3 = c_cw.ap()[:, pc * 4 + 3:pc * 4 + 4]
            bcol = c_cb.ap()[:, pc:pc + 1]
            v = vts(dstc, src, w3, bcol, ALU.mult, ALU.add,
                    waits=tuple(ev_toks) + ((conv_done,) if conv_done else ()))
            for i in (2, 1, 0):
                sh = 3 - i
                wi = c_cw.ap()[:, pc * 4 + i:pc * 4 + i + 1]
                v = vstt(dstc[:, sh:2048], src[:, 0:2048 - sh], wi,
                         dstc[:, sh:2048], ALU.mult, ALU.add, waits=(v,))
            ncv[ms] = v
            sl = sact(dstc, dstc, AF.Silu, waits=(v,))
            p.wait("gpsimd", sl)
            st_toks.append(p.dma("gpsimd", xbc_h[pc * 128:(pc + 1) * 128, :],
                                 dstc, "SG"))
        conv_done = st_toks[-1]
        prev_cv = ncv
        prev_stripe_evs = ev_toks

    wdtok = wld(0, DPROJ - 64, 64, w_in, waits=(wtoks[0], wlast[0]))
    lt = None
    last_dt_mm = None
    for tq in range(4):
        ltq = None
        for dc in range(16):
            w = [wdtok, lt] if dc == 0 else []
            if dc == 0 and lt is None:
                w.append(prev_stripe_evs[3])
            ltq = mm(ps(3)[0:64, 0:512],
                     wst[0].ap()[:, dc * 64:dc * 64 + 64],
                     xh1[:, dc * 2048 + tq * 512:dc * 2048 + tq * 512 + 512],
                     dc == 0, dc == 15, waits=w)
        last_dt_mm = ltq
        lt = vcopy(dtw.ap()[:, tq * 512:(tq + 1) * 512], ps(3)[0:64, 0:512],
                   waits=(ltq,))
    spa = a_ev.bitcast(F32)[0:NH, 0:T]
    spb = a_cv.bitcast(F32)[0:NH, 0:T]
    s1 = sact(spa, dtw.ap(), AF.Abs, bias=c_sm.ap()[0:NH, 0:1],
              waits=(lt, tok_negA, conv_done))
    s2 = sact(spa, spa, AF.Exp, scale=-1.0, waits=(s1,))
    s3 = sact(spa, spa, AF.Ln, bias=1.0, waits=(s2,))
    v1 = vts(spb, dtw.ap(), c_sm.ap()[0:NH, 0:1], 0.0, ALU.add, ALU.max,
             waits=(lt, conv_done))
    tok_aw = vtt(dtw.ap(), spa, spb, ALU.add, waits=(s3, v1))

    a_zr = [AR.bf16(512), AR.bf16(512)]
    prev_z2 = [None, None]
    zev = [prev_stripe_evs[6], prev_stripe_evs[7], prev_stripe_evs[2], lt]
    wlast = [last_dt_mm, wlast[1]]
    for zs in range(8):
        buf = zs % 2
        wtoks[buf] = wld(buf, zs * 512, 512, w_in,
                         waits=(wtoks[buf], wlast[buf]), sem=WSEMS[zs % 4])
        for tc in range(16):
            bank = tc % 4
            ltz = None
            for dc in range(16):
                w = [wtoks[buf]] if dc == 0 else []
                if dc == 0:
                    w.append(zev[bank])
                ltz = mm(ps(bank)[:, 0:512],
                         xh1[:, dc * 2048 + tc * 128:dc * 2048 + tc * 128 + 128],
                         wst[buf].ap()[:, dc * 512:(dc + 1) * 512],
                         dc == 0, dc == 15, waits=w)
            wlast[buf] = ltz
            zb = tc % 2
            ev = vcopy(a_zr[zb], ps(bank)[:, 0:512],
                       waits=(ltz, prev_z2[zb]))
            zev[bank] = ev
            p.wait("sync", ev)
            prev_z2[zb] = p.dma("sync",
                                z_h[tc * 128:(tc + 1) * 128,
                                    zs * 512:(zs + 1) * 512],
                                a_zr[zb], "SS" if zb == 0 else "SZ")
    AR.pop()
    p.barrier()

    if taps & {"zx", "dt", "zz", "stats"}:
        AR.push()
        tb = AR.f32(4096)
        if "zx" in tap_out:
            pst = None
            for pc in range(34):
                tl = p.dma("sync", tb[:, 0:2048],
                           xbc_h[pc * 128:(pc + 1) * 128, :], "LD",
                           waits=(pst,))
                pst = p.dma("sync", tap_out["zx"][pc * 128:(pc + 1) * 128, :],
                            tb[:, 0:2048], "SS", waits=(tl,))
        p.barrier()
        if "zz" in tap_out:
            pst = None
            for tcc in range(16):
                tl = p.dma("gpsimd", tb[:, 0:4096],
                           z_h[tcc * 128:(tcc + 1) * 128, :], "LG",
                           waits=(pst,))
                pst = p.dma("gpsimd",
                            tap_out["zz"][tcc * 128:(tcc + 1) * 128, :],
                            tb[:, 0:4096], "SG", waits=(tl,))
        p.barrier()
        if "dt" in tap_out:
            p.dma("sync", tap_out["dt"], dtw.ap(), "SS")
        if "stats" in tap_out:
            p.barrier()
            tl = p.dma("gpsimd", tb[0:1, 0:2048], sgi_r.ap(), "LG")
            p.dma("gpsimd", tap_out["stats"][0:1, :], tb[0:1, 0:2048], "SG",
                  waits=(tl,))
            tl = p.dma("gpsimd", tb[0:1, 2048:4096], msg_r.ap(), "LG")
            p.dma("gpsimd", tap_out["stats"][1:2, :], tb[0:1, 2048:4096], "SG",
                  waits=(tl,))
        AR.pop()
        p.barrier()

    AR.push()
    a_xbc = AR.bf16(34 * 128)
    a_z = AR.bf16(4096)
    a_xtm = AR.bf16(4096)
    a_xti = AR.bf16(4096)
    a_xbr = AR.bf16(4096)
    a_mt = AR.bf16(NH * CH)
    a_st = AR.bf16(4096)
    a_stt = AR.bf16(512)
    a_y = AR.bf16(4096)
    a_yf = a_xtm
    a_sq2 = a_xtm
    a_acs = AR.f32(CH)
    a_acss = AR.f32(CH)
    a_ahw = AR.bf16(CH)
    a_alw = AR.f32(CH)
    a_stk = AR.f32(128)
    a_tm1 = AR.f32(128)
    a_dec = AR.f32(CH)
    a_eac = AR.f32(CH)
    a_gt = AR.bf16(128)
    a_btm = AR.bf16(128)
    a_dtb = AR.bf16(64)
    a_ect = AR.bf16(64)
    a_dct = AR.bf16(64)
    a_gmb = AR.f32(64)
    a_aw = AR.f32(CH)
    a_nhb = AR.bf16(CH)
    a_nlb = AR.bf16(CH)
    a_me = [AR.bf16(512) for _ in range(2)]
    srow_b = srow.ap().bitcast(BF16)

    prev_chunk = {"xbc": None, "z": None, "drhs": None, "y": None,
                  "acs": None, "yst": None}
    for ci in range(NCH):
        tsl = slice(ci * 128, (ci + 1) * 128)
        xl = []
        for part in range(6):
            lo = part * 6
            hi = min(34, lo + 6)
            xl.append(p.dma(
                "gpsimd", rr(a_xbc, "p (c q) -> p c q", c=34)[:, lo:hi, :],
                rr(xbc_h, "(c p) t -> p c t", p=128)[:, lo:hi, tsl],
                "LG", waits=(prev_chunk["xbc"],)))
        tok_xbc = xl[-1]
        tok_z = p.dma("sync", a_z, z_h[tsl, :], "LD",
                      waits=(prev_chunk["z"],))

        v = vts(a_aw[0:NH, :], dtw.ap()[:, tsl], c_sm.ap()[0:NH, 1:2], None,
                ALU.mult, waits=(tok_aw, prev_chunk["acs"]))
        v = vop(lambda e: e.tensor_tensor_scan(
            a_acs[0:NH, :], a_aw[0:NH, :], a_aw[0:NH, :], 0.0,
            op0=ALU.add, op1=ALU.bypass), waits=(v,))
        v = vts(a_acss[0:NH, :], a_acs[0:NH, :], a_acs[0:NH, 63:64], None,
                ALU.subtract, waits=(v,))
        vh = vcopy(a_ahw[0:NH, :], a_acss[0:NH, :], waits=(v,))
        vl = vtt(a_alw[0:NH, :], a_acss[0:NH, :], a_ahw[0:NH, :],
                 ALU.subtract, waits=(vh,))
        t_hi = vts(a_nhb[0:NH, :], a_ahw[0:NH, :], -1.0, None, ALU.mult,
                   waits=(vh, prev_chunk["drhs"]))
        t_hi = p.dma("sync", dlhs_hi.ap()[1:65, :], a_nhb[0:NH, :], "SS",
                     waits=(t_hi,))
        t_lo = vts(a_nlb[0:NH, :], a_alw[0:NH, :], -1.0, None, ALU.mult,
                   waits=(vl,))
        t_lo = p.dma("sync", dlhs_lo.ap()[1:65, :], a_nlb[0:NH, :], "SS",
                     waits=(t_lo,))
        a_alb = a_acss.bitcast(BF16)
        v_lob = vcopy(a_alb[0:NH, 0:CH], a_alw[0:NH, :], waits=(vl,))
        vd = vts(a_dec[0:NH, :], a_acs[0:NH, :], a_acs[0:NH, 127:128], -1.0,
                 ALU.subtract, ALU.mult, waits=(v,))
        sd = sact(a_dec[0:NH, :], a_dec[0:NH, :], AF.Exp, waits=(vd,))
        se = sact(a_eac[0:NH, :], a_acs[0:NH, :], AF.Exp, waits=(v,))
        c1 = vcopy(a_stk[0:64, :], dtw.ap()[:, tsl], waits=())
        d2 = p.dma("sync", a_stk[64:128, :], a_dec[0:NH, :], "SS", waits=(sd,))
        tt = tr(ps(6)[:, 0:128], a_stk, c_idf.ap(), waits=(c1, d2),
                inc=("TE", 1))
        e1 = vcopy(a_tm1, ps(6)[:, 0:128], waits=(tt,))
        cv_dt = vcopy(a_dtb, a_tm1[:, 0:64], waits=(e1,))
        cv_dc = vcopy(a_dct, a_tm1[:, 64:128], waits=(e1,))
        c2 = vcopy(a_stk[0:64, :], a_eac[0:NH, :], waits=(se, e1))
        tt = tr(ps(6)[:, 0:128], a_stk, c_idf.ap(), waits=(c2,),
                inc=("TE", 1))
        e2 = vcopy(a_tm1, ps(6)[:, 0:128], waits=(tt,))
        cv_ec = vcopy(a_ect, a_tm1[:, 0:64], waits=(e2,))
        d4 = p.dma("sync", srow.ap()[0:1, 0:64], a_eac[0:NH, 127:128], "SS",
                   waits=(se,))
        vgr = vcopy(srow_b[0:1, 256:320], srow.ap()[0:1, 0:64], waits=(d4,))
        bg = mm(ps(6)[:, 128:192], c_orb.ap(), srow_b[0:1, 256:320], True,
                True, waits=(vgr,))
        vgb = vcopy(a_gmb, ps(6)[:, 128:192], waits=(bg,))

        B_fm = a_xbc[:, 32 * 128:33 * 128]
        C_fm = a_xbc[:, 33 * 128:34 * 128]
        tb = tr(psb(6)[:, 768:896], B_fm, c_idb.ap(), waits=(tok_xbc,),
                inc=("TE", 1))
        eb = vcopy(a_btm, psb(6)[:, 768:896], waits=(tb,))
        tg = mm(ps(6)[:, 192:320], B_fm, C_fm, True, True, waits=(tok_xbc,))
        eg = vcopy(a_gt, ps(6)[:, 192:320], waits=(tg,))
        evx = []
        for g in range(8):
            tt = None
            for j in range(4):
                pc = g * 4 + j
                w = [xl[min(pc // 6, 5)]]
                if j == 0:
                    w.append(evx[-1] if evx else prev_chunk["y"])
                tt = tr(psb(7)[:, j * 128:(j + 1) * 128],
                        a_xbc[:, pc * 128:(pc + 1) * 128], c_idb.ap(),
                        waits=w, inc=("TE", 1) if j == 3 else None)
            evx.append(vcopy(a_xtm[:, g * 512:(g + 1) * 512],
                             psb(7)[:, 0:512],
                             waits=(tt, prev_chunk["yst"])))
        dt_ap = a_dtb.unsqueeze(2).to_broadcast([128, 64, 64])
        dc_ap = a_dct.unsqueeze(2).to_broadcast([128, 64, 64])
        vxt = vtt(rr(a_xti, "p (h q) -> p h q", h=64),
                  rr(a_xtm, "p (h q) -> p h q", h=64), dt_ap, ALU.mult,
                  waits=(evx[-1], cv_dt))
        vxb = vtt(rr(a_xbr, "p (h q) -> p h q", h=64),
                  rr(a_xti, "p (h q) -> p h q", h=64), dc_ap, ALU.mult,
                  waits=(vxt, cv_dc))

        mt_toks = []
        gate_banks = (evx[-1], eg, eb, vgb, cv_ec, cv_dt, cv_dc)
        tok_dmm = None
        for half in range(2):
            hh0 = half * 32
            rsl = slice(half * 32 * CH, (half + 1) * 32 * CH)
            fh = p.dma("sync", drhs.ap()[0:1, rsl],
                       a_ahw[hh0:hh0 + 32, :], "SS",
                       waits=(vh, prev_chunk["drhs"], tok_dmm))
            hi_last = None
            for g8 in range(8):
                g = half * 8 + g8
                cs = slice(g * 512, (g + 1) * 512)
                w0 = []
                if g8 == 0:
                    w0 += [t_hi, fh]
                    if mt_toks:
                        w0.append(mt_toks[-1][1])
                if half == 0 and g8 >= 6:
                    w0 += list(gate_banks)
                hi_last = mm(ps(g8)[:, 0:512], dlhs_hi.ap(),
                             drhs.ap()[:, cs], True, False, waits=w0,
                             inc=("TE", 1))
            fl = p.dma("sync", drhs.ap()[0:1, rsl],
                       a_alb[hh0:hh0 + 32, 0:CH], "SS",
                       waits=(hi_last, v_lob))
            for g8 in range(8):
                g = half * 8 + g8
                cs = slice(g * 512, (g + 1) * 512)
                mm(ps(g8)[:, 0:512], dlhs_lo.ap(), drhs.ap()[:, cs],
                   False, False, waits=(t_lo, fl) if g8 == 0 else (),
                   inc=None)
                tD = mm(ps(g8)[:, 0:512], c_ngm.ap(), c_idt.ap(), False, True)
                sE = sact(a_me[g8 % 2], ps(g8), AF.Exp,
                          waits=(tD,) + ((mt_toks[-2][1],)
                                         if len(mt_toks) >= 2 else ()))
                gt_ap = a_gt.unsqueeze(1).to_broadcast([128, 4, 128])
                mv = vtt(rr(a_mt[:, cs], "p (h l) -> p h l", h=4),
                         rr(a_me[g8 % 2], "p (h l) -> p h l", h=4), gt_ap,
                         ALU.mult, waits=(sE, eg))
                mt_toks.append((sE, mv))
            tok_dmm = mt_toks[-1][0]
        tok_mt = mt_toks[-1][1]

        yoff_done = []
        st_upd = []
        last_soff = None
        for hg in range(8):
            hs = slice(hg * 512, (hg + 1) * 512)
            if ci > 0:
                w = [tok_xbc, st_upd[hg - 1] if hg else None, last_soff]
                if hg == 0:
                    w.append(tok_dmm)
                last_soff = mm(ps(2)[:, 0:512], C_fm, a_st[:, hs], True, True,
                               waits=w)
                ect_ap = a_ect[:, hg * 8:hg * 8 + 8].unsqueeze(2).to_broadcast(
                    [128, 8, 64])
                voff = vtt(rr(a_stt, "p (h q) -> p h q", h=8),
                           rr(ps(2)[:, 0:512], "p (h q) -> p h q", h=8),
                           ect_ap, ALU.mult,
                           waits=(last_soff, cv_ec,
                                  yoff_done[hg - 1] if hg else None))
            else:
                voff = None
            stt = None
            for hh in range(8):
                h = hg * 8 + hh
                w = [vxb, eb] if hh == 0 else []
                if hh == 0 and st_upd:
                    w.append(st_upd[-1])
                if hh == 0 and hg == 0 and ci == 0:
                    w.append(tok_dmm)
                stt = mm(ps(4)[:, hh * 64:(hh + 1) * 64], a_btm,
                         a_xbr[:, h * 64:(h + 1) * 64], True, True,
                         waits=w, inc=("TE", 1) if hh == 7 else None)
            if ci == 0:
                su = vcopy(a_st[:, hs], ps(4)[:, 0:512], waits=(stt,))
            else:
                gm_ap = a_gmb[:, hg * 8:hg * 8 + 8].unsqueeze(2).to_broadcast(
                    [128, 8, 64])
                su1 = vtt(rr(a_st[:, hs], "p (h q) -> p h q", h=8),
                          rr(a_st[:, hs], "p (h q) -> p h q", h=8), gm_ap,
                          ALU.mult, waits=(stt, voff, vgb))
                su = vop(lambda e, hs=hs: e.tensor_tensor(
                    a_st[:, hs], a_st[:, hs], ps(4)[:, 0:512], op=ALU.add),
                    waits=(su1,))
            st_upd.append(su)
            if ci > 0:
                yoff_done.append(vcopy(a_y[:, hs], a_stt, waits=(voff,)))
            else:
                yoff_done.append(None)
        db_ap = c_dbc.ap().unsqueeze(2).to_broadcast([128, 64, 64])
        vxd = vtt(rr(a_xbr, "p (h q) -> p h q", h=64),
                  rr(a_xtm, "p (h q) -> p h q", h=64), db_ap, ALU.mult,
                  waits=(st_upd[-1],))
        y_toks = []
        for hg in range(8):
            hs = slice(hg * 512, (hg + 1) * 512)
            bank = 3 if hg % 2 == 0 else 5
            w = [vxd, tok_mt]
            if hg == 0:
                w.append(tok_dmm)
            if len(y_toks) >= 2:
                w.append(y_toks[-2])
            td = None
            for hh in range(8):
                h = hg * 8 + hh
                mm(ps(bank)[:, hh * 64:(hh + 1) * 64], c_idb.ap(),
                   a_xbr[:, h * 64:(h + 1) * 64], True, False,
                   waits=w if hh == 0 else (), inc=None)
                td = mm(ps(bank)[:, hh * 64:(hh + 1) * 64],
                        a_mt[:, h * 128:(h + 1) * 128],
                        a_xti[:, h * 64:(h + 1) * 64], False, True,
                        inc=("TE", 1) if hh == 7 else None)
            if ci > 0:
                yv = vop(lambda e, hs=hs, bank=bank: e.tensor_tensor(
                    a_y[:, hs], a_y[:, hs], ps(bank)[:, 0:512], op=ALU.add),
                    waits=(td, yoff_done[hg]))
            else:
                yv = vcopy(a_y[:, hs], ps(bank)[:, 0:512],
                           waits=(td,))
            y_toks.append(yv)

        slz = sact(a_z, a_z, AF.Silu, waits=(tok_z,))
        vg = vtt(a_y, a_y, a_z, ALU.mult, waits=(slz,) + tuple(y_toks))
        sq = sact(a_sq2, a_y, AF.Square, accum=scol.ap()[:, 4:5],
                  waits=(vg, prev_chunk["yst"]))
        vr = vts(scol.ap()[:, 5:6], scol.ap()[:, 4:5], 1.0 / DIN, RMS_EPS,
                 ALU.mult, ALU.add, waits=(sq,))
        vr = vop(lambda e: e.reciprocal(scol.ap()[:, 5:6], scol.ap()[:, 5:6]),
                 waits=(vr,))
        sr = sact(scol.ap()[:, 6:7], scol.ap()[:, 5:6], AF.Sqrt, waits=(vr,))
        vy = vstt(a_y, a_y, scol.ap()[:, 6:7], c_nsb.ap(), ALU.mult, ALU.mult,
                  waits=(sr,))
        evy = []
        for g in range(8):
            tt = None
            for j in range(4):
                pc = g * 4 + j
                w = [vy, prev_chunk["yst"]] if j == 0 and g == 0 else []
                if j == 0 and evy:
                    w.append(evy[-1])
                tt = tr(psb(7)[:, j * 128:(j + 1) * 128],
                        a_y[:, pc * 128:(pc + 1) * 128], c_idb.ap(),
                        waits=w, inc=("TE", 1) if j == 3 else None)
            evy.append(vcopy(a_yf[:, g * 512:(g + 1) * 512], psb(7)[:, 0:512],
                             waits=(tt, prev_chunk["yst"])))
        p.wait("sync", *evy)
        sty = p.dma("sync", yh_h[ci], a_yf, "SZ")

        p.light_barrier()
        prev_chunk = {"xbc": y_toks[-1], "z": vg,
                      "drhs": mt_toks[-1][0], "y": y_toks[-1],
                      "acs": cv_ec, "yst": sty}

    AR.pop()
    p.barrier()

    if "yhat" in tap_out:
        AR.push()
        tbb = AR.bf16(4096)
        tb = AR.f32(4096)
        pst = None
        for ci in range(NCH):
            tl = p.dma("sync", tbb, yh_h[ci], "LD", waits=(pst,))
            cv = vcopy(tb, tbb, waits=(tl, pst))
            p.wait("sync", cv)
            pst = p.dma("sync", tap_out["yhat"][ci], tb, "SS")
        AR.pop()
        p.barrier()

    def outproj_pass(w_src, w_rows, dst_h, store_stats=True, src_fm=None,
                     scale=None, wrow=None):
        npc = w_rows // 128
        sw = 256
        AR.push()
        a_yb = AR.bf16(4 * 4096)
        a_xi = AR.f32(8192)
        a_xo2 = AR.f32(512)
        a_cb = AR.bf16(512)
        a_cs = AR.bf16(512)
        a_c2 = AR.f32(512)
        for tq in range(4):
            p.barrier()
            ybl = []
            if src_fm is None:
                for j in range(4):
                    ybl.append(p.dma("sync", a_yb[:, j * 4096:(j + 1) * 4096],
                                     yh_h[tq * 4 + j], "LY"))
            else:
                ybl.append(p.dma(
                    "sync",
                    rr(a_yb[:, 0:npc * 512], "p (c q) -> p c q", c=npc),
                    rr(src_fm, "(c p) t -> p c t", p=128)
                    [:, :, tq * 512:(tq + 1) * 512], "LY"))
            xls = []
            for j in range(4):
                xls.append(p.dma("sync",
                                 rr(a_xi, "p (j c q) -> p j c q", j=4, c=16)
                                 [:, j, :, :],
                                 xfm_v(tq * 4 + j), "LX"))
            wtk = [None, None]
            wlst = [None, None]
            mst = None
            msx = None
            prev_ev = None
            prev_st = None
            opev = [None, None, None, None]
            for si in range(8):
                buf = si % 2
                wtk[buf] = p.dma(
                    "gpsimd",
                    rr(wst[buf].ap()[:, 0:npc * sw], "p (c m) -> p c m",
                       c=npc),
                    rr(w_src, "(c p) m -> p c m", p=128)
                    [:, :, si * sw:(si + 1) * sw],
                    WSEMS[si % 4],
                    waits=(wtk[buf], wlst[buf]))
                for msub in range(2):
                    dm = si * 2 + msub
                    bank = dm % 4
                    ltp = None
                    for pcc in range(npc):
                        w = [wtk[buf]] + ybl + xls if pcc == 0 else []
                        if pcc == 0:
                            w.append(opev[bank])
                        if src_fm is None:
                            rhs = rr(a_yb, "p (j c q) -> p c j q", j=4,
                                     c=npc)[:, pcc, :, :]
                        else:
                            rhs = rr(a_yb[:, 0:npc * 512], "p (c q) -> p c q",
                                     c=npc)[:, pcc, :]
                        ltp = mm(ps(bank)[:, 0:512],
                                 wst[buf].ap()[:, pcc * sw + msub * 128:
                                               pcc * sw + msub * 128 + 128],
                                 rhs, pcc == 0, pcc == npc - 1, waits=w)
                    wlst[buf] = ltp
                    xold = rr(a_xi, "p (j c q) -> p c j q", j=4,
                              c=16)[:, dm, :, :]
                    if scale is None:
                        ev = vtt(sb3(a_xo2, c=4), xold,
                                 sb3(ps(bank)[:, 0:512], c=4), ALU.add,
                                 waits=(ltp, prev_st))
                    else:
                        ev = vop(lambda e, xold=xold, bank=bank:
                                 e.scalar_tensor_tensor(
                                     sb3(a_xo2, c=4),
                                     sb3(ps(bank)[:, 0:512], c=4), scale,
                                     xold, op0=ALU.mult, op1=ALU.add),
                                 waits=(ltp, prev_st))
                    opev[bank] = ev
                    p.wait("sync", ev)
                    prev_st = p.dma("sync",
                                    rr(dst_h, "(c p) t -> p c t", p=128)
                                    [:, dm, tq * 512:(tq + 1) * 512],
                                    a_xo2, "SS" if dm % 2 == 0 else "SZ")
                    if store_stats:
                        cb = vcopy(a_cb, a_xo2, waits=(msx, ev))
                        sq = sact(a_cs, a_xo2, AF.Square, waits=(mst, ev))
                        if wrow is None:
                            msx = mm(ps(6)[0:1, 0:512], c_ocb.ap(), a_cb,
                                     dm == 0, dm == 15, waits=(cb,),
                                     inc=("TE", 1))
                        else:
                            msx = mm(ps(6)[0:2, 0:512],
                                     wrow[:, dm * 2:(dm + 1) * 2], a_cb,
                                     dm == 0, dm == 15, waits=(cb,),
                                     inc=("TE", 1))
                        mst = mm(ps(7)[0:1, 0:512], c_ocb.ap(), a_cs,
                                 dm == 0, dm == 15, waits=(sq,))
                    prev_ev = ev
            if store_stats:
                tcs = slice(tq * 512, (tq + 1) * 512)
                if wrow is None:
                    sv1 = vcopy(sx_r.ap()[0:1, tcs], ps(6)[0:1, 0:512],
                                waits=(msx, mst))
                else:
                    c2v = vcopy(a_c2[0:2, :], ps(6)[0:2, 0:512],
                                waits=(msx, mst))
                    sv1 = vcopy(sx_r.ap()[0:1, tcs], a_c2[0:1, :],
                                waits=(c2v,))
                    p.dma("sync", sxw_r.ap()[0:1, tcs], a_c2[1:2, :], "SS",
                          waits=(c2v,))
                sv2 = vcopy(sxx_r.ap()[0:1, tcs], ps(7)[0:1, 0:512],
                            waits=(mst,))
        AR.pop()
        return (sv1, sv2)

    st_toks = outproj_pass(w_out0, DIN, xfm_h)
    p.barrier()
    tok_stats = finalize_stats(ready=st_toks)
    p.barrier()

    if "xfm1" in tap_out:
        AR.push()
        tb = AR.f32(2048)
        pst = None
        for dc in range(16):
            tl = p.dma("sync", tb[:, 0:2048],
                       xfm_h[dc * 128:(dc + 1) * 128, :], "LD", waits=(pst,))
            pst = p.dma("sync", tap_out["xfm1"][dc * 128:(dc + 1) * 128, :],
                        tb[:, 0:2048], "SS", waits=(tl,))
        AR.pop()
        p.barrier()

    AR.push()
    wl_a = p.dma("gpsimd",
                 rr(wst[0].ap(), "p (c m) -> p c m", c=16),
                 rr(a_all, "(c p) m -> p c m", p=128), "LW")
    wl_b = p.dma("gpsimd",
                 rr(wst[1].ap(), "p (c m) -> p c m", c=4),
                 rr(b_all, "(c p) m -> p c m", p=128), "LZ")
    a_wr = AR.bf16(128)
    wl_r = p.dma("gpsimd", rr(a_wr, "p (c m) -> p c m", c=16),
                 rr(wr0, "(c p) m -> p c m", p=128), "LV")
    a_xh2 = AR.bf16(2048)
    a_g2 = AR.bf16(512)
    a_gf = AR.bf16(8192)
    a_e8 = AR.f32(16)
    a_i8 = AR.bf16(64)
    prevv = None
    prev_gt = None
    prev_lr = ()
    vg_hist = [None, None]
    for tc in range(16):
        xl = p.dma("sync", xin.ap(), xfm_v(tc), "LD", waits=(prevv,))
        b1 = mm(ps(5)[:, 0:128], c_orb.ap(),
                sgi_r.ap()[0:1, tc * 128:(tc + 1) * 128], True, True,
                waits=(prevv,) + tuple(tok_stats), inc=None)
        b2 = mm(ps(5)[:, 128:256], c_orb.ap(),
                msg_r.ap()[0:1, tc * 128:(tc + 1) * 128], True, True)
        v1 = vtt(sb3(a_xh2), sb3(xin.ap()), bc_in1(ps(5)[:, 0:128], 16),
                 ALU.mult, waits=(xl, b2, prevv))
        vxh = vtt(sb3(a_xh2), sb3(a_xh2), bc_in1(ps(5)[:, 128:256], 16),
                  ALU.subtract, waits=(v1,))
        bank = tc % 2
        lth = None
        ltr = None
        for dc in range(16):
            w = [vxh, wl_a, prev_gt, vg_hist[bank]] if dc == 0 else []
            lth = mm(ps(bank)[:, 0:512],
                     a_xh2[:, dc * 128:(dc + 1) * 128],
                     wst[0].ap()[:, dc * 512:(dc + 1) * 512],
                     dc == 0, dc == 15, waits=w)
            w = [wl_r] + list(prev_lr) if dc == 0 else []
            ltr = mm(ps(2)[:, 0:8], a_xh2[:, dc * 128:(dc + 1) * 128],
                     a_wr[:, dc * 8:(dc + 1) * 8], dc == 0, dc == 15,
                     waits=w)
        vm = vop(lambda e: e.reduce_max(a_e8[:, 0:1], ps(2)[:, 0:8],
                                        axis=mybir.AxisListType.X),
                 waits=(ltr, prevv))
        vnm = vts(a_e8[:, 1:2], a_e8[:, 0:1], -1.0, None, ALU.mult,
                  waits=(vm,))
        sE = sact(a_e8[:, 8:16], ps(2)[:, 0:8], AF.Exp,
                  bias=a_e8[:, 1:2], accum=a_e8[:, 2:3], waits=(vnm,))
        vw = vop(lambda e: e.reciprocal(a_e8[:, 3:4], a_e8[:, 2:3]),
                 waits=(sE,))
        vw = vts(a_e8[:, 3:4], a_e8[:, 3:4], 1.0 / 64.0, None, ALU.mult,
                 waits=(vw,))
        vi = vts(a_i8[:, 0:8], ps(2)[:, 0:8], a_e8[:, 0:1], None, ALU.is_ge,
                 waits=(vm,))
        vg = vop(lambda e: e.scalar_tensor_tensor(
            rr(a_g2, "p (e r) -> p e r", e=8),
            rr(ps(0 if bank == 0 else 1)[:, 0:512], "p (e r) -> p e r", e=8)
            if False else rr(ps(bank)[:, 0:512], "p (e r) -> p e r", e=8),
            a_e8[:, 3:4],
            a_i8[:, 0:8].unsqueeze(2).to_broadcast([128, 8, 64]),
            op0=ALU.mult, op1=ALU.mult), waits=(lth, vw, vi))
        gtt = None
        for j in range(4):
            gtt = tr(psb(3)[:, j * 128:(j + 1) * 128],
                     a_g2[:, j * 128:(j + 1) * 128], c_idb.ap(),
                     waits=(vg, prevv) if j == 0 else (),
                     inc=("TE", 1) if j == 3 else None)
        prevv = vcopy(rr(a_gf, "p (c q) -> p c q", c=4)
                      [:, :, tc * 128:(tc + 1) * 128],
                      sb3(psb(3)[:, 0:512], c=4), waits=(gtt,))
        prev_gt = gtt
        prev_lr = (vi, sE, vw)
        vg_hist[bank] = vg
    a_xi2 = AR.f32(8192)
    a_xo3 = AR.f32(512)
    a_cb2 = AR.bf16(512)
    a_cs2 = AR.bf16(512)
    p.wait("tensor", prevv)
    sv1 = sv2 = None
    for tq in range(4):
        p.barrier()
        xls = []
        for j in range(4):
            xls.append(p.dma("sync",
                             rr(a_xi2, "p (j c q) -> p j c q", j=4, c=16)
                             [:, j, :, :], xfm_v(tq * 4 + j), "LX"))
        mst = None
        msx = None
        prev_st = None
        opev2 = [None, None, None, None]
        for dm in range(16):
            bank = dm % 4
            ltp = None
            for erc in range(4):
                w = [wl_b] + xls if erc == 0 else []
                if erc == 0:
                    w.append(opev2[bank])
                ltp = mm(ps(bank)[:, 0:512],
                         wst[1].ap()[:, erc * 2048 + dm * 128:
                                     erc * 2048 + dm * 128 + 128],
                         rr(a_gf, "p (c q) -> p c q", c=4)
                         [:, erc, tq * 512:(tq + 1) * 512],
                         erc == 0, erc == 3, waits=w)
            xold = rr(a_xi2, "p (j c q) -> p c j q", j=4, c=16)[:, dm, :, :]
            ev = vtt(sb3(a_xo3, c=4), xold, sb3(ps(bank)[:, 0:512], c=4),
                     ALU.add, waits=(ltp, prev_st))
            opev2[bank] = ev
            p.wait("sync", ev)
            prev_st = p.dma("sync",
                            rr(xfm_h, "(c p) t -> p c t", p=128)
                            [:, dm, tq * 512:(tq + 1) * 512], a_xo3,
                            "SS" if dm % 2 == 0 else "SZ")
            cb = vcopy(a_cb2, a_xo3, waits=(msx, ev))
            sq = sact(a_cs2, a_xo3, AF.Square, waits=(mst, ev))
            msx = mm(ps(4)[0:1, 0:512], c_ocb.ap(), a_cb2, dm == 0,
                     dm == 15, waits=(cb,), inc=("TE", 1))
            mst = mm(ps(6)[0:1, 0:512], c_ocb.ap(), a_cs2, dm == 0,
                     dm == 15, waits=(sq,))
        tcs = slice(tq * 512, (tq + 1) * 512)
        sv1 = vcopy(sx_r.ap()[0:1, tcs], ps(4)[0:1, 0:512], waits=(msx, mst))
        sv2 = vcopy(sxx_r.ap()[0:1, tcs], ps(6)[0:1, 0:512], waits=(mst,))
    AR.pop()
    p.barrier()
    tok_stats = finalize_stats(ready=(sv1, sv2))
    p.barrier()

    if "xfm2" in tap_out:
        AR.push()
        tb = AR.f32(2048)
        pst = None
        for dc in range(16):
            tl = p.dma("sync", tb[:, 0:2048],
                       xfm_h[dc * 128:(dc + 1) * 128, :], "LD", waits=(pst,))
            pst = p.dma("sync", tap_out["xfm2"][dc * 128:(dc + 1) * 128, :],
                        tb[:, 0:2048], "SS", waits=(tl,))
        AR.pop()
        p.barrier()

    AR.push()
    xh3 = AR.bf16(32768)
    xh3_done = build_xhat(xh3, extra=tok_stats)
    a_pq = AR.bf16(2048)
    a_pk = AR.bf16(2048)
    a_kv = AR.bf16(2048)
    a_kvc = AR.bf16(2048)
    a_kc = AR.bf16(2048)
    wtoks = [None, None]
    wlast = [xh3_done[-1], xh3_done[-1]]
    prev_head = [None, None]
    pbank = {}
    for h in range(16):
        buf = h % 2
        wsem = WSEMS[h % 4]
        wv3 = rr(wst[buf].ap()[:, 0:16 * 384], "p (c m) -> p c m", c=16)
        wtoks[buf] = p.dma("gpsimd", wv3[:, :, 0:128],
                           rr(w_qkv_t, "(c p) m -> p c m", p=128)
                           [:, :, h * 128:(h + 1) * 128], wsem,
                           waits=(wtoks[buf], wlast[buf]))
        p.dma("gpsimd", wv3[:, :, 128:256],
              rr(w_qkv_t, "(c p) m -> p c m", p=128)
              [:, :, D + h * 128:D + (h + 1) * 128], wsem)
        wtoks[buf] = p.dma("gpsimd", wv3[:, :, 256:384],
                           rr(w_qkv_t, "(c p) m -> p c m", p=128)
                           [:, :, 2 * D + h * 128:2 * D + (h + 1) * 128],
                           wsem)
        ph_toks = []
        for tsl in range(4):
            t4 = slice(tsl * 512, (tsl + 1) * 512)
            bq, bk, bv = (0, 1, 2) if tsl % 2 == 0 else (3, 4, 5)
            lq = lk = lv = None
            for dc in range(16):
                w = [wtoks[buf], pbank.get(bq), pbank.get(bk),
                     pbank.get(bv)] if dc == 0 else []
                lq = mm(ps(bq)[:, 0:512],
                        wst[buf].ap()[:, dc * 384:dc * 384 + 128],
                        xh3[:, dc * 2048 + tsl * 512:
                            dc * 2048 + tsl * 512 + 512],
                        dc == 0, dc == 15, waits=w)
                lk = mm(ps(bk)[:, 0:512],
                        wst[buf].ap()[:, dc * 384 + 128:dc * 384 + 256],
                        xh3[:, dc * 2048 + tsl * 512:
                            dc * 2048 + tsl * 512 + 512],
                        dc == 0, dc == 15)
                lv = mm(ps(bv)[:, 0:512],
                        wst[buf].ap()[:, dc * 384 + 256:dc * 384 + 384],
                        xh3[:, dc * 2048 + tsl * 512:
                            dc * 2048 + tsl * 512 + 512],
                        dc == 0, dc == 15)
            prev_sex = ph_toks[-1] if ph_toks else prev_head[0]
            for (lsrc, bankq, dst) in ((lq, bq, a_pq), (lk, bk, a_pk)):
                vmn = vts(a_kvc[:, t4], ps(bankq)[:, 0:512], 0.0, None,
                          ALU.min, waits=(lsrc, prev_sex, prev_head[0],
                                          prev_head[1]))
                sEx = sact(dst[:, t4], a_kvc[:, t4], AF.Exp,
                           waits=(vmn, prev_head[0]))
                prev_sex = sEx
                tkn = vop(
                    lambda e, bankq=bankq, dst=dst, t4=t4:
                    e.scalar_tensor_tensor(dst[:, t4], ps(bankq)[:, 0:512],
                                           0.0, dst[:, t4], op0=ALU.max,
                                           op1=ALU.add), waits=(sEx,))
                ph_toks.append(tkn)
                pbank[bankq] = tkn
            tkn = vtt(a_kv[:, t4], a_pk[:, t4], ps(bv)[:, 0:512],
                      ALU.mult,
                      waits=(lv, ph_toks[-1], prev_head[0]))
            ph_toks.append(tkn)
            pbank[bv] = tkn
        sc1 = vop(lambda e: e.tensor_tensor_scan(
            a_kvc, a_kv, a_kv, 0.0, op0=ALU.add, op1=ALU.bypass),
            waits=tuple(ph_toks))
        sc2 = vop(lambda e: e.tensor_tensor_scan(
            a_kc, a_pk, a_pk, 0.0, op0=ALU.add, op1=ALU.bypass),
            waits=(sc1,))
        n1 = vtt(a_kv, a_pq, a_kvc, ALU.mult, waits=(sc1,))
        dd = vtt(a_kvc, a_pq, a_kc, ALU.mult, waits=(sc2, n1))
        dd = vts(a_kvc, a_kvc, 1e-9, None, ALU.add, waits=(dd,))
        at = dd
        for t5 in range(4):
            t4b = slice(t5 * 512, (t5 + 1) * 512)
            at = vop(lambda e, t4b=t4b: e.reciprocal(ps(3)[:, 0:512],
                                                     a_kvc[:, t4b]),
                     waits=(at,))
            at = vtt(a_pk[:, t4b], a_kv[:, t4b], ps(3)[:, 0:512], ALU.mult,
                     waits=(at,))
        p.wait("sync", at)
        sth = p.dma("sync", at_h[h * 128:(h + 1) * 128, :], a_pk,
                    "SS" if h % 2 == 0 else "SZ")
        prev_head = [at, sth]
        pbank[3] = at
        wlast[buf] = lv
    AR.pop()
    p.barrier()
    AR.push()
    a_ow = AR.bf16(32)
    owl = p.dma("gpsimd", rr(a_ow, "p (c m) -> p c m", c=16),
                rr(onesw1, "(c p) m -> p c m", p=128), "LW")
    p.barrier()
    st_toks = outproj_pass(w_out1_t, D, xfm_h, src_fm=at_h,
                           scale=float(HD) ** -0.5, wrow=a_ow)
    p.barrier()
    tok_stats = finalize_stats(ready=st_toks)
    p.barrier()
    AR.pop()

    if "xfm3" in tap_out:
        AR.push()
        tb = AR.f32(2048)
        pst = None
        for dc in range(16):
            tl = p.dma("sync", tb[:, 0:2048],
                       xfm_h[dc * 128:(dc + 1) * 128, :], "LD", waits=(pst,))
            pst = p.dma("sync", tap_out["xfm3"][dc * 128:(dc + 1) * 128, :],
                        tb[:, 0:2048], "SS", waits=(tl,))
        AR.pop()
        p.barrier()

    AR.push()
    a_ptm = AR.bf16(16 * 256)
    a_am = AR.bf16(16 * 256)
    a_gb = AR.f32(256)
    a_sc = AR.f32(16 * 8)
    sgc = a_sc[:, 0:16]
    mgc = a_sc[:, 16:32]
    mkc = a_sc[:, 32:48]
    rkc = a_sc[:, 48:64]
    sccb = a_sc.bitcast(BF16)[:, 128:144]
    sg1 = a_sc[:, 80:96]
    a_rsg = a_sc[:, 96:112]
    a_cnb = AR.f32(32)
    AR.push()
    a_iot = AR.f32(256)
    a_orf = AR.f32(128)
    a_ow4 = AR.bf16(32)
    owl4 = p.dma("gpsimd", rr(a_ow4, "p (c m) -> p c m", c=16),
                 rr(onesw1, "(c p) m -> p c m", p=128), "LW")

    dio = p.dma("sync", a_iot, iota_r.broadcast_to([128, 256]), "SS")
    vo1 = vcopy(a_orf[0:1, 0:128], c_orb.ap())
    vo2 = vcopy(a_orf[:, 0:1], c_ocb.ap(), waits=(vo1,))

    swm = None
    for dc in range(16):
        swm = mm(ps(6)[0:1, 0:1], a_ow4[:, dc * 2 + 1:dc * 2 + 2],
                 c_ocb.ap(), dc == 0, dc == 15,
                 waits=(owl4,) if dc == 0 else ())
    vsw = vcopy(srow.ap()[0:1, 64:65], ps(6)[0:1, 0:1], waits=(swm,))
    v = vts(sco_r.ap(), sx_r.ap(), srow.ap()[0:1, 64:65], None, ALU.mult,
            waits=(vsw,) + tuple(tok_stats))
    v = vtt(sco_r.ap(), sxw_r.ap(), sco_r.ap(), ALU.subtract, waits=(v,))
    vsc = vtt(sco_r.ap(), sco_r.ap(), sgi_r.ap(), ALU.mult, waits=(v,))

    a_scb = AR.f32(16)
    dsc = p.dma("sync", a_scb, sco_r.ap(), "SS", waits=(vsc,))
    vmx = vop(lambda e: e.reduce_max(a_cnb[:, 0:1], a_scb,
                                     axis=mybir.AxisListType.X),
              waits=(dsc,))
    vmn = vop(lambda e: e.tensor_reduce(a_cnb[:, 1:2], a_scb,
                                        axis=mybir.AxisListType.X,
                                        op=ALU.min), waits=(dsc,))
    d1 = p.dma("sync", srow.ap()[0:1, 128:256], a_cnb[:, 0:1], "SS",
               waits=(vmx,))
    d2 = p.dma("sync", srow.ap()[0:1, 256:384], a_cnb[:, 1:2], "SS",
               waits=(vmn, d1))
    HI = srow.ap()[0:1, 70:71]
    LO = srow.ap()[0:1, 71:72]
    MD = srow.ap()[0:1, 72:73]
    PR = srow.ap()[0:1, 73:74]
    DD = srow.ap()[0:1, 74:75]
    NP = srow.ap()[0:1, 75:76]
    vb = vop(lambda e: e.reduce_max(HI, srow.ap()[0:1, 128:256],
                                    axis=mybir.AxisListType.X),
             waits=(d1, d2))
    vb = vts(HI, HI, 1.0, None, ALU.add, waits=(vb,))
    vb = vop(lambda e: e.tensor_reduce(LO, srow.ap()[0:1, 256:384],
                                       axis=mybir.AxisListType.X,
                                       op=ALU.min), waits=(vb,))
    vb = vts(LO, LO, -1.0, None, ALU.add, waits=(vb,))
    for it in range(36):
        vb = vts(MD, LO, HI, 0.5, ALU.add, ALU.mult, waits=(vb,))
        bmd = mm(ps(5)[:, 0:1], a_orf[0:1, 0:128], MD, True, True,
                 waits=(vb, vo1))
        vb = vts(a_cnb[:, 2:18], a_scb, ps(5)[:, 0:1], None, ALU.is_ge,
                 waits=(bmd,))
        vb = vop(lambda e: e.reduce_sum(a_cnb[:, 18:19], a_cnb[:, 2:18],
                                        axis=mybir.AxisListType.X),
                 waits=(vb,))
        cbank = 1 if it % 2 == 0 else 2
        bct = mm(ps(5)[0:1, cbank:cbank + 1], a_orf[:, 0:1],
                 a_cnb[:, 18:19], True, True, waits=(vb, vo2))
        vb = vts(PR, ps(5)[0:1, cbank:cbank + 1], 256.0, None, ALU.is_ge,
                 waits=(bct,))
        vb = vtt(DD, MD, LO, ALU.subtract, waits=(vb,))
        vb = vtt(DD, DD, PR, ALU.mult, waits=(vb,))
        vb = vtt(LO, LO, DD, ALU.add, waits=(vb,))
        vb = vts(NP, PR, -1.0, 1.0, ALU.mult, ALU.add, waits=(vb,))
        vb = vtt(DD, MD, HI, ALU.subtract, waits=(vb,))
        vb = vtt(DD, DD, NP, ALU.mult, waits=(vb,))
        vb = vtt(HI, HI, DD, ALU.add, waits=(vb,))

    vmk = vts(sxw_r.ap(), sco_r.ap(), LO, None, ALU.is_ge, waits=(vb,))
    vrk = vop(lambda e: e.tensor_tensor_scan(sx_r.ap(), sxw_r.ap(),
                                             sxw_r.ap(), 0.0, op0=ALU.add,
                                             op1=ALU.bypass), waits=(vmk,))
    vrk = vts(sx_r.ap(), sx_r.ap(), -1.0, None, ALU.add, waits=(vrk,))
    _r2c_prev = [None]

    def row_to_cols(row_ap, col_ap, waits=()):
        src_bf = row_ap.dtype == BF16
        cast = (col_ap.dtype != row_ap.dtype)
        hb = rowb_h if src_bf else rowf_h
        da = p.dma("sync", hb, row_ap, "SS",
                   waits=tuple(waits) + (_r2c_prev[0],))
        eng = "gpsimd" if cast else "sync"
        sem = "LG" if cast else "SS"
        db = p.dma(eng, col_ap, rr(hb, "o (c p) -> (o p) c", p=128),
                   sem, waits=(da,), slow=True)
        _r2c_prev[0] = db
        return db

    dmk = row_to_cols(sxw_r.ap(), mkc, waits=(vmk, vrk))
    drk = row_to_cols(sx_r.ap(), rkc, waits=(vrk,))
    dsm = row_to_cols(sgi_r.ap(), sgc)
    dsg = row_to_cols(msg_r.ap(), mgc)
    dss = row_to_cols(sco_r.ap(), sccb, waits=(vmk,))
    vsg1 = vts(sg1, sgc, 1.0, None, ALU.add, waits=(dsm,))

    ptoks = []
    for tc in range(16):
        ptoks.append(vop(lambda e, tc=tc: e.scalar_tensor_tensor(
            a_ptm[:, tc * 256:(tc + 1) * 256], a_iot, rkc[:, tc:tc + 1],
            mkc[:, tc:tc + 1].to_broadcast([128, 256]), op0=ALU.is_equal,
            op1=ALU.mult), waits=(dio, dmk, drk)))
    tok_ptm = ptoks[-1]
    p.barrier()
    AR.pop()

    AR.push()
    a_xt4 = AR.f32(2048)
    a_xh4 = AR.bf16(2048)
    prevh = None
    last_tt4 = None
    for tc in range(16):
        tl = p.dma("sync", xin.ap(), xfm_v(tc), "LD", waits=(last_tt4,))
        evs = []
        for g in range(4):
            tt = None
            for j in range(4):
                dc = g * 4 + j
                w = [tl, prevh] if j == 0 else []
                tt = tr(ps(g)[:, j * 128:(j + 1) * 128],
                        sb3(xin.ap())[:, dc, :], c_idf.ap(),
                        waits=w, inc=("TE", 1) if j == 3 else None)
            evs.append(vcopy(a_xt4[:, g * 512:(g + 1) * 512], ps(g),
                             waits=(tt,)))
        last_tt4 = tt
        vh4 = vop(lambda e, tc=tc: e.tensor_scalar(
            a_xh4, a_xt4, sgc[:, tc:tc + 1], mgc[:, tc:tc + 1],
            op0=ALU.mult, op1=ALU.subtract),
            waits=tuple(evs) + (dsm, dsg, prevh))
        p.wait("sync", vh4)
        prevh = p.dma("sync", xh4_h[tc * 128:(tc + 1) * 128, :], a_xh4,
                      "SS" if tc % 2 == 0 else "SZ")
    AR.pop()
    p.barrier()

    AR.push()
    a_xh4b = AR.bf16(2048)
    a_sel = AR.bf16(2 * 2176)
    a_rpt = AR.bf16(2048)
    gl = None
    gmm = None
    for tc in range(16):
        gl = p.dma("sync", a_xh4b, xh4_h[tc * 128:(tc + 1) * 128, :], "LX",
                   waits=(gl, gmm))
        for rh in range(2):
            for dsl in range(4):
                bank = rh * 4 + dsl
                w = ()
                if dsl == 0:
                    w = (gl, tok_ptm) if (rh == 0 and tc == 0) else (gl,)
                gmm = mm(ps(bank)[:, 0:512],
                         a_ptm[:, tc * 256 + rh * 128:
                               tc * 256 + rh * 128 + 128],
                         a_xh4b[:, dsl * 512:(dsl + 1) * 512],
                         tc == 0, tc == 15, waits=w,
                         inc=("TE", 1) if (rh == 1 and dsl == 3) else None)
    sel_evs = []
    for rh in range(2):
        for dsl in range(4):
            bank = rh * 4 + dsl
            sel_evs.append(vcopy(
                a_sel[:, rh * 2176 + dsl * 512:rh * 2176 + dsl * 512 + 512],
                ps(bank)[:, 0:512], waits=(gmm,)))
    grl = p.dma("sync", sb3(a_rpt), rr(ropetab, "(c p) m -> p c m", p=128),
                "LD", waits=())
    grm = gsm = None
    for tc in range(16):
        for rh in range(2):
            w1 = (grl, sel_evs[-1]) if tc == 0 else ()
            grm = mm(ps(rh)[:, 0:128],
                     a_ptm[:, tc * 256 + rh * 128:tc * 256 + rh * 128 + 128],
                     a_rpt[:, tc * 128:(tc + 1) * 128],
                     tc == 0, tc == 15, waits=w1)
            w2 = (dss, sel_evs[-1]) if tc == 0 else ()
            gsm = mm(ps(2 + rh)[:, 0:1],
                     a_ptm[:, tc * 256 + rh * 128:tc * 256 + rh * 128 + 128],
                     sccb[:, tc:tc + 1], tc == 0, tc == 15, waits=w2)
    rp_evs = []
    for rh in range(2):
        rp_evs.append(vcopy(a_sel[:, rh * 2176 + 2048:rh * 2176 + 2176],
                            ps(rh)[:, 0:128], waits=(grm, gsm)))
        rp_evs.append(vcopy(a_cnb[:, 20 + rh:21 + rh],
                            ps(2 + rh)[:, 0:1], waits=(grm, gsm)))
    dgs = p.dma("sync", srow.ap()[0:1, 256:384], a_cnb[:, 20:21], "SS",
                waits=tuple(rp_evs))
    dgs = p.dma("sync", srow.ap()[0:1, 384:512], a_cnb[:, 21:22], "SS",
                waits=(dgs,))
    sgt = sact(srow.ap()[0:1, 256:512], srow.ap()[0:1, 256:512], AF.Sigmoid,
               waits=(dgs,))
    vgr2 = vcopy(srow.ap().bitcast(BF16)[0:1, 0:256],
                 srow.ap()[0:1, 256:512], waits=(sgt,))
    bgb = mm(ps(6)[:, 0:256], c_orb.ap(),
             srow.ap().bitcast(BF16)[0:1, 0:256], True, True,
             waits=(vgr2,))
    vgb2 = vcopy(a_gb, ps(6)[:, 0:256], waits=(bgb,))
    dag = p.dma("sync", rr(ag1_i, "(h p) m -> p h m", p=128),
                rr(a_sel, "p (h m) -> p h m", h=2), "SS",
                waits=tuple(sel_evs) + tuple(rp_evs))
    p.wait("gpsimd", dag)
    cc1 = p.op("gpsimd", lambda e: e.collective_compute(
        "AllGather", ALU.bypass, replica_groups=[list(range(NCORES))],
        ins=[ag1_i.opt()], outs=[ag1_o.opt()]), inc=("CC", 1))
    AR.pop()
    p.barrier()

    AR.push()
    a_cs4 = AR.bf16(16 * 128)
    a_xsl = AR.bf16(2176)
    a_xff = AR.bf16(2048)
    prevx = None
    prevst = None
    for sc in range(16):
        xl4 = p.dma("sync", a_xsl, ag1_o[sc * 128:(sc + 1) * 128, :], "LD",
                    waits=(cc1, prevx))
        sq4 = sact(a_xff, a_xsl[:, 0:2048], AF.Square,
                   accum=a_cnb[:, 24:25], waits=(xl4, prevst))
        vr4 = vts(a_cnb[:, 25:26], a_cnb[:, 24:25], 1.0 / D, 1e-6,
                  ALU.mult, ALU.add, waits=(sq4,))
        vr4 = vop(lambda e: e.reciprocal(a_cnb[:, 25:26], a_cnb[:, 25:26]),
                  waits=(vr4,))
        sr4 = sact(a_rsg[:, sc:sc + 1], a_cnb[:, 25:26], AF.Sqrt,
                   waits=(vr4,))
        vcs = vcopy(a_cs4[:, sc * 128:(sc + 1) * 128], a_xsl[:, 2048:2176],
                    waits=(xl4,))
        evf = []
        for g in range(4):
            tt = None
            for j in range(4):
                dc = g * 4 + j
                w = []
                if j == 0:
                    w = [sq4, vcs, evf[-1] if evf else prevx]
                tt = tr(psb(7)[:, j * 128:(j + 1) * 128],
                        a_xsl[:, dc * 128:(dc + 1) * 128], c_idb.ap(),
                        waits=w, inc=("TE", 1) if j == 3 else None)
            evf.append(vcopy(a_xff[:, g * 512:(g + 1) * 512],
                             psb(7)[:, 0:512], waits=(tt,)))
        prevx = evf[-1]
        p.wait("sync", *evf)
        prevst = p.dma("sync",
                       rr(xsf_h, "(c p) t -> p c t", p=128)
                       [:, :, sc * 128:(sc + 1) * 128],
                       sb3(a_xff), "SS")
    p.barrier()

    a_wq = AR.bf16(16 * 256)
    a_wk = AR.bf16(16 * 128)
    a_wv = AR.bf16(16 * 128)
    a_bq = AR.bf16(256)
    a_bk = AR.bf16(128)
    a_bv = AR.bf16(128)
    a_qf = AR.bf16(2 * 2048)
    a_kf = AR.bf16(2048)
    a_vt = AR.bf16(2048)
    a_q1 = AR.f32(256)
    a_q2 = AR.f32(256)
    a_q3 = AR.f32(256)
    a_xfb = AR.bf16(2048)
    wq_l = p.dma("gpsimd", rr(a_wq, "p (c m) -> p c m", c=16),
                 rr(wq_my, "(c p) m -> p c m", p=128), "LW")
    wk_l = p.dma("gpsimd", rr(a_wk, "p (c m) -> p c m", c=16),
                 rr(wk_my, "(c p) m -> p c m", p=128), "LW")
    wv_l = p.dma("gpsimd", rr(a_wv, "p (c m) -> p c m", c=16),
                 rr(wv_my, "(c p) m -> p c m", p=128), "LW")
    bq_l = p.dma("gpsimd", a_bq, bq_my.broadcast_to([128, 256]), "LG")
    bk_l = p.dma("gpsimd", a_bk, bk_my.broadcast_to([128, 128]), "LG")
    bv_l = p.dma("gpsimd", a_bv, bv_my.broadcast_to([128, 128]), "LG")
    a_l1c = a_cnb[:, 26:27]
    wtoks4 = []
    for dc in range(16):
        dl = p.dma("sync", a_l1c, ln1_c[dc * 128:(dc + 1) * 128, :], "LD",
                   waits=(wtoks4[-1] if wtoks4 else None,))
        t1 = vts(a_wq[:, dc * 256:(dc + 1) * 256],
                 a_wq[:, dc * 256:(dc + 1) * 256], a_l1c, None, ALU.mult,
                 waits=(dl, wq_l))
        t2 = vts(a_wk[:, dc * 128:(dc + 1) * 128],
                 a_wk[:, dc * 128:(dc + 1) * 128], a_l1c, None, ALU.mult,
                 waits=(dl, wk_l, t1))
        t3 = vts(a_wv[:, dc * 128:(dc + 1) * 128],
                 a_wv[:, dc * 128:(dc + 1) * 128], a_l1c, None, ALU.mult,
                 waits=(dl, wv_l, t2))
        wtoks4.append(t3)
    tok_w4 = wtoks4[-1]
    prevq = None
    xl5 = None
    lvm = None
    for sc in range(16):
        xl5 = p.dma("sync", sb3(a_xfb),
                    rr(xsf_h, "(c p) t -> p c t", p=128)
                    [:, :, sc * 128:(sc + 1) * 128], "LX", waits=(xl5, lvm))
        lqm = lkm = None
        for dc in range(16):
            w = (xl5, tok_w4, prevq) if dc == 0 else ()
            lqm = mm(ps(0)[:, 0:256], a_xfb[:, dc * 128:(dc + 1) * 128],
                     a_wq[:, dc * 256:(dc + 1) * 256], dc == 0, dc == 15,
                     waits=w)
            lkm = mm(ps(1)[:, 0:128], a_xfb[:, dc * 128:(dc + 1) * 128],
                     a_wk[:, dc * 128:(dc + 1) * 128], dc == 0, dc == 15)
            lvm = mm(ps(2)[:, 0:128], a_xfb[:, dc * 128:(dc + 1) * 128],
                     a_wv[:, dc * 128:(dc + 1) * 128], dc == 0, dc == 15)
        vv2 = vop(lambda e, sc=sc: e.scalar_tensor_tensor(
            a_vt[:, sc * 128:(sc + 1) * 128], ps(2)[:, 0:128],
            a_rsg[:, sc:sc + 1], a_bv, op0=ALU.mult, op1=ALU.add),
            waits=(lvm, bv_l))
        vq = vop(lambda e, sc=sc: e.scalar_tensor_tensor(
            a_q1, ps(0)[:, 0:256], a_rsg[:, sc:sc + 1], a_bq,
            op0=ALU.mult, op1=ALU.add), waits=(lqm, bq_l, prevq))
        vr1 = None
        for hh in range(2):
            vr1 = vts(a_q2[:, hh * 128:hh * 128 + 64],
                      a_q1[:, hh * 128 + 64:hh * 128 + 128], -1.0, None,
                      ALU.mult, waits=(vq,))
            vr1 = vcopy(a_q2[:, hh * 128 + 64:hh * 128 + 128],
                        a_q1[:, hh * 128:hh * 128 + 64], waits=(vr1,))
        csl = a_cs4[:, sc * 128:sc * 128 + 64]
        snl = a_cs4[:, sc * 128 + 64:sc * 128 + 128]
        cq = csl.unsqueeze(1).to_broadcast([128, 4, 64])
        sq_ = snl.unsqueeze(1).to_broadcast([128, 4, 64])
        vq3 = vtt(rr(a_q3, "p (a b) -> p a b", a=4),
                  rr(a_q1, "p (a b) -> p a b", a=4), cq, ALU.mult,
                  waits=(vq, vr1))
        vq3 = vtt(rr(a_q2, "p (a b) -> p a b", a=4),
                  rr(a_q2, "p (a b) -> p a b", a=4), sq_, ALU.mult,
                  waits=(vq3,))
        vq3 = vtt(a_q3, a_q3, a_q2, ALU.add, waits=(vq3,))
        vk = vop(lambda e, sc=sc: e.scalar_tensor_tensor(
            a_q1[:, 0:128], ps(1)[:, 0:128], a_rsg[:, sc:sc + 1], a_bk,
            op0=ALU.mult, op1=ALU.add), waits=(lkm, bk_l, vq3))
        vr2 = vts(a_q2[:, 0:64], a_q1[:, 64:128], -1.0, None, ALU.mult,
                  waits=(vk,))
        vr2 = vcopy(a_q2[:, 64:128], a_q1[:, 0:64], waits=(vr2,))
        ck = csl.unsqueeze(1).to_broadcast([128, 2, 64])
        sk = snl.unsqueeze(1).to_broadcast([128, 2, 64])
        vk3 = vtt(rr(a_q2[:, 128:256], "p (a b) -> p a b", a=2),
                  rr(a_q1[:, 0:128], "p (a b) -> p a b", a=2), ck, ALU.mult,
                  waits=(vr2,))
        vk3 = vtt(rr(a_q2[:, 0:128], "p (a b) -> p a b", a=2),
                  rr(a_q2[:, 0:128], "p (a b) -> p a b", a=2), sk, ALU.mult,
                  waits=(vk3,))
        vk3 = vtt(a_q2[:, 128:256], a_q2[:, 128:256], a_q2[:, 0:128],
                  ALU.add, waits=(vk3,))
        vcq = vcopy(a_q1.bitcast(BF16)[:, 0:256], a_q3, waits=(vq3, vk3))
        vck = vcopy(a_q1.bitcast(BF16)[:, 256:384], a_q2[:, 128:256],
                    waits=(vcq,))
        tq1 = tr(psb(6)[:, 0:128], a_q1.bitcast(BF16)[:, 0:128], c_idb.ap(),
                 waits=(vcq, prevq))
        tq2 = tr(psb(6)[:, 128:256], a_q1.bitcast(BF16)[:, 128:256],
                 c_idb.ap())
        tq3 = tr(psb(6)[:, 256:384], a_q1.bitcast(BF16)[:, 256:384],
                 c_idb.ap(), waits=(vck,), inc=("TE", 1))
        e1 = vcopy(a_qf[:, sc * 128:(sc + 1) * 128], psb(6)[:, 0:128],
                   waits=(tq3,))
        e2 = vcopy(a_qf[:, 2048 + sc * 128:2048 + (sc + 1) * 128],
                   psb(6)[:, 128:256], waits=(tq3,))
        prevq = vcopy(a_kf[:, sc * 128:(sc + 1) * 128], psb(6)[:, 256:384],
                      waits=(tq3, e1, e2))
    p.barrier()

    a_sr = AR.f32(2048)
    a_se = AR.bf16(2048)
    a_at = AR.bf16(128)
    prevat = None
    for qh in range(2):
        for scq in range(16):
            nv = (scq + 1) * 128
            lsm = None
            for tg in range((nv + 511) // 512):
                n = min(512, nv - tg * 512)
                lsm = mm(ps(tg)[:, 0:n],
                         a_qf[:, qh * 2048 + scq * 128:
                              qh * 2048 + scq * 128 + 128],
                         a_kf[:, tg * 512:tg * 512 + n], True, True,
                         waits=(prevat,) if tg == 0 else ())
            evs = []
            for tg in range((nv + 511) // 512):
                n = min(512, nv - tg * 512)
                evs.append(vcopy(a_sr[:, tg * 512:tg * 512 + n],
                                 ps(tg)[:, 0:n], waits=(lsm,)))
            vmx4 = vop(lambda e, nv=nv: e.reduce_max(
                a_cnb[:, 27:28], a_sr[:, 0:nv], axis=mybir.AxisListType.X),
                waits=tuple(evs))
            vnm4 = vts(a_cnb[:, 28:29], a_cnb[:, 27:28],
                       -float(128) ** -0.5, None, ALU.mult, waits=(vmx4,))
            sex4 = sact(a_se[:, 0:nv], a_sr[:, 0:nv], AF.Exp,
                        bias=a_cnb[:, 28:29], scale=float(128) ** -0.5,
                        waits=(vnm4,))
            vzm = vtt(a_se[:, scq * 128:nv], a_se[:, scq * 128:nv],
                      c_tri.ap(), ALU.mult, waits=(sex4,))
            vsm4 = vop(lambda e, nv=nv: e.reduce_sum(
                a_cnb[:, 29:30], a_se[:, 0:nv], axis=mybir.AxisListType.X),
                waits=(vzm,))
            vrc4 = vop(lambda e: e.reciprocal(a_cnb[:, 30:31],
                                              a_cnb[:, 29:30]),
                       waits=(vsm4,))
            oacc = None
            prevev = None
            for tcq in range(scq + 1):
                ta = tr(psb(6)[:, 0:128],
                        a_se[:, tcq * 128:(tcq + 1) * 128], c_idb.ap(),
                        waits=(vzm, prevev), inc=("TE", 1))
                eva = vcopy(a_at, psb(6)[:, 0:128], waits=(ta,))
                p.wait("tensor", eva)
                if tcq == 0:
                    p.wait("tensor", prevat)
                oacc = mm(ps(7)[:, 0:128], a_at,
                          a_vt[:, tcq * 128:(tcq + 1) * 128],
                          tcq == 0, tcq == scq)
                prevev = oacc
            vo4 = vop(lambda e, qh=qh, scq=scq: e.tensor_scalar(
                a_am[:, scq * 256 + qh * 128:scq * 256 + qh * 128 + 128],
                ps(7)[:, 0:128], a_cnb[:, 30:31], None, op0=ALU.mult),
                waits=(oacc, vrc4))
            prevat = vo4
    p.barrier()
    AR.pop()
    da2 = p.dma("sync", rr(a2a_i, "(c p) m -> p c m", p=128),
                sb3(a_am), "SS", waits=(prevat,))
    p.wait("gpsimd", da2)
    cc2 = p.op("gpsimd", lambda e: e.collective_compute(
        "AllToAll", ALU.bypass, replica_groups=[list(range(NCORES))],
        ins=[a2a_i.opt()], outs=[a2a_o.opt()]), inc=("CC", 1))
    p.barrier()

    a_xm = AR.bf16(16 * 256)
    a_x1 = AR.bf16(16 * 256)
    a_dl = a_xm
    AR.push()
    a_ax = AR.bf16(16 * 256)
    a_blk = AR.bf16(512)
    prevbe = None
    bl = None
    for blk in range(8):
        bl = p.dma("sync", rr(a_blk, "p (h m) -> p h m", h=2),
                   rr(a2a_o[blk * 256:(blk + 1) * 256, :],
                      "(h p) m -> p h m", p=128), "LD",
                   waits=(cc2, bl, prevbe))
        tts = []
        for h2 in range(2):
            for m2 in range(2):
                tts.append(tr(psb(6)[:, (h2 * 2 + m2) * 128:
                                     (h2 * 2 + m2) * 128 + 128],
                              a_blk[:, h2 * 256 + m2 * 128:
                                    h2 * 256 + m2 * 128 + 128],
                              c_idb.ap(),
                              waits=(bl, prevbe) if h2 + m2 == 0 else (),
                              inc=("TE", 1) if h2 * 2 + m2 == 3 else None))
        prevbe = vcopy(
            rr(a_ax[:, blk * 512:(blk + 1) * 512], "p (c q) -> p c q", c=2),
            rr(psb(6)[:, 0:512], "p (h m q) -> p m (h q)", h=2, m=2)
            if False else
            rr(psb(6)[:, 0:512], "p (h m q) -> p m h q", h=2, m=2),
            waits=(tts[-1],))
    a_slm = AR.bf16(2176)
    prevxm = None
    for rh in range(2):
        ml = p.dma("sync", a_slm, ag1_i[rh * 128:(rh + 1) * 128, :], "LD",
                   waits=(prevxm,))
        for g in range(4):
            tt = None
            for j in range(4):
                dc = g * 4 + j
                w = [ml, prevxm] if j == 0 else []
                tt = tr(psb(7)[:, j * 128:(j + 1) * 128],
                        a_slm[:, dc * 128:(dc + 1) * 128], c_idb.ap(),
                        waits=w, inc=("TE", 1) if j == 3 else None)
            prevxm = vcopy(
                rr(a_xm, "p (c q) -> p c q", c=16)
                [:, g * 4:(g + 1) * 4, rh * 128:(rh + 1) * 128],
                sb3(psb(7)[:, 0:512], c=4), waits=(tt,))
    p.barrier()
    wtk4 = [None, None]
    wls4 = [None, None]
    pvx1 = [None, None, None, None]
    for si in range(8):
        buf = si % 2
        wtk4[buf] = p.dma("gpsimd",
                          rr(wst[buf].ap()[:, 0:16 * 256],
                             "p (c m) -> p c m", c=16),
                          rr(wo1, "(c p) m -> p c m", p=128)
                          [:, :, si * 256:(si + 1) * 256],
                          WSEMS[si % 4],
                          waits=(wtk4[buf], wls4[buf]))
        for msub in range(2):
            dm = si * 2 + msub
            bank = dm % 4
            ltw = None
            for cc in range(16):
                w = (wtk4[buf], prevbe, prevxm, pvx1[bank])                     if cc == 0 else ()
                ltw = mm(ps(bank)[:, 0:256],
                         wst[buf].ap()[:, cc * 256 + msub * 128:
                                       cc * 256 + msub * 128 + 128],
                         a_ax[:, cc * 256:(cc + 1) * 256], cc == 0, cc == 15,
                         waits=w)
            wls4[buf] = ltw
            vx1 = vtt(a_x1[:, dm * 256:(dm + 1) * 256],
                      a_xm[:, dm * 256:(dm + 1) * 256],
                      ps(bank)[:, 0:256], ALU.add, waits=(ltw,))
            pvx1[bank] = vx1
    p.barrier()
    AR.pop()
    AR.push()
    a_h2 = AR.bf16(16 * 256)
    a_sq4 = AR.bf16(256)
    msq = None
    sq5 = None
    for dm in range(16):
        sq5 = sact(a_sq4, a_x1[:, dm * 256:(dm + 1) * 256], AF.Square,
                   waits=(msq,))
        msq = mm(ps(4)[0:1, 0:256], c_ocb.ap(), a_sq4, dm == 0, dm == 15,
                 waits=(sq5,), inc=("TE", 1))
    vq5 = vts(srow.ap()[0:1, 128:384], ps(4)[0:1, 0:256], 1.0 / D, 1e-6,
              ALU.mult, ALU.add, waits=(msq,))
    vq5 = vop(lambda e: e.reciprocal(srow.ap()[0:1, 128:384],
                                     srow.ap()[0:1, 128:384]), waits=(vq5,))
    sq6 = sact(srow.ap()[0:1, 128:384], srow.ap()[0:1, 128:384], AF.Sqrt,
               waits=(vq5,))
    vq6 = vcopy(srow.ap().bitcast(BF16)[0:1, 768:1024],
                srow.ap()[0:1, 128:384], waits=(sq6,))
    brs = mm(ps(5)[:, 0:256], c_orb.ap(),
             srow.ap().bitcast(BF16)[0:1, 768:1024], True, True,
             waits=(vq6,))
    h2toks = []
    for dm in range(16):
        dl2 = p.dma("sync", a_cnb[:, 26:27],
                    ln2_c[dm * 128:(dm + 1) * 128, :], "LD",
                    waits=(h2toks[-1] if h2toks else None,))
        h2toks.append(vop(lambda e, dm=dm: e.scalar_tensor_tensor(
            a_h2[:, dm * 256:(dm + 1) * 256],
            a_x1[:, dm * 256:(dm + 1) * 256], a_cnb[:, 26:27],
            ps(5)[:, 0:256], op0=ALU.mult, op1=ALU.mult),
            waits=(dl2, brs)))
    tok_h2 = h2toks[-1]
    p.barrier()
    a_gu = AR.bf16(64 * 256)
    gu_toks = []
    wg_l = [None, None]
    wu_l = [None, None]
    wg_last = [None, None]
    wu_last = [None, None]
    for ic2 in range(32):
        half = ic2 % 2
        hof = half * 4096
        wg_l[half] = p.dma("gpsimd",
                           rr(wst[0].ap()[:, hof:hof + 4096],
                              "p (c m) -> p c m", c=16),
                           rr(w_gate, "(c p) m -> p c m", p=128)
                           [:, :, ic2 * 256:(ic2 + 1) * 256],
                           "LW" if half == 0 else "LV",
                           waits=(wg_l[half], wg_last[half]))
        wu_l[half] = p.dma("gpsimd",
                           rr(wst[1].ap()[:, hof:hof + 4096],
                              "p (c m) -> p c m", c=16),
                           rr(w_up, "(c p) m -> p c m", p=128)
                           [:, :, ic2 * 256:(ic2 + 1) * 256],
                           "LZ" if half == 0 else "LU",
                           waits=(wu_l[half], wu_last[half]))
        for isub in range(2):
            ii = ic2 * 2 + isub
            bg, bu = (0, 1) if ii % 2 == 0 else (2, 3)
            lg = lu = None
            for dc in range(16):
                w = ()
                if dc == 0:
                    w = (wg_l[half], wu_l[half], tok_h2,
                         gu_toks[-2] if len(gu_toks) >= 2 else None)
                lg = mm(ps(bg)[:, 0:256],
                        wst[0].ap()[:, hof + dc * 256 + isub * 128:
                                    hof + dc * 256 + isub * 128 + 128],
                        a_h2[:, dc * 256:(dc + 1) * 256], dc == 0, dc == 15,
                        waits=w)
                lu = mm(ps(bu)[:, 0:256],
                        wst[1].ap()[:, hof + dc * 256 + isub * 128:
                                    hof + dc * 256 + isub * 128 + 128],
                        a_h2[:, dc * 256:(dc + 1) * 256], dc == 0, dc == 15)
            wg_last[half] = lg
            wu_last[half] = lu
            sg5 = sact(a_gu[:, ii * 256:(ii + 1) * 256], ps(bg)[:, 0:256],
                       AF.Silu, waits=(lg, gu_toks[-4] if len(gu_toks) >= 4
                                       else None))
            gu_toks.append(vop(lambda e, ii=ii, bu=bu: e.tensor_tensor(
                a_gu[:, ii * 256:(ii + 1) * 256],
                a_gu[:, ii * 256:(ii + 1) * 256], ps(bu)[:, 0:256],
                op=ALU.mult), waits=(sg5, lu)))
    tok_gu = gu_toks[-1]
    p.barrier()
    wd_l = [None, None]
    wdl_last = [None, None]
    dl_toks = []
    for si in range(16):
        buf = si % 2
        wd_l[buf] = p.dma("gpsimd",
                          rr(wst[buf].ap(), "p (c m) -> p c m", c=64),
                          rr(w_down, "(c p) m -> p c m", p=128)
                          [:, :, si * 128:(si + 1) * 128],
                          WSEMS[si % 4],
                          waits=(wd_l[buf], wdl_last[buf]))
        dm = si
        bank = dm % 4
        ltd = None
        for icc in range(64):
            w = ()
            if icc == 0:
                w = (wd_l[buf], tok_gu,
                     dl_toks[-4] if len(dl_toks) >= 4 else None)
            ltd = mm(ps(bank)[:, 0:256],
                     wst[buf].ap()[:, icc * 128:(icc + 1) * 128],
                     a_gu[:, icc * 256:(icc + 1) * 256], icc == 0, icc == 63,
                     waits=w)
        wdl_last[buf] = ltd
        vpr = vtt(a_x1[:, dm * 256:(dm + 1) * 256],
                  a_x1[:, dm * 256:(dm + 1) * 256], ps(bank)[:, 0:256],
                  ALU.add, waits=(ltd, dl_toks[-1] if dl_toks else None))
        vd1 = vtt(a_x1[:, dm * 256:(dm + 1) * 256],
                  a_x1[:, dm * 256:(dm + 1) * 256],
                  a_xm[:, dm * 256:(dm + 1) * 256], ALU.subtract,
                  waits=(vpr,))
        dl_toks.append(vtt(a_dl[:, dm * 256:(dm + 1) * 256],
                           a_x1[:, dm * 256:(dm + 1) * 256], a_gb,
                           ALU.mult, waits=(vd1, vgb2)))
    tok_dl = dl_toks[-1]
    p.barrier()
    AR.pop()
    a_dt = AR.bf16(2 * 2048)
    a_pr = AR.bf16(2 * 2048)
    prevg = None
    for dm in range(16):
        tt = tr(psb(6)[:, 0:128], a_dl[:, dm * 256:dm * 256 + 128],
                c_idb.ap(), waits=(tok_dl, prevg))
        tt = tr(psb(6)[:, 128:256], a_dl[:, dm * 256 + 128:dm * 256 + 256],
                c_idb.ap(), inc=("TE", 1))
        prevg = vcopy(rr(a_dt, "p (h q) -> p h q", h=2)
                      [:, :, dm * 128:(dm + 1) * 128],
                      sb3(psb(6)[:, 0:256], c=2), waits=(tt,))
    for tc in range(16):
        tt = tr(psb(6)[:, 0:128], a_ptm[:, tc * 256:tc * 256 + 128],
                c_idb.ap(), waits=(prevg,) if tc == 0 else (prevg,))
        tt = tr(psb(6)[:, 128:256], a_ptm[:, tc * 256 + 128:tc * 256 + 256],
                c_idb.ap(), inc=("TE", 1))
        prevg = vcopy(rr(a_pr, "p (h q) -> p h q", h=2)
                      [:, :, tc * 128:(tc + 1) * 128],
                      sb3(psb(6)[:, 0:256], c=2), waits=(tt,))
    a_oo = AR.f32(2048)
    last_tt5 = None
    prevo = None
    prev_v5 = [None, None, None, None]
    for tc in range(16):
        tl = p.dma("sync", xin.ap(), xfm_v(tc), "LD", waits=(last_tt5,))
        evs = []
        for g in range(4):
            tt = None
            for j in range(4):
                dc = g * 4 + j
                w = [tl, prev_v5[g]]
                if g == 0 and j == 0:
                    w += [prevg, prevo]
                if j != 0:
                    w = []
                tt = tr(ps(2 + g)[:, j * 128:(j + 1) * 128],
                        sb3(xin.ap())[:, dc, :], c_idf.ap(),
                        waits=w, inc=("TE", 1) if j == 3 else None)
            evs.append(tt)
        last_tt5 = tt
        for dsl in range(4):
            sm = None
            for rh in range(2):
                w = []
                if rh == 0:
                    w = [prev_v5[dsl - 2] if dsl >= 2 else prev_v5[dsl],
                         prevg, prevo]
                sm = mm(ps(0 if dsl % 2 == 0 else 1)[:, 0:512],
                        a_pr[:, rh * 2048 + tc * 128:
                             rh * 2048 + tc * 128 + 128],
                        a_dt[:, rh * 2048 + dsl * 512:
                             rh * 2048 + dsl * 512 + 512],
                        rh == 0, rh == 1, waits=w)
            v5 = vop(lambda e, tc=tc, g=dsl: e.tensor_scalar(
                a_oo[:, g * 512:(g + 1) * 512], ps(2 + g)[:, 0:512],
                sg1[:, tc:tc + 1], mgc[:, tc:tc + 1], op0=ALU.mult,
                op1=ALU.subtract), waits=(evs[dsl], vsg1, dsg, prevo))
            v5 = vop(lambda e, g=dsl: e.tensor_tensor(
                a_oo[:, g * 512:(g + 1) * 512],
                a_oo[:, g * 512:(g + 1) * 512],
                ps(0 if g % 2 == 0 else 1)[:, 0:512], op=ALU.add),
                waits=(v5, sm))
            prev_v5[dsl] = v5
            prevo2 = v5
        p.wait("sync", prevo2)
        prevo = p.dma("sync", out[tc * 128:(tc + 1) * 128, :], a_oo,
                      "SS" if tc % 2 == 0 else "SZ")
    AR.pop()

    p.wait("sync", ("SS", p.semv["SS"]), ("SZ", p.semv["SZ"]))
    p.wait("gpsimd", ("SG", p.semv["SG"]), ("LG", p.semv["LG"]),
           ("LW", p.semv["LW"]), ("LZ", p.semv["LZ"]),
           ("LV", p.semv["LV"]), ("LU", p.semv["LU"]))

    p.playback()
    return nc, sorted(tap_out.keys())


def host_inputs(inputs, core):
    f = np.float32
    import ml_dtypes
    bf = ml_dtypes.bfloat16

    def c(a, dt=f):
        return np.ascontiguousarray(a, dtype=dt)

    i = inputs
    m = {}
    m["u"] = c(i["L0_attn_u"][core])
    m["w_in"] = c(i["L0_attn_W_in"])
    m["conv_wt"] = c(i["L0_attn_conv_w"].T)
    m["conv_bc"] = c(i["L0_attn_conv_b"].reshape(CONVD, 1))
    m["dtb_c"] = c(i["L0_attn_dt_bias"].reshape(NH, 1))
    m["alog_c"] = c(i["L0_attn_A_log"].reshape(NH, 1))
    m["d_row"] = c(i["L0_attn_D"].reshape(1, NH))
    m["ns_row"] = c(i["L0_attn_norm_scale"].reshape(1, DIN))
    m["w_out0"] = c(i["L0_attn_W_out"])
    m["a_all"] = c(np.transpose(i["L0_ffn_A"], (2, 0, 1)).reshape(D, 512))
    m["b_all"] = c(np.transpose(i["L0_ffn_B"], (0, 2, 1)).reshape(512, D))
    m["wr0"] = c(i["L0_ffn_Wr"].T)
    m["w_qkv_t"] = c(i["L1_attn_W_qkv"].T)
    m["w_out1_t"] = c(i["L1_attn_W_out"].T)
    ow = np.ones((D, 2), f)
    ow[:, 1] = i["L1_ffn_w_router"]
    m["onesw1"] = c(ow)
    m["wq_my"] = c(i["L1_ffn_wq"][:, core * 256:(core + 1) * 256])
    m["wk_my"] = c(i["L1_ffn_wk"][:, core * 128:(core + 1) * 128])
    m["wv_my"] = c(i["L1_ffn_wv"][:, core * 128:(core + 1) * 128])
    m["bq_my"] = c(i["L1_ffn_bq"][core * 256:(core + 1) * 256].reshape(1, 256))
    m["bk_my"] = c(i["L1_ffn_bk"][core * 128:(core + 1) * 128].reshape(1, 128))
    m["bv_my"] = c(i["L1_ffn_bv"][core * 128:(core + 1) * 128].reshape(1, 128))
    m["wo1"] = c(i["L1_ffn_wo"])
    m["w_gate"] = c(i["L1_ffn_w_gate"])
    m["w_up"] = c(i["L1_ffn_w_up"])
    m["w_down"] = c(i["L1_ffn_w_down"])
    m["ln1_c"] = c(i["L1_ffn_ln1_w"].reshape(D, 1))
    m["ln2_c"] = c(i["L1_ffn_ln2_w"].reshape(D, 1))
    m["ident_f"] = c(np.eye(128))
    m["ident_b"] = c(np.eye(128), bf)
    m["ident_t"] = c(np.tile(np.eye(128), (1, 4)), bf)
    ngm = np.zeros((128, 128), f)
    for l in range(128):
        ngm[l, l + 1:] = -1e9
    m["negmask"] = c(ngm, bf)
    m["tril_in"] = c(np.tril(np.ones((128, 128))), bf)
    m["iota_r"] = c(np.arange(256).reshape(1, 256))
    ind = np.zeros((NH, NH * CH), f)
    for h in range(NH):
        ind[h, h * CH:(h + 1) * CH] = 1.0
    m["ind64"] = c(ind, bf)
    pos = np.arange(T, dtype=np.float64)
    invf = 1.0 / (10000.0 ** (np.arange(0, 128, 2, dtype=np.float64) / 128.0))
    ang = pos[:, None] * invf[None, :]
    m["ropetab"] = c(np.concatenate([np.cos(ang), np.sin(ang)], axis=1), bf)
    m["ones_rb"] = c(np.ones((1, 128)), bf)
    m["ones_cb"] = c(np.ones((128, 1)), bf)
    return m


_BUILD_CACHE = {}


def run(inputs, taps=()):
    key = tuple(sorted(taps))
    if key not in _BUILD_CACHE:
        with allow_ici_all_to_all():
            _BUILD_CACHE[key] = build_program(taps)
    nc, tapnames = _BUILD_CACHE[key]
    in_maps = [host_inputs(inputs, c) for c in range(NCORES)]
    res = run_bass_kernel_spmd(nc, in_maps, core_ids=list(range(NCORES)))
    return res


def kernel(**inputs):
    res = run(inputs)
    return np.stack([res.results[c]["out"] for c in range(NCORES)], axis=0)


# version 09 (final)
# speedup vs baseline: 1.1280x; 1.0001x over previous
import numpy as np
from contextlib import ExitStack

import concourse.bass as bass
import concourse.mybir as mybir
from concourse.bass_utils import run_bass_kernel_spmd
from concourse.aot_env import allow_ici_all_to_all

F32 = mybir.dt.float32
BF16 = mybir.dt.bfloat16
AF = mybir.ActivationFunctionType
ALU = mybir.AluOpType

T = 2048
D = 2048
DIN = 4096
NH = 64
HD = 64
NST = 128
CONVD = DIN + 2 * NST
DPROJ = 2 * DIN + 2 * NST + NH
CH = 128
NCH = T // CH
RMS_EPS = 1e-5
LN_EPS = 1e-5
NCORES = 8

ENGINES = ("sync", "tensor", "vector", "scalar", "gpsimd")
DMA_SEMS = {"LD", "LG", "LW", "LZ", "LX", "LY", "SS", "SG"}


class Prog:
    def __init__(self, nc):
        self.nc = nc
        self.q = {e: [] for e in ENGINES}
        self.stack = ExitStack()
        self.sems = {}
        self.semv = {}

    def sem(self, name):
        if name not in self.sems:
            self.sems[name] = self.stack.enter_context(self.nc.semaphore(name))
            self.semv[name] = 0
        return self.sems[name]

    def op(self, eng, fn, inc=None, waits=()):
        h = self.sems
        wl = []
        for w in waits:
            if w is None:
                continue
            sn, v = w
            if sn in DMA_SEMS:
                v = self.semv[sn]
            wl.append((sn, v))
        seen = {}
        for sn, v in wl:
            seen[sn] = max(v, seen.get(sn, -1))
        wl = sorted(seen.items())

        def play(e, fn=fn, wl=wl, inc=inc):
            for (sn, v) in wl:
                e.wait_ge(h[sn], v)
            inst = fn(e)
            if inc is not None:
                inst = inst.then_inc(h[inc[0]], inc[1])
            return inst

        self.q[eng].append(play)
        if inc is not None:
            self.semv[inc[0]] += inc[1]
            return (inc[0], self.semv[inc[0]])
        return None

    def wait(self, eng, *toks):
        h = self.sems
        tl = []
        for t in toks:
            if t is None:
                continue
            sn, v = t
            if sn in DMA_SEMS:
                v = self.semv[sn]
            tl.append((sn, v))
        seen = {}
        for sn, v in tl:
            seen[sn] = max(v, seen.get(sn, -1))
        tl = sorted(seen.items())
        if not tl:
            return

        def play(e, toks=tl):
            for (sn, v) in toks:
                e.wait_ge(h[sn], v)

        self.q[eng].append(play)

    def dma(self, eng, out, in_, semname, waits=(), slow=False):
        waits = tuple(waits) + ((semname, self.semv[semname]),)
        return self.op(eng,
                       lambda e: e.dma_start(out=out, in_=in_,
                                             allow_slow_non_contiguous=slow),
                       inc=(semname, 16), waits=waits)

    def light_barrier(self):
        self.wait("vector", ("VE", self.semv["VE"]))
        self.wait("scalar", ("SC", self.semv["SC"]))
        self.wait("tensor", ("TE", self.semv["TE"]))
        self.wait("gpsimd", ("GE", self.semv["GE"]))
        for e in ENGINES:
            self.op(e, lambda en: en.nop(), inc=("PH", 1))
        v = self.semv["PH"]
        for e in ENGINES:
            self.wait(e, ("PH", v))

    def barrier(self):
        self.wait("sync", ("SS", self.semv["SS"]), ("LD", self.semv["LD"]))
        self.wait("gpsimd", ("SG", self.semv["SG"]), ("LG", self.semv["LG"]),
                  ("LW", self.semv["LW"]), ("LZ", self.semv["LZ"]))
        self.wait("vector", ("VE", self.semv["VE"]))
        self.wait("scalar", ("SC", self.semv["SC"]))
        self.wait("tensor", ("TE", self.semv["TE"]))
        self.wait("gpsimd", ("GE", self.semv["GE"]))
        for e in ENGINES:
            self.op(e, lambda en: en.nop(), inc=("PH", 1))
        v = self.semv["PH"]
        for e in ENGINES:
            self.wait(e, ("PH", v))

    def playback(self):
        with self.nc.Block() as block:
            for e in ENGINES:
                ops = self.q[e]
                getattr(block, e)(lambda eng, ops=ops: [f(eng) for f in ops])


class Arena:
    def __init__(self, nc, name, kb):
        self.n = kb * 1024 // 2
        self.t = nc.alloc_sbuf_tensor(name, [128, self.n], BF16)
        self.off = 0
        self.marks = []

    def push(self):
        self.marks.append(self.off)

    def pop(self):
        self.off = self.marks.pop()

    def bf16(self, cols):
        c = (cols + 63) & ~63
        assert self.off + c <= self.n, f"arena overflow {self.off}+{c}>{self.n}"
        ap = self.t.ap()[:, self.off:self.off + cols]
        self.off += c
        return ap

    def f32(self, cols):
        return self.bf16(2 * cols).bitcast(F32)


def rr(ap_, spec, **kw):
    return ap_.rearrange(spec, **kw)


def build_program(taps=()):
    taps = set(taps)
    nc = bass.Bass("TRN2", target_bir_lowering=False)
    p = Prog(nc)
    for name in ("LD", "LG", "TE", "VE", "SC", "GE", "SS", "SG", "CC", "PH",
                 "LW", "LZ", "LX", "LY"):
        p.sem(name)

    def ext(name, shape, dtype=F32):
        return nc.dram_tensor(name, list(shape), dtype, kind="ExternalInput").ap()

    u = ext("u", [T, D])
    w_in = ext("w_in", [D, DPROJ])
    conv_wt = ext("conv_wt", [CONVD, 4])
    conv_bc = ext("conv_bc", [CONVD, 1])
    dtb_c = ext("dtb_c", [NH, 1])
    alog_c = ext("alog_c", [NH, 1])
    d_row = ext("d_row", [1, NH])
    ns_row = ext("ns_row", [1, DIN])
    w_out0 = ext("w_out0", [DIN, D])
    a_all = ext("a_all", [D, 512])
    b_all = ext("b_all", [512, D])
    wr0 = ext("wr0", [D, 8])
    w_qkv_t = ext("w_qkv_t", [D, 3 * D])
    w_out1_t = ext("w_out1_t", [D, D])
    onesw1 = ext("onesw1", [D, 2])
    wq_my = ext("wq_my", [D, 256])
    wk_my = ext("wk_my", [D, 128])
    wv_my = ext("wv_my", [D, 128])
    bq_my = ext("bq_my", [1, 256])
    bk_my = ext("bk_my", [1, 128])
    bv_my = ext("bv_my", [1, 128])
    wo1 = ext("wo1", [D, D])
    w_gate = ext("w_gate", [D, 8192])
    w_up = ext("w_up", [D, 8192])
    w_down = ext("w_down", [8192, D])
    ln1_c = ext("ln1_c", [D, 1])
    ln2_c = ext("ln2_c", [D, 1])
    ident_f = ext("ident_f", [128, 128])
    ident_b = ext("ident_b", [128, 128], BF16)
    ident_t = ext("ident_t", [128, 512], BF16)
    negmask = ext("negmask", [128, 128], BF16)
    tril_in = ext("tril_in", [128, 128], BF16)
    iota_r = ext("iota_r", [1, 256])
    ind64 = ext("ind64", [NH, NH * CH], BF16)
    ropetab = ext("ropetab", [T, 128], BF16)
    ones_rb = ext("ones_rb", [1, 128], BF16)
    ones_cb = ext("ones_cb", [128, 1], BF16)

    out = nc.dram_tensor("out", [T, D], F32, kind="ExternalOutput").ap()
    tap_out = {}
    tap_specs = {"xfm1": [D, T], "xfm2": [D, T], "xfm3": [D, T],
                 "zx": [CONVD, T], "dt": [NH, T], "stats": [2, T],
                 "yhat": [NCH, 128, DIN], "zz": [T, DIN]}
    for name in taps:
        tap_out[name] = nc.dram_tensor("tap_" + name, tap_specs[name], F32,
                                       kind="ExternalOutput").ap()

    xfm_h = nc.dram_tensor("xfm_h", [D, T], F32).ap()
    xbc_h = nc.dram_tensor("xbc_h", [CONVD, T], F32).ap()
    z_h = nc.dram_tensor("z_h", [T, DIN], BF16).ap()
    yh_h = nc.dram_tensor("yh_h", [NCH, 128, DIN], BF16).ap()
    at_h = nc.dram_tensor("at_h", [D, T], BF16).ap()
    xsf_h = nc.dram_tensor("xsf_h", [D, T], BF16).ap()
    xh4_h = nc.dram_tensor("xh4_h", [T, D], BF16).ap()
    rowf_h = nc.dram_tensor("rowf_h", [1, T], F32).ap()
    rowb_h = nc.dram_tensor("rowb_h", [1, T], BF16).ap()
    ag1_i = nc.dram_tensor("ag1_i", [256, 2176], BF16).ap()
    ag1_o = nc.dram_tensor("ag1_o", [2048, 2176], BF16).ap()
    a2a_i = nc.dram_tensor("a2a_i", [2048, 256], BF16).ap()
    a2a_o = nc.dram_tensor("a2a_o", [2048, 256], BF16).ap()

    sb = nc.alloc_sbuf_tensor
    c_idf = sb("c_idf", [128, 128], F32)
    c_idb = sb("c_idb", [128, 128], BF16)
    c_idt = sb("c_idt", [128, 512], BF16)
    c_ngm = sb("c_ngm", [128, 128], BF16)
    c_tri = sb("c_tri", [128, 128], BF16)
    c_orb = sb("c_orb", [1, 128], BF16)
    c_ocb = sb("c_ocb", [128, 1], BF16)
    c_dbc = sb("c_dbc", [128, NH], BF16)
    c_nsb = sb("c_nsb", [128, DIN], BF16)
    c_cw = sb("c_cw", [128, 34 * 4], F32)
    c_cb = sb("c_cb", [128, 34], F32)
    c_sm = sb("c_sm", [128, 32], F32)
    drhs = sb("drhs", [65, NH * CH], BF16)
    dlhs_hi = sb("dlhs_hi", [65, CH], BF16)
    dlhs_lo = sb("dlhs_lo", [65, CH], BF16)
    xin = sb("xin", [128, 2048], F32)
    wst = [sb(f"wst{i}", [128, 8192], BF16) for i in range(2)]
    dtw = sb("dtw", [NH, T], F32)
    sgi_r = sb("sgi_r", [1, T], BF16)
    msg_r = sb("msg_r", [1, T], BF16)
    sx_r = sb("sx_r", [1, T], F32)
    sxx_r = sb("sxx_r", [1, T], F32)
    sco_r = sb("sco_r", [1, T], F32)
    sxw_r = sb("sxw_r", [1, T], F32)
    scr_r = sco_r
    srow = sb("srow", [1, 512], F32)
    scol = sb("scol", [128, 32], F32)
    AR = Arena(nc, "AR", 84)

    PS = [nc.alloc_psum_tensor(f"ps{i}", [128, 512], F32) for i in range(8)]

    def ps(i):
        return PS[i].ap()

    def psb(i):
        return PS[i].ap().bitcast(BF16)

    def mm(pst, lhsT, rhs, first, last, waits=(), inc="dflt"):
        if inc == "dflt":
            inc = ("TE", 1) if last else None
        return p.op("tensor",
                    lambda e: e.matmul(pst, lhsT, rhs, start=first, stop=last),
                    inc=inc, waits=waits)

    def tr(pst, src, idap, waits=(), inc=None):
        return p.op("tensor", lambda e: e.transpose(pst, src, idap),
                    inc=inc, waits=waits)

    def vop(fn, waits=(), inc=("VE", 1)):
        return p.op("vector", fn, inc=inc, waits=waits)

    def sop(fn, waits=(), inc=("SC", 1)):
        return p.op("scalar", fn, inc=inc, waits=waits)

    def vcopy(outap, inap, waits=()):
        return vop(lambda e: e.tensor_copy(outap, inap), waits=waits)

    def vtt(outap, a, b, alu, waits=()):
        return vop(lambda e: e.tensor_tensor(outap, a, b, op=alu), waits=waits)

    def vts(outap, inap, s1, s2, op0, op1=None, waits=()):
        if op1 is None:
            return vop(lambda e: e.tensor_scalar(outap, inap, s1, None,
                                                 op0=op0), waits=waits)
        return vop(lambda e: e.tensor_scalar(outap, inap, s1, s2, op0=op0,
                                             op1=op1), waits=waits)

    def vstt(outap, in0, scalar, in1, op0, op1, waits=()):
        return vop(lambda e: e.scalar_tensor_tensor(outap, in0, scalar, in1,
                                                    op0=op0, op1=op1),
                   waits=waits)

    def sact(outap, inap, func, bias=None, scale=None, accum=None, waits=()):
        kw = {}
        if bias is not None:
            kw["bias"] = bias
        if scale is not None:
            kw["scale"] = scale
        if accum is not None:
            kw["accum_out"] = accum
        return sop(lambda e: e.activation(outap, inap, func, **kw),
                   waits=waits)

    def xfm_v(tc, h=xfm_h):
        return rr(h, "(c p) t -> p c t", p=128)[:, :, tc * 128:(tc + 1) * 128]

    def sb3(ap_, c=16):
        return rr(ap_, "p (c q) -> p c q", c=c)

    def bc_in1(psum_slice, reps):
        return psum_slice.unsqueeze(1).to_broadcast(
            [128, reps, psum_slice.shape[1]])

    for dst, src in [(c_idf, ident_f), (c_idb, ident_b), (c_idt, ident_t),
                     (c_ngm, negmask), (c_tri, tril_in), (c_orb, ones_rb),
                     (c_ocb, ones_cb)]:
        p.dma("sync", dst.ap(), src, "LD")
    p.dma("sync", drhs.ap()[1:65, :], ind64, "LD")
    p.dma("sync", dlhs_hi.ap()[0:1, :], ones_rb, "LD")
    p.dma("sync", dlhs_lo.ap()[0:1, :], ones_rb, "LD")
    p.dma("sync", rr(c_cw.ap(), "p (c k) -> p c k", c=34),
          rr(conv_wt, "(c p) k -> p c k", p=128), "LD")
    p.dma("sync", c_cb.ap().unsqueeze(2),
          rr(conv_bc, "(c p) k -> p c k", p=128), "LD", slow=True)
    p.dma("sync", c_sm.ap()[0:NH, 0:1], dtb_c, "LD")
    tld_sm = p.dma("sync", c_sm.ap()[0:NH, 1:2], alog_c, "LD")
    p.dma("gpsimd", c_dbc.ap(), d_row.broadcast_to([128, NH]), "LG")
    p.dma("gpsimd", c_nsb.ap(), ns_row.broadcast_to([128, DIN]), "LG")
    tk = sact(c_sm.ap()[0:NH, 2:3], c_sm.ap()[0:NH, 1:2], AF.Exp,
              waits=(tld_sm,))
    tok_negA = vts(c_sm.ap()[0:NH, 1:2], c_sm.ap()[0:NH, 2:3], -1.0, None,
                   ALU.mult, waits=(tk,))

    def stats_chunk(src_ap, tc, xb, sq, ready=()):
        cast = vcopy(xb, src_ap, waits=ready)
        sqt = sact(sq, src_ap, AF.Square, waits=ready)
        mt = None
        for dc in range(16):
            mm(ps(4)[0:1, 0:128], c_ocb.ap(), xb[:, dc * 128:(dc + 1) * 128],
               dc == 0, dc == 15, waits=(cast,) if dc == 0 else (), inc=None)
            mt = mm(ps(6)[0:1, 0:128], c_ocb.ap(),
                    sq[:, dc * 128:(dc + 1) * 128], dc == 0, dc == 15,
                    waits=(sqt,) if dc == 0 else ())
        tcs = slice(tc * 128, (tc + 1) * 128)
        v1 = vcopy(sx_r.ap()[0:1, tcs], ps(4)[0:1, 0:128], waits=(mt,))
        v2 = vcopy(sxx_r.ap()[0:1, tcs], ps(6)[0:1, 0:128], waits=(mt,))
        return (v1, v2)

    def finalize_stats(ready=()):
        v = vts(sx_r.ap(), sx_r.ap(), 1.0 / D, None, ALU.mult, waits=ready)
        v2 = vts(sxx_r.ap(), sxx_r.ap(), 1.0 / D, None, ALU.mult,
                 waits=ready)
        sq = sact(scr_r.ap(), sx_r.ap(), AF.Square, waits=(v,))
        v3 = vtt(sxx_r.ap(), sxx_r.ap(), scr_r.ap(), ALU.subtract,
                 waits=(v2, sq))
        v3 = vts(sxx_r.ap(), sxx_r.ap(), LN_EPS, None, ALU.add, waits=(v3,))
        v3 = vop(lambda e: e.reciprocal(sxx_r.ap(), sxx_r.ap()), waits=(v3,))
        s1 = sact(sgi_r.ap(), sxx_r.ap(), AF.Sqrt, waits=(v3,))
        v4 = vtt(msg_r.ap(), sx_r.ap(), sgi_r.ap(), ALU.mult, waits=(s1,))
        return (s1, v4)

    AR.push()
    a_xb = AR.bf16(2048)
    a_sq = AR.bf16(2048)
    a_xo = AR.f32(2048)
    last_tt = None
    prev_toks = ()
    for tc in range(16):
        tl = p.dma("sync", xin.ap(), u[tc * 128:(tc + 1) * 128, :], "LD",
                   waits=(last_tt,))
        evs = []
        for g in range(4):
            tt = None
            for j in range(4):
                dc = g * 4 + j
                w = list(prev_toks) + [tl] if j == 0 else []
                tt = tr(ps(g)[:, j * 128:(j + 1) * 128],
                        xin.ap()[:, dc * 128:(dc + 1) * 128], c_idf.ap(),
                        waits=w, inc=("TE", 1) if j == 3 else None)
            evs.append(vcopy(a_xo[:, g * 512:(g + 1) * 512], ps(g),
                             waits=(tt,)))
        last_tt = tt
        p.wait("sync", *evs)
        st = p.dma("sync", xfm_v(tc), sb3(a_xo), "SS")
        sc = stats_chunk(a_xo, tc, a_xb, a_sq, ready=tuple(evs))
        prev_toks = (st, sc[0], sc[1])
    AR.pop()
    tok_stats = finalize_stats(ready=prev_toks)
    p.barrier()

    AR.push()
    xh1 = AR.bf16(16 * 2048)

    def build_xhat(xh_dst, extra=()):
        prev = None
        toks = []
        for tc in range(16):
            tl = p.dma("sync", xin.ap(), xfm_v(tc), "LD", waits=(prev,))
            b1 = mm(ps(5)[:, 0:128], c_orb.ap(),
                    sgi_r.ap()[0:1, tc * 128:(tc + 1) * 128], True, True,
                    waits=(prev,) + tuple(extra), inc=None)
            b2 = mm(ps(5)[:, 128:256], c_orb.ap(),
                    msg_r.ap()[0:1, tc * 128:(tc + 1) * 128], True, True)
            dst = rr(xh_dst, "p (c q) -> p c q", c=16)[:, :,
                                                       tc * 128:(tc + 1) * 128]
            v1 = vtt(dst, sb3(xin.ap()), bc_in1(ps(5)[:, 0:128], 16),
                     ALU.mult, waits=(tl, b2))
            prev = vtt(dst, dst, bc_in1(ps(5)[:, 128:256], 16),
                       ALU.subtract, waits=(v1,))
            toks.append(prev)
        return toks

    xh1_done = build_xhat(xh1, extra=tok_stats)
    tok_xh1 = xh1_done[-1]

    a_ev = AR.bf16(2 * 2048)
    a_cv = AR.bf16(2 * 2048)

    def wld(buf, m0, wdt, wsrc, waits=()):
        return p.dma("gpsimd",
                     rr(wst[buf].ap()[:, 0:16 * wdt], "p (c m) -> p c m", c=16),
                     rr(wsrc, "(c p) m -> p c m", p=128)[:, :, m0:m0 + wdt],
                     "LW" if buf == 0 else "LZ", waits=waits)

    wtoks = [None, None]
    wlast = [None, None]
    conv_done = None
    prev_cv = [None, None]
    prev_stripe_evs = []
    for si in range(17):
        m0 = DIN + si * 256
        buf = si % 2
        wtoks[buf] = wld(buf, m0, 256, w_in, waits=(wtoks[buf], wlast[buf]))
        ev_toks = []
        for ms in range(2):
            for tq in range(4):
                j = ms * 4 + tq
                bank = j % 6
                lt = None
                for dc in range(16):
                    w = [wtoks[buf], tok_xh1] if dc == 0 else []
                    if dc == 0 and j >= 6:
                        w.append(ev_toks[j - 6])
                    if dc == 0 and j < 6 and prev_stripe_evs:
                        w.append(prev_stripe_evs[j + 6 if j < 2 else j])
                    lt = mm(ps(bank)[:, 0:512],
                            wst[buf].ap()[:, dc * 256 + ms * 128:
                                          dc * 256 + ms * 128 + 128],
                            xh1[:, dc * 2048 + tq * 512:
                                dc * 2048 + tq * 512 + 512],
                            dc == 0, dc == 15, waits=w)
                wlast[buf] = lt
                ev_toks.append(vcopy(
                    a_ev[:, ms * 2048 + tq * 512:ms * 2048 + tq * 512 + 512],
                    ps(bank)[:, 0:512], waits=(lt, prev_cv[ms])))
        st_toks = []
        ncv = [None, None]
        for ms in range(2):
            pc = si * 2 + ms
            src = a_ev[:, ms * 2048:(ms + 1) * 2048]
            dstc = a_cv[:, ms * 2048:(ms + 1) * 2048]
            w3 = c_cw.ap()[:, pc * 4 + 3:pc * 4 + 4]
            bcol = c_cb.ap()[:, pc:pc + 1]
            v = vts(dstc, src, w3, bcol, ALU.mult, ALU.add,
                    waits=tuple(ev_toks) + ((conv_done,) if conv_done else ()))
            for i in (2, 1, 0):
                sh = 3 - i
                wi = c_cw.ap()[:, pc * 4 + i:pc * 4 + i + 1]
                v = vstt(dstc[:, sh:2048], src[:, 0:2048 - sh], wi,
                         dstc[:, sh:2048], ALU.mult, ALU.add, waits=(v,))
            ncv[ms] = v
            sl = sact(dstc, dstc, AF.Silu, waits=(v,))
            p.wait("gpsimd", sl)
            st_toks.append(p.dma("gpsimd", xbc_h[pc * 128:(pc + 1) * 128, :],
                                 dstc, "SG"))
        conv_done = st_toks[-1]
        prev_cv = ncv
        prev_stripe_evs = ev_toks

    wdtok = wld(0, DPROJ - 64, 64, w_in, waits=(wtoks[0], wlast[0]))
    lt = None
    last_dt_mm = None
    for tq in range(4):
        ltq = None
        for dc in range(16):
            w = [wdtok, lt] if dc == 0 else []
            if dc == 0 and lt is None:
                w.append(prev_stripe_evs[3])
            ltq = mm(ps(3)[0:64, 0:512],
                     wst[0].ap()[:, dc * 64:dc * 64 + 64],
                     xh1[:, dc * 2048 + tq * 512:dc * 2048 + tq * 512 + 512],
                     dc == 0, dc == 15, waits=w)
        last_dt_mm = ltq
        lt = vcopy(dtw.ap()[:, tq * 512:(tq + 1) * 512], ps(3)[0:64, 0:512],
                   waits=(ltq,))
    spa = a_ev.bitcast(F32)[0:NH, 0:T]
    spb = a_cv.bitcast(F32)[0:NH, 0:T]
    s1 = sact(spa, dtw.ap(), AF.Abs, bias=c_sm.ap()[0:NH, 0:1],
              waits=(lt, tok_negA, conv_done))
    s2 = sact(spa, spa, AF.Exp, scale=-1.0, waits=(s1,))
    s3 = sact(spa, spa, AF.Ln, bias=1.0, waits=(s2,))
    v1 = vts(spb, dtw.ap(), c_sm.ap()[0:NH, 0:1], 0.0, ALU.add, ALU.max,
             waits=(lt, conv_done))
    tok_aw = vtt(dtw.ap(), spa, spb, ALU.add, waits=(s3, v1))

    a_zr = [AR.bf16(512), AR.bf16(512)]
    prev_z2 = [None, None]
    zev = [prev_stripe_evs[6], prev_stripe_evs[7], prev_stripe_evs[2], lt]
    wlast = [last_dt_mm, wlast[1]]
    for zs in range(8):
        buf = zs % 2
        wtoks[buf] = wld(buf, zs * 512, 512, w_in,
                         waits=(wtoks[buf], wlast[buf]))
        for tc in range(16):
            bank = tc % 4
            ltz = None
            for dc in range(16):
                w = [wtoks[buf]] if dc == 0 else []
                if dc == 0:
                    w.append(zev[bank])
                ltz = mm(ps(bank)[:, 0:512],
                         xh1[:, dc * 2048 + tc * 128:dc * 2048 + tc * 128 + 128],
                         wst[buf].ap()[:, dc * 512:(dc + 1) * 512],
                         dc == 0, dc == 15, waits=w)
            wlast[buf] = ltz
            zb = tc % 2
            ev = vcopy(a_zr[zb], ps(bank)[:, 0:512],
                       waits=(ltz, prev_z2[zb]))
            zev[bank] = ev
            p.wait("sync", ev)
            prev_z2[zb] = p.dma("sync",
                                z_h[tc * 128:(tc + 1) * 128,
                                    zs * 512:(zs + 1) * 512],
                                a_zr[zb], "SS")
    AR.pop()
    p.barrier()

    if taps & {"zx", "dt", "zz", "stats"}:
        AR.push()
        tb = AR.f32(4096)
        if "zx" in tap_out:
            pst = None
            for pc in range(34):
                tl = p.dma("sync", tb[:, 0:2048],
                           xbc_h[pc * 128:(pc + 1) * 128, :], "LD",
                           waits=(pst,))
                pst = p.dma("sync", tap_out["zx"][pc * 128:(pc + 1) * 128, :],
                            tb[:, 0:2048], "SS", waits=(tl,))
        p.barrier()
        if "zz" in tap_out:
            pst = None
            for tcc in range(16):
                tl = p.dma("gpsimd", tb[:, 0:4096],
                           z_h[tcc * 128:(tcc + 1) * 128, :], "LG",
                           waits=(pst,))
                pst = p.dma("gpsimd",
                            tap_out["zz"][tcc * 128:(tcc + 1) * 128, :],
                            tb[:, 0:4096], "SG", waits=(tl,))
        p.barrier()
        if "dt" in tap_out:
            p.dma("sync", tap_out["dt"], dtw.ap(), "SS")
        if "stats" in tap_out:
            p.barrier()
            tl = p.dma("gpsimd", tb[0:1, 0:2048], sgi_r.ap(), "LG")
            p.dma("gpsimd", tap_out["stats"][0:1, :], tb[0:1, 0:2048], "SG",
                  waits=(tl,))
            tl = p.dma("gpsimd", tb[0:1, 2048:4096], msg_r.ap(), "LG")
            p.dma("gpsimd", tap_out["stats"][1:2, :], tb[0:1, 2048:4096], "SG",
                  waits=(tl,))
        AR.pop()
        p.barrier()

    AR.push()
    a_xbc = AR.bf16(34 * 128)
    a_z = AR.bf16(4096)
    a_xtm = AR.bf16(4096)
    a_xti = AR.bf16(4096)
    a_xbr = AR.bf16(4096)
    a_mt = AR.bf16(NH * CH)
    a_st = AR.bf16(4096)
    a_stt = AR.bf16(512)
    a_y = AR.bf16(4096)
    a_yf = a_xtm
    a_sq2 = a_xtm
    a_acs = AR.f32(CH)
    a_acss = AR.f32(CH)
    a_ahw = AR.bf16(CH)
    a_alw = AR.f32(CH)
    a_stk = AR.f32(128)
    a_tm1 = AR.f32(128)
    a_dec = AR.f32(CH)
    a_eac = AR.f32(CH)
    a_gt = AR.bf16(128)
    a_btm = AR.bf16(128)
    a_dtb = AR.bf16(64)
    a_ect = AR.bf16(64)
    a_dct = AR.bf16(64)
    a_gmb = AR.f32(64)
    a_aw = AR.f32(CH)
    a_nhb = AR.bf16(CH)
    a_nlb = AR.bf16(CH)
    a_me = [AR.bf16(512) for _ in range(2)]
    srow_b = srow.ap().bitcast(BF16)

    prev_chunk = {"xbc": None, "z": None, "drhs": None, "y": None,
                  "acs": None, "yst": None}
    for ci in range(NCH):
        tsl = slice(ci * 128, (ci + 1) * 128)
        xl = []
        for part in range(6):
            lo = part * 6
            hi = min(34, lo + 6)
            xl.append(p.dma(
                "gpsimd", rr(a_xbc, "p (c q) -> p c q", c=34)[:, lo:hi, :],
                rr(xbc_h, "(c p) t -> p c t", p=128)[:, lo:hi, tsl],
                "LG", waits=(prev_chunk["xbc"],)))
        tok_xbc = xl[-1]
        tok_z = p.dma("sync", a_z, z_h[tsl, :], "LD",
                      waits=(prev_chunk["z"],))

        v = vts(a_aw[0:NH, :], dtw.ap()[:, tsl], c_sm.ap()[0:NH, 1:2], None,
                ALU.mult, waits=(tok_aw, prev_chunk["acs"]))
        v = vop(lambda e: e.tensor_tensor_scan(
            a_acs[0:NH, :], a_aw[0:NH, :], a_aw[0:NH, :], 0.0,
            op0=ALU.add, op1=ALU.bypass), waits=(v,))
        v = vts(a_acss[0:NH, :], a_acs[0:NH, :], a_acs[0:NH, 63:64], None,
                ALU.subtract, waits=(v,))
        vh = vcopy(a_ahw[0:NH, :], a_acss[0:NH, :], waits=(v,))
        vl = vtt(a_alw[0:NH, :], a_acss[0:NH, :], a_ahw[0:NH, :],
                 ALU.subtract, waits=(vh,))
        t_hi = vts(a_nhb[0:NH, :], a_ahw[0:NH, :], -1.0, None, ALU.mult,
                   waits=(vh, prev_chunk["drhs"]))
        t_hi = p.dma("sync", dlhs_hi.ap()[1:65, :], a_nhb[0:NH, :], "SS",
                     waits=(t_hi,))
        t_lo = vts(a_nlb[0:NH, :], a_alw[0:NH, :], -1.0, None, ALU.mult,
                   waits=(vl,))
        t_lo = p.dma("sync", dlhs_lo.ap()[1:65, :], a_nlb[0:NH, :], "SS",
                     waits=(t_lo,))
        a_alb = a_acss.bitcast(BF16)
        v_lob = vcopy(a_alb[0:NH, 0:CH], a_alw[0:NH, :], waits=(vl,))
        vd = vts(a_dec[0:NH, :], a_acs[0:NH, :], a_acs[0:NH, 127:128], -1.0,
                 ALU.subtract, ALU.mult, waits=(v,))
        sd = sact(a_dec[0:NH, :], a_dec[0:NH, :], AF.Exp, waits=(vd,))
        se = sact(a_eac[0:NH, :], a_acs[0:NH, :], AF.Exp, waits=(v,))
        c1 = vcopy(a_stk[0:64, :], dtw.ap()[:, tsl], waits=())
        d2 = p.dma("sync", a_stk[64:128, :], a_dec[0:NH, :], "SS", waits=(sd,))
        tt = tr(ps(6)[:, 0:128], a_stk, c_idf.ap(), waits=(c1, d2),
                inc=("TE", 1))
        e1 = vcopy(a_tm1, ps(6)[:, 0:128], waits=(tt,))
        cv_dt = vcopy(a_dtb, a_tm1[:, 0:64], waits=(e1,))
        cv_dc = vcopy(a_dct, a_tm1[:, 64:128], waits=(e1,))
        c2 = vcopy(a_stk[0:64, :], a_eac[0:NH, :], waits=(se, e1))
        tt = tr(ps(6)[:, 0:128], a_stk, c_idf.ap(), waits=(c2,),
                inc=("TE", 1))
        e2 = vcopy(a_tm1, ps(6)[:, 0:128], waits=(tt,))
        cv_ec = vcopy(a_ect, a_tm1[:, 0:64], waits=(e2,))
        d4 = p.dma("sync", srow.ap()[0:1, 0:64], a_eac[0:NH, 127:128], "SS",
                   waits=(se,))
        vgr = vcopy(srow_b[0:1, 256:320], srow.ap()[0:1, 0:64], waits=(d4,))
        bg = mm(ps(6)[:, 128:192], c_orb.ap(), srow_b[0:1, 256:320], True,
                True, waits=(vgr,))
        vgb = vcopy(a_gmb, ps(6)[:, 128:192], waits=(bg,))

        B_fm = a_xbc[:, 32 * 128:33 * 128]
        C_fm = a_xbc[:, 33 * 128:34 * 128]
        tb = tr(psb(6)[:, 768:896], B_fm, c_idb.ap(), waits=(tok_xbc,),
                inc=("TE", 1))
        eb = vcopy(a_btm, psb(6)[:, 768:896], waits=(tb,))
        tg = mm(ps(6)[:, 192:320], B_fm, C_fm, True, True, waits=(tok_xbc,))
        eg = vcopy(a_gt, ps(6)[:, 192:320], waits=(tg,))
        evx = []
        for g in range(8):
            tt = None
            for j in range(4):
                pc = g * 4 + j
                w = [xl[min(pc // 6, 5)]]
                if j == 0:
                    w.append(evx[-1] if evx else prev_chunk["y"])
                tt = tr(psb(7)[:, j * 128:(j + 1) * 128],
                        a_xbc[:, pc * 128:(pc + 1) * 128], c_idb.ap(),
                        waits=w, inc=("TE", 1) if j == 3 else None)
            evx.append(vcopy(a_xtm[:, g * 512:(g + 1) * 512],
                             psb(7)[:, 0:512], waits=(tt,)))
        dt_ap = a_dtb.unsqueeze(2).to_broadcast([128, 64, 64])
        dc_ap = a_dct.unsqueeze(2).to_broadcast([128, 64, 64])
        vxt = vtt(rr(a_xti, "p (h q) -> p h q", h=64),
                  rr(a_xtm, "p (h q) -> p h q", h=64), dt_ap, ALU.mult,
                  waits=(evx[-1], cv_dt))
        vxb = vtt(rr(a_xbr, "p (h q) -> p h q", h=64),
                  rr(a_xti, "p (h q) -> p h q", h=64), dc_ap, ALU.mult,
                  waits=(vxt, cv_dc))

        mt_toks = []
        gate_banks = (evx[-1], eg, eb, vgb, cv_ec, cv_dt, cv_dc)
        tok_dmm = None
        for half in range(2):
            hh0 = half * 32
            rsl = slice(half * 32 * CH, (half + 1) * 32 * CH)
            fh = p.dma("sync", drhs.ap()[0:1, rsl],
                       a_ahw[hh0:hh0 + 32, :], "SS",
                       waits=(vh, prev_chunk["drhs"], tok_dmm))
            hi_last = None
            for g8 in range(8):
                g = half * 8 + g8
                cs = slice(g * 512, (g + 1) * 512)
                w0 = []
                if g8 == 0:
                    w0 += [t_hi, fh]
                    if mt_toks:
                        w0.append(mt_toks[-1][1])
                if half == 0 and g8 >= 6:
                    w0 += list(gate_banks)
                hi_last = mm(ps(g8)[:, 0:512], dlhs_hi.ap(),
                             drhs.ap()[:, cs], True, False, waits=w0,
                             inc=("TE", 1))
            fl = p.dma("sync", drhs.ap()[0:1, rsl],
                       a_alb[hh0:hh0 + 32, 0:CH], "SS",
                       waits=(hi_last, v_lob))
            for g8 in range(8):
                g = half * 8 + g8
                cs = slice(g * 512, (g + 1) * 512)
                mm(ps(g8)[:, 0:512], dlhs_lo.ap(), drhs.ap()[:, cs],
                   False, False, waits=(t_lo, fl) if g8 == 0 else (),
                   inc=None)
                tD = mm(ps(g8)[:, 0:512], c_ngm.ap(), c_idt.ap(), False, True)
                sE = sact(a_me[g8 % 2], ps(g8), AF.Exp,
                          waits=(tD,) + ((mt_toks[-2][1],)
                                         if len(mt_toks) >= 2 else ()))
                gt_ap = a_gt.unsqueeze(1).to_broadcast([128, 4, 128])
                mv = vtt(rr(a_mt[:, cs], "p (h l) -> p h l", h=4),
                         rr(a_me[g8 % 2], "p (h l) -> p h l", h=4), gt_ap,
                         ALU.mult, waits=(sE, eg))
                mt_toks.append((sE, mv))
            tok_dmm = mt_toks[-1][0]
        tok_mt = mt_toks[-1][1]

        yoff_done = []
        st_upd = []
        last_soff = None
        for hg in range(8):
            hs = slice(hg * 512, (hg + 1) * 512)
            if ci > 0:
                w = [tok_xbc, st_upd[hg - 1] if hg else None, last_soff]
                if hg == 0:
                    w.append(tok_dmm)
                last_soff = mm(ps(2)[:, 0:512], C_fm, a_st[:, hs], True, True,
                               waits=w)
                ect_ap = a_ect[:, hg * 8:hg * 8 + 8].unsqueeze(2).to_broadcast(
                    [128, 8, 64])
                voff = vtt(rr(a_stt, "p (h q) -> p h q", h=8),
                           rr(ps(2)[:, 0:512], "p (h q) -> p h q", h=8),
                           ect_ap, ALU.mult,
                           waits=(last_soff, cv_ec,
                                  yoff_done[hg - 1] if hg else None))
            else:
                voff = None
            stt = None
            for hh in range(8):
                h = hg * 8 + hh
                w = [vxb, eb] if hh == 0 else []
                if hh == 0 and st_upd:
                    w.append(st_upd[-1])
                if hh == 0 and hg == 0 and ci == 0:
                    w.append(tok_dmm)
                stt = mm(ps(4)[:, hh * 64:(hh + 1) * 64], a_btm,
                         a_xbr[:, h * 64:(h + 1) * 64], True, True,
                         waits=w, inc=("TE", 1) if hh == 7 else None)
            if ci == 0:
                su = vcopy(a_st[:, hs], ps(4)[:, 0:512], waits=(stt,))
            else:
                gm_ap = a_gmb[:, hg * 8:hg * 8 + 8].unsqueeze(2).to_broadcast(
                    [128, 8, 64])
                su1 = vtt(rr(a_st[:, hs], "p (h q) -> p h q", h=8),
                          rr(a_st[:, hs], "p (h q) -> p h q", h=8), gm_ap,
                          ALU.mult, waits=(stt, voff, vgb))
                su = vop(lambda e, hs=hs: e.tensor_tensor(
                    a_st[:, hs], a_st[:, hs], ps(4)[:, 0:512], op=ALU.add),
                    waits=(su1,))
            st_upd.append(su)
            if ci > 0:
                yoff_done.append(vcopy(a_y[:, hs], a_stt, waits=(voff,)))
            else:
                yoff_done.append(None)
        db_ap = c_dbc.ap().unsqueeze(2).to_broadcast([128, 64, 64])
        vxd = vtt(rr(a_xbr, "p (h q) -> p h q", h=64),
                  rr(a_xtm, "p (h q) -> p h q", h=64), db_ap, ALU.mult,
                  waits=(st_upd[-1],))
        y_toks = []
        for hg in range(8):
            hs = slice(hg * 512, (hg + 1) * 512)
            bank = 3 if hg % 2 == 0 else 5
            w = [vxd, tok_mt]
            if hg == 0:
                w.append(tok_dmm)
            if len(y_toks) >= 2:
                w.append(y_toks[-2])
            td = None
            for hh in range(8):
                h = hg * 8 + hh
                mm(ps(bank)[:, hh * 64:(hh + 1) * 64], c_idb.ap(),
                   a_xbr[:, h * 64:(h + 1) * 64], True, False,
                   waits=w if hh == 0 else (), inc=None)
                td = mm(ps(bank)[:, hh * 64:(hh + 1) * 64],
                        a_mt[:, h * 128:(h + 1) * 128],
                        a_xti[:, h * 64:(h + 1) * 64], False, True,
                        inc=("TE", 1) if hh == 7 else None)
            if ci > 0:
                yv = vop(lambda e, hs=hs, bank=bank: e.tensor_tensor(
                    a_y[:, hs], a_y[:, hs], ps(bank)[:, 0:512], op=ALU.add),
                    waits=(td, yoff_done[hg]))
            else:
                yv = vcopy(a_y[:, hs], ps(bank)[:, 0:512],
                           waits=(td,))
            y_toks.append(yv)

        slz = sact(a_z, a_z, AF.Silu, waits=(tok_z,))
        vg = vtt(a_y, a_y, a_z, ALU.mult, waits=(slz,) + tuple(y_toks))
        sq = sact(a_sq2, a_y, AF.Square, accum=scol.ap()[:, 4:5], waits=(vg,))
        vr = vts(scol.ap()[:, 5:6], scol.ap()[:, 4:5], 1.0 / DIN, RMS_EPS,
                 ALU.mult, ALU.add, waits=(sq,))
        vr = vop(lambda e: e.reciprocal(scol.ap()[:, 5:6], scol.ap()[:, 5:6]),
                 waits=(vr,))
        sr = sact(scol.ap()[:, 6:7], scol.ap()[:, 5:6], AF.Sqrt, waits=(vr,))
        vy = vstt(a_y, a_y, scol.ap()[:, 6:7], c_nsb.ap(), ALU.mult, ALU.mult,
                  waits=(sr,))
        evy = []
        for g in range(8):
            tt = None
            for j in range(4):
                pc = g * 4 + j
                w = [vy, prev_chunk["yst"]] if j == 0 and g == 0 else []
                if j == 0 and evy:
                    w.append(evy[-1])
                tt = tr(psb(7)[:, j * 128:(j + 1) * 128],
                        a_y[:, pc * 128:(pc + 1) * 128], c_idb.ap(),
                        waits=w, inc=("TE", 1) if j == 3 else None)
            evy.append(vcopy(a_yf[:, g * 512:(g + 1) * 512], psb(7)[:, 0:512],
                             waits=(tt,)))
        p.wait("sync", *evy)
        sty = p.dma("sync", yh_h[ci], a_yf, "SS")

        p.light_barrier()
        prev_chunk = {"xbc": y_toks[-1], "z": vg,
                      "drhs": mt_toks[-1][0], "y": y_toks[-1],
                      "acs": cv_ec, "yst": sty}

    AR.pop()
    p.barrier()

    if "yhat" in tap_out:
        AR.push()
        tbb = AR.bf16(4096)
        tb = AR.f32(4096)
        pst = None
        for ci in range(NCH):
            tl = p.dma("sync", tbb, yh_h[ci], "LD", waits=(pst,))
            cv = vcopy(tb, tbb, waits=(tl, pst))
            p.wait("sync", cv)
            pst = p.dma("sync", tap_out["yhat"][ci], tb, "SS")
        AR.pop()
        p.barrier()

    def outproj_pass(w_src, w_rows, dst_h, store_stats=True, src_fm=None,
                     scale=None, wrow=None):
        npc = w_rows // 128
        sw = 256
        AR.push()
        a_yb = AR.bf16(4 * 4096)
        a_xi = AR.f32(8192)
        a_xo2 = AR.f32(512)
        a_cb = AR.bf16(512)
        a_cs = AR.bf16(512)
        a_c2 = AR.f32(512)
        wtk = [None, None]
        wlst = [None, None]
        mst = None
        msx = None
        prev_st = None
        opev = [None, None, None, None]
        psv1 = None
        psv2 = None
        for tq in range(4):
            ygate = (wlst[0], wlst[1])
            ybl = []
            if src_fm is None:
                for j in range(4):
                    ybl.append(p.dma("sync", a_yb[:, j * 4096:(j + 1) * 4096],
                                     yh_h[tq * 4 + j], "LY", waits=ygate))
            else:
                ybl.append(p.dma(
                    "sync",
                    rr(a_yb[:, 0:npc * 512], "p (c q) -> p c q", c=npc),
                    rr(src_fm, "(c p) t -> p c t", p=128)
                    [:, :, tq * 512:(tq + 1) * 512], "LY", waits=ygate))
            xls = []
            for j in range(4):
                xls.append(p.dma("sync",
                                 rr(a_xi, "p (j c q) -> p j c q", j=4, c=16)
                                 [:, j, :, :],
                                 xfm_v(tq * 4 + j), "LX",
                                 waits=(opev[3],)))
            for si in range(8):
                buf = si % 2
                wtk[buf] = p.dma(
                    "gpsimd",
                    rr(wst[buf].ap()[:, 0:npc * sw], "p (c m) -> p c m",
                       c=npc),
                    rr(w_src, "(c p) m -> p c m", p=128)
                    [:, :, si * sw:(si + 1) * sw],
                    "LW" if buf == 0 else "LZ",
                    waits=(wtk[buf], wlst[buf]))
                for msub in range(2):
                    dm = si * 2 + msub
                    bank = dm % 4
                    ltp = None
                    for pcc in range(npc):
                        w = [wtk[buf]] + ybl + xls if pcc == 0 else []
                        if pcc == 0:
                            w.append(opev[bank])
                        if src_fm is None:
                            rhs = rr(a_yb, "p (j c q) -> p c j q", j=4,
                                     c=npc)[:, pcc, :, :]
                        else:
                            rhs = rr(a_yb[:, 0:npc * 512], "p (c q) -> p c q",
                                     c=npc)[:, pcc, :]
                        ltp = mm(ps(bank)[:, 0:512],
                                 wst[buf].ap()[:, pcc * sw + msub * 128:
                                               pcc * sw + msub * 128 + 128],
                                 rhs, pcc == 0, pcc == npc - 1, waits=w)
                    wlst[buf] = ltp
                    xold = rr(a_xi, "p (j c q) -> p c j q", j=4,
                              c=16)[:, dm, :, :]
                    if scale is None:
                        ev = vtt(sb3(a_xo2, c=4), xold,
                                 sb3(ps(bank)[:, 0:512], c=4), ALU.add,
                                 waits=(ltp, prev_st))
                    else:
                        ev = vop(lambda e, xold=xold, bank=bank:
                                 e.scalar_tensor_tensor(
                                     sb3(a_xo2, c=4),
                                     sb3(ps(bank)[:, 0:512], c=4), scale,
                                     xold, op0=ALU.mult, op1=ALU.add),
                                 waits=(ltp, prev_st))
                    opev[bank] = ev
                    p.wait("sync", ev)
                    prev_st = p.dma("sync",
                                    rr(dst_h, "(c p) t -> p c t", p=128)
                                    [:, dm, tq * 512:(tq + 1) * 512],
                                    a_xo2, "SS")
                    if store_stats:
                        cb = vcopy(a_cb, a_xo2, waits=(msx, ev))
                        sq = sact(a_cs, a_xo2, AF.Square, waits=(mst, ev))
                        wsx = (cb, psv1) if dm == 0 else (cb,)
                        if wrow is None:
                            msx = mm(ps(6)[0:1, 0:512], c_ocb.ap(), a_cb,
                                     dm == 0, dm == 15, waits=wsx,
                                     inc=("TE", 1))
                        else:
                            msx = mm(ps(6)[0:2, 0:512],
                                     wrow[:, dm * 2:(dm + 1) * 2], a_cb,
                                     dm == 0, dm == 15, waits=wsx,
                                     inc=("TE", 1))
                        mst = mm(ps(7)[0:1, 0:512], c_ocb.ap(), a_cs,
                                 dm == 0, dm == 15,
                                 waits=(sq, psv2) if dm == 0 else (sq,))
            if store_stats:
                tcs = slice(tq * 512, (tq + 1) * 512)
                if wrow is None:
                    sv1 = vcopy(sx_r.ap()[0:1, tcs], ps(6)[0:1, 0:512],
                                waits=(msx, mst))
                else:
                    c2v = vcopy(a_c2[0:2, :], ps(6)[0:2, 0:512],
                                waits=(msx, mst))
                    sv1 = vcopy(sx_r.ap()[0:1, tcs], a_c2[0:1, :],
                                waits=(c2v,))
                    p.dma("sync", sxw_r.ap()[0:1, tcs], a_c2[1:2, :], "SS",
                          waits=(c2v,))
                sv2 = vcopy(sxx_r.ap()[0:1, tcs], ps(7)[0:1, 0:512],
                            waits=(mst,))
                psv1 = sv1 if wrow is None else c2v
                psv2 = sv2
        AR.pop()
        return (sv1, sv2)

    st_toks = outproj_pass(w_out0, DIN, xfm_h)
    p.barrier()
    tok_stats = finalize_stats(ready=st_toks)
    p.barrier()

    if "xfm1" in tap_out:
        AR.push()
        tb = AR.f32(2048)
        pst = None
        for dc in range(16):
            tl = p.dma("sync", tb[:, 0:2048],
                       xfm_h[dc * 128:(dc + 1) * 128, :], "LD", waits=(pst,))
            pst = p.dma("sync", tap_out["xfm1"][dc * 128:(dc + 1) * 128, :],
                        tb[:, 0:2048], "SS", waits=(tl,))
        AR.pop()
        p.barrier()

    AR.push()
    wl_a = p.dma("gpsimd",
                 rr(wst[0].ap(), "p (c m) -> p c m", c=16),
                 rr(a_all, "(c p) m -> p c m", p=128), "LW")
    wl_b = p.dma("gpsimd",
                 rr(wst[1].ap(), "p (c m) -> p c m", c=4),
                 rr(b_all, "(c p) m -> p c m", p=128), "LZ")
    a_wr = AR.bf16(128)
    wl_r = p.dma("gpsimd", rr(a_wr, "p (c m) -> p c m", c=16),
                 rr(wr0, "(c p) m -> p c m", p=128), "LW")
    a_xh2 = AR.bf16(2048)
    a_g2 = AR.bf16(512)
    a_gf = AR.bf16(8192)
    a_e8 = AR.f32(16)
    a_i8 = AR.bf16(64)
    prevv = None
    prev_gt = None
    prev_lr = ()
    vg_hist = [None, None]
    for tc in range(16):
        xl = p.dma("sync", xin.ap(), xfm_v(tc), "LD", waits=(prevv,))
        b1 = mm(ps(5)[:, 0:128], c_orb.ap(),
                sgi_r.ap()[0:1, tc * 128:(tc + 1) * 128], True, True,
                waits=(prevv,) + tuple(tok_stats), inc=None)
        b2 = mm(ps(5)[:, 128:256], c_orb.ap(),
                msg_r.ap()[0:1, tc * 128:(tc + 1) * 128], True, True)
        v1 = vtt(sb3(a_xh2), sb3(xin.ap()), bc_in1(ps(5)[:, 0:128], 16),
                 ALU.mult, waits=(xl, b2, prevv))
        vxh = vtt(sb3(a_xh2), sb3(a_xh2), bc_in1(ps(5)[:, 128:256], 16),
                  ALU.subtract, waits=(v1,))
        bank = tc % 2
        lth = None
        ltr = None
        for dc in range(16):
            w = [vxh, wl_a, prev_gt, vg_hist[bank]] if dc == 0 else []
            lth = mm(ps(bank)[:, 0:512],
                     a_xh2[:, dc * 128:(dc + 1) * 128],
                     wst[0].ap()[:, dc * 512:(dc + 1) * 512],
                     dc == 0, dc == 15, waits=w)
            w = [wl_r] + list(prev_lr) if dc == 0 else []
            ltr = mm(ps(2)[:, 0:8], a_xh2[:, dc * 128:(dc + 1) * 128],
                     a_wr[:, dc * 8:(dc + 1) * 8], dc == 0, dc == 15,
                     waits=w)
        vm = vop(lambda e: e.reduce_max(a_e8[:, 0:1], ps(2)[:, 0:8],
                                        axis=mybir.AxisListType.X),
                 waits=(ltr, prevv))
        vnm = vts(a_e8[:, 1:2], a_e8[:, 0:1], -1.0, None, ALU.mult,
                  waits=(vm,))
        sE = sact(a_e8[:, 8:16], ps(2)[:, 0:8], AF.Exp,
                  bias=a_e8[:, 1:2], accum=a_e8[:, 2:3], waits=(vnm,))
        vw = vop(lambda e: e.reciprocal(a_e8[:, 3:4], a_e8[:, 2:3]),
                 waits=(sE,))
        vw = vts(a_e8[:, 3:4], a_e8[:, 3:4], 1.0 / 64.0, None, ALU.mult,
                 waits=(vw,))
        vi = vts(a_i8[:, 0:8], ps(2)[:, 0:8], a_e8[:, 0:1], None, ALU.is_ge,
                 waits=(vm,))
        vg = vop(lambda e: e.scalar_tensor_tensor(
            rr(a_g2, "p (e r) -> p e r", e=8),
            rr(ps(0 if bank == 0 else 1)[:, 0:512], "p (e r) -> p e r", e=8)
            if False else rr(ps(bank)[:, 0:512], "p (e r) -> p e r", e=8),
            a_e8[:, 3:4],
            a_i8[:, 0:8].unsqueeze(2).to_broadcast([128, 8, 64]),
            op0=ALU.mult, op1=ALU.mult), waits=(lth, vw, vi))
        gtt = None
        for j in range(4):
            gtt = tr(psb(3)[:, j * 128:(j + 1) * 128],
                     a_g2[:, j * 128:(j + 1) * 128], c_idb.ap(),
                     waits=(vg, prevv) if j == 0 else (),
                     inc=("TE", 1) if j == 3 else None)
        prevv = vcopy(rr(a_gf, "p (c q) -> p c q", c=4)
                      [:, :, tc * 128:(tc + 1) * 128],
                      sb3(psb(3)[:, 0:512], c=4), waits=(gtt,))
        prev_gt = gtt
        prev_lr = (vi, sE, vw)
        vg_hist[bank] = vg
    a_xi2 = AR.f32(8192)
    a_xo3 = AR.f32(512)
    a_cb2 = AR.bf16(512)
    a_cs2 = AR.bf16(512)
    p.wait("tensor", prevv)
    sv1 = sv2 = None
    p.barrier()
    mst = None
    msx = None
    prev_st = None
    opev2 = [None, None, None, None]
    for tq in range(4):
        xls = []
        for j in range(4):
            xls.append(p.dma("sync",
                             rr(a_xi2, "p (j c q) -> p j c q", j=4, c=16)
                             [:, j, :, :], xfm_v(tq * 4 + j), "LX",
                             waits=(opev2[3],)))
        for dm in range(16):
            bank = dm % 4
            ltp = None
            for erc in range(4):
                w = [wl_b] + xls if erc == 0 else []
                if erc == 0:
                    w.append(opev2[bank])
                ltp = mm(ps(bank)[:, 0:512],
                         wst[1].ap()[:, erc * 2048 + dm * 128:
                                     erc * 2048 + dm * 128 + 128],
                         rr(a_gf, "p (c q) -> p c q", c=4)
                         [:, erc, tq * 512:(tq + 1) * 512],
                         erc == 0, erc == 3, waits=w)
            xold = rr(a_xi2, "p (j c q) -> p c j q", j=4, c=16)[:, dm, :, :]
            ev = vtt(sb3(a_xo3, c=4), xold, sb3(ps(bank)[:, 0:512], c=4),
                     ALU.add, waits=(ltp, prev_st))
            opev2[bank] = ev
            p.wait("sync", ev)
            prev_st = p.dma("sync",
                            rr(xfm_h, "(c p) t -> p c t", p=128)
                            [:, dm, tq * 512:(tq + 1) * 512], a_xo3, "SS")
            cb = vcopy(a_cb2, a_xo3, waits=(msx, ev))
            sq = sact(a_cs2, a_xo3, AF.Square, waits=(mst, ev))
            msx = mm(ps(4)[0:1, 0:512], c_ocb.ap(), a_cb2, dm == 0,
                     dm == 15, waits=(cb, sv1) if dm == 0 else (cb,),
                     inc=("TE", 1))
            mst = mm(ps(6)[0:1, 0:512], c_ocb.ap(), a_cs2, dm == 0,
                     dm == 15, waits=(sq, sv2) if dm == 0 else (sq,))
        tcs = slice(tq * 512, (tq + 1) * 512)
        sv1 = vcopy(sx_r.ap()[0:1, tcs], ps(4)[0:1, 0:512], waits=(msx, mst))
        sv2 = vcopy(sxx_r.ap()[0:1, tcs], ps(6)[0:1, 0:512], waits=(mst,))
    AR.pop()
    p.barrier()
    tok_stats = finalize_stats(ready=(sv1, sv2))
    p.barrier()

    if "xfm2" in tap_out:
        AR.push()
        tb = AR.f32(2048)
        pst = None
        for dc in range(16):
            tl = p.dma("sync", tb[:, 0:2048],
                       xfm_h[dc * 128:(dc + 1) * 128, :], "LD", waits=(pst,))
            pst = p.dma("sync", tap_out["xfm2"][dc * 128:(dc + 1) * 128, :],
                        tb[:, 0:2048], "SS", waits=(tl,))
        AR.pop()
        p.barrier()

    AR.push()
    xh3 = AR.bf16(32768)
    xh3_done = build_xhat(xh3, extra=tok_stats)
    a_pq = AR.bf16(2048)
    a_pk = AR.bf16(2048)
    a_kv = AR.bf16(2048)
    a_kvc = AR.bf16(2048)
    a_kc = AR.bf16(2048)
    wtoks = [None, None]
    wlast = [xh3_done[-1], xh3_done[-1]]
    prev_head = [None, None]
    pbank = {}
    for h in range(16):
        buf = h % 2
        wsem = "LW" if buf == 0 else "LZ"
        wv3 = rr(wst[buf].ap()[:, 0:16 * 384], "p (c m) -> p c m", c=16)
        wtoks[buf] = p.dma("gpsimd", wv3[:, :, 0:128],
                           rr(w_qkv_t, "(c p) m -> p c m", p=128)
                           [:, :, h * 128:(h + 1) * 128], wsem,
                           waits=(wtoks[buf], wlast[buf]))
        p.dma("gpsimd", wv3[:, :, 128:256],
              rr(w_qkv_t, "(c p) m -> p c m", p=128)
              [:, :, D + h * 128:D + (h + 1) * 128], wsem)
        wtoks[buf] = p.dma("gpsimd", wv3[:, :, 256:384],
                           rr(w_qkv_t, "(c p) m -> p c m", p=128)
                           [:, :, 2 * D + h * 128:2 * D + (h + 1) * 128],
                           wsem)
        ph_toks = []
        for tsl in range(4):
            t4 = slice(tsl * 512, (tsl + 1) * 512)
            bq, bk, bv = (0, 1, 2) if tsl % 2 == 0 else (3, 4, 5)
            lq = lk = lv = None
            for dc in range(16):
                w = [wtoks[buf], pbank.get(bq), pbank.get(bk),
                     pbank.get(bv)] if dc == 0 else []
                lq = mm(ps(bq)[:, 0:512],
                        wst[buf].ap()[:, dc * 384:dc * 384 + 128],
                        xh3[:, dc * 2048 + tsl * 512:
                            dc * 2048 + tsl * 512 + 512],
                        dc == 0, dc == 15, waits=w)
                lk = mm(ps(bk)[:, 0:512],
                        wst[buf].ap()[:, dc * 384 + 128:dc * 384 + 256],
                        xh3[:, dc * 2048 + tsl * 512:
                            dc * 2048 + tsl * 512 + 512],
                        dc == 0, dc == 15)
                lv = mm(ps(bv)[:, 0:512],
                        wst[buf].ap()[:, dc * 384 + 256:dc * 384 + 384],
                        xh3[:, dc * 2048 + tsl * 512:
                            dc * 2048 + tsl * 512 + 512],
                        dc == 0, dc == 15)
            prev_sex = ph_toks[-1] if ph_toks else prev_head[0]
            for (lsrc, bankq, dst) in ((lq, bq, a_pq), (lk, bk, a_pk)):
                vmn = vts(a_kvc[:, t4], ps(bankq)[:, 0:512], 0.0, None,
                          ALU.min, waits=(lsrc, prev_sex, prev_head[0],
                                          prev_head[1]))
                sEx = sact(dst[:, t4], a_kvc[:, t4], AF.Exp,
                           waits=(vmn, prev_head[0]))
                prev_sex = sEx
                tkn = vop(
                    lambda e, bankq=bankq, dst=dst, t4=t4:
                    e.scalar_tensor_tensor(dst[:, t4], ps(bankq)[:, 0:512],
                                           0.0, dst[:, t4], op0=ALU.max,
                                           op1=ALU.add), waits=(sEx,))
                ph_toks.append(tkn)
                pbank[bankq] = tkn
            tkn = vtt(a_kv[:, t4], a_pk[:, t4], ps(bv)[:, 0:512],
                      ALU.mult,
                      waits=(lv, ph_toks[-1], prev_head[0]))
            ph_toks.append(tkn)
            pbank[bv] = tkn
        sc1 = vop(lambda e: e.tensor_tensor_scan(
            a_kvc, a_kv, a_kv, 0.0, op0=ALU.add, op1=ALU.bypass),
            waits=tuple(ph_toks))
        sc2 = vop(lambda e: e.tensor_tensor_scan(
            a_kc, a_pk, a_pk, 0.0, op0=ALU.add, op1=ALU.bypass),
            waits=(sc1,))
        n1 = vtt(a_kv, a_pq, a_kvc, ALU.mult, waits=(sc1,))
        dd = vtt(a_kvc, a_pq, a_kc, ALU.mult, waits=(sc2, n1))
        dd = vts(a_kvc, a_kvc, 1e-9, None, ALU.add, waits=(dd,))
        at = dd
        for t5 in range(4):
            t4b = slice(t5 * 512, (t5 + 1) * 512)
            at = vop(lambda e, t4b=t4b: e.reciprocal(ps(3)[:, 0:512],
                                                     a_kvc[:, t4b]),
                     waits=(at,))
            at = vtt(a_pk[:, t4b], a_kv[:, t4b], ps(3)[:, 0:512], ALU.mult,
                     waits=(at,))
        p.wait("sync", at)
        sth = p.dma("sync", at_h[h * 128:(h + 1) * 128, :], a_pk, "SS")
        prev_head = [at, sth]
        pbank[3] = at
        wlast[buf] = lv
    AR.pop()
    p.barrier()
    AR.push()
    a_ow = AR.bf16(32)
    owl = p.dma("gpsimd", rr(a_ow, "p (c m) -> p c m", c=16),
                rr(onesw1, "(c p) m -> p c m", p=128), "LW")
    p.barrier()
    st_toks = outproj_pass(w_out1_t, D, xfm_h, src_fm=at_h,
                           scale=float(HD) ** -0.5, wrow=a_ow)
    p.barrier()
    tok_stats = finalize_stats(ready=st_toks)
    p.barrier()
    AR.pop()

    if "xfm3" in tap_out:
        AR.push()
        tb = AR.f32(2048)
        pst = None
        for dc in range(16):
            tl = p.dma("sync", tb[:, 0:2048],
                       xfm_h[dc * 128:(dc + 1) * 128, :], "LD", waits=(pst,))
            pst = p.dma("sync", tap_out["xfm3"][dc * 128:(dc + 1) * 128, :],
                        tb[:, 0:2048], "SS", waits=(tl,))
        AR.pop()
        p.barrier()

    AR.push()
    a_ptm = AR.bf16(16 * 256)
    a_am = AR.bf16(16 * 256)
    a_gb = AR.f32(256)
    a_sc = AR.f32(16 * 8)
    sgc = a_sc[:, 0:16]
    mgc = a_sc[:, 16:32]
    mkc = a_sc[:, 32:48]
    rkc = a_sc[:, 48:64]
    sccb = a_sc.bitcast(BF16)[:, 128:144]
    sg1 = a_sc[:, 80:96]
    a_rsg = a_sc[:, 96:112]
    a_cnb = AR.f32(32)
    AR.push()
    a_iot = AR.f32(256)
    a_orf = AR.f32(128)
    a_ow4 = AR.bf16(32)
    owl4 = p.dma("gpsimd", rr(a_ow4, "p (c m) -> p c m", c=16),
                 rr(onesw1, "(c p) m -> p c m", p=128), "LW")

    dio = p.dma("sync", a_iot, iota_r.broadcast_to([128, 256]), "SS")
    vo1 = vcopy(a_orf[0:1, 0:128], c_orb.ap())
    vo2 = vcopy(a_orf[:, 0:1], c_ocb.ap(), waits=(vo1,))

    swm = None
    for dc in range(16):
        swm = mm(ps(6)[0:1, 0:1], a_ow4[:, dc * 2 + 1:dc * 2 + 2],
                 c_ocb.ap(), dc == 0, dc == 15,
                 waits=(owl4,) if dc == 0 else ())
    vsw = vcopy(srow.ap()[0:1, 64:65], ps(6)[0:1, 0:1], waits=(swm,))
    v = vts(sco_r.ap(), sx_r.ap(), srow.ap()[0:1, 64:65], None, ALU.mult,
            waits=(vsw,) + tuple(tok_stats))
    v = vtt(sco_r.ap(), sxw_r.ap(), sco_r.ap(), ALU.subtract, waits=(v,))
    vsc = vtt(sco_r.ap(), sco_r.ap(), sgi_r.ap(), ALU.mult, waits=(v,))

    a_scb = AR.f32(16)
    dsc = p.dma("sync", a_scb, sco_r.ap(), "SS", waits=(vsc,))
    vmx = vop(lambda e: e.reduce_max(a_cnb[:, 0:1], a_scb,
                                     axis=mybir.AxisListType.X),
              waits=(dsc,))
    vmn = vop(lambda e: e.tensor_reduce(a_cnb[:, 1:2], a_scb,
                                        axis=mybir.AxisListType.X,
                                        op=ALU.min), waits=(dsc,))
    d1 = p.dma("sync", srow.ap()[0:1, 128:256], a_cnb[:, 0:1], "SS",
               waits=(vmx,))
    d2 = p.dma("sync", srow.ap()[0:1, 256:384], a_cnb[:, 1:2], "SS",
               waits=(vmn, d1))
    HI = srow.ap()[0:1, 70:71]
    LO = srow.ap()[0:1, 71:72]
    MD = srow.ap()[0:1, 72:73]
    PR = srow.ap()[0:1, 73:74]
    DD = srow.ap()[0:1, 74:75]
    NP = srow.ap()[0:1, 75:76]
    vb = vop(lambda e: e.reduce_max(HI, srow.ap()[0:1, 128:256],
                                    axis=mybir.AxisListType.X),
             waits=(d1, d2))
    vb = vts(HI, HI, 1.0, None, ALU.add, waits=(vb,))
    vb = vop(lambda e: e.tensor_reduce(LO, srow.ap()[0:1, 256:384],
                                       axis=mybir.AxisListType.X,
                                       op=ALU.min), waits=(vb,))
    vb = vts(LO, LO, -1.0, None, ALU.add, waits=(vb,))
    for it in range(36):
        vb = vts(MD, LO, HI, 0.5, ALU.add, ALU.mult, waits=(vb,))
        bmd = mm(ps(5)[:, 0:1], a_orf[0:1, 0:128], MD, True, True,
                 waits=(vb, vo1))
        vb = vts(a_cnb[:, 2:18], a_scb, ps(5)[:, 0:1], None, ALU.is_ge,
                 waits=(bmd,))
        vb = vop(lambda e: e.reduce_sum(a_cnb[:, 18:19], a_cnb[:, 2:18],
                                        axis=mybir.AxisListType.X),
                 waits=(vb,))
        cbank = 1 if it % 2 == 0 else 2
        bct = mm(ps(5)[0:1, cbank:cbank + 1], a_orf[:, 0:1],
                 a_cnb[:, 18:19], True, True, waits=(vb, vo2))
        vb = vts(PR, ps(5)[0:1, cbank:cbank + 1], 256.0, None, ALU.is_ge,
                 waits=(bct,))
        vb = vtt(DD, MD, LO, ALU.subtract, waits=(vb,))
        vb = vtt(DD, DD, PR, ALU.mult, waits=(vb,))
        vb = vtt(LO, LO, DD, ALU.add, waits=(vb,))
        vb = vts(NP, PR, -1.0, 1.0, ALU.mult, ALU.add, waits=(vb,))
        vb = vtt(DD, MD, HI, ALU.subtract, waits=(vb,))
        vb = vtt(DD, DD, NP, ALU.mult, waits=(vb,))
        vb = vtt(HI, HI, DD, ALU.add, waits=(vb,))

    vmk = vts(sxw_r.ap(), sco_r.ap(), LO, None, ALU.is_ge, waits=(vb,))
    vrk = vop(lambda e: e.tensor_tensor_scan(sx_r.ap(), sxw_r.ap(),
                                             sxw_r.ap(), 0.0, op0=ALU.add,
                                             op1=ALU.bypass), waits=(vmk,))
    vrk = vts(sx_r.ap(), sx_r.ap(), -1.0, None, ALU.add, waits=(vrk,))
    _r2c_prev = [None]

    def row_to_cols(row_ap, col_ap, waits=()):
        src_bf = row_ap.dtype == BF16
        cast = (col_ap.dtype != row_ap.dtype)
        hb = rowb_h if src_bf else rowf_h
        da = p.dma("sync", hb, row_ap, "SS",
                   waits=tuple(waits) + (_r2c_prev[0],))
        eng = "gpsimd" if cast else "sync"
        sem = "LG" if cast else "SS"
        db = p.dma(eng, col_ap, rr(hb, "o (c p) -> (o p) c", p=128),
                   sem, waits=(da,), slow=True)
        _r2c_prev[0] = db
        return db

    dmk = row_to_cols(sxw_r.ap(), mkc, waits=(vmk, vrk))
    drk = row_to_cols(sx_r.ap(), rkc, waits=(vrk,))
    dsm = row_to_cols(sgi_r.ap(), sgc)
    dsg = row_to_cols(msg_r.ap(), mgc)
    dss = row_to_cols(sco_r.ap(), sccb, waits=(vmk,))
    vsg1 = vts(sg1, sgc, 1.0, None, ALU.add, waits=(dsm,))

    ptoks = []
    for tc in range(16):
        ptoks.append(vop(lambda e, tc=tc: e.scalar_tensor_tensor(
            a_ptm[:, tc * 256:(tc + 1) * 256], a_iot, rkc[:, tc:tc + 1],
            mkc[:, tc:tc + 1].to_broadcast([128, 256]), op0=ALU.is_equal,
            op1=ALU.mult), waits=(dio, dmk, drk)))
    tok_ptm = ptoks[-1]
    p.barrier()
    AR.pop()

    AR.push()
    a_xt4 = AR.f32(2048)
    a_xh4 = AR.bf16(2048)
    prevh = None
    last_tt4 = None
    for tc in range(16):
        tl = p.dma("sync", xin.ap(), xfm_v(tc), "LD", waits=(last_tt4,))
        evs = []
        for g in range(4):
            tt = None
            for j in range(4):
                dc = g * 4 + j
                w = [tl, prevh] if j == 0 else []
                tt = tr(ps(g)[:, j * 128:(j + 1) * 128],
                        sb3(xin.ap())[:, dc, :], c_idf.ap(),
                        waits=w, inc=("TE", 1) if j == 3 else None)
            evs.append(vcopy(a_xt4[:, g * 512:(g + 1) * 512], ps(g),
                             waits=(tt,)))
        last_tt4 = tt
        vh4 = vop(lambda e, tc=tc: e.tensor_scalar(
            a_xh4, a_xt4, sgc[:, tc:tc + 1], mgc[:, tc:tc + 1],
            op0=ALU.mult, op1=ALU.subtract),
            waits=tuple(evs) + (dsm, dsg, prevh))
        p.wait("sync", vh4)
        prevh = p.dma("sync", xh4_h[tc * 128:(tc + 1) * 128, :], a_xh4, "SS")
    AR.pop()
    p.barrier()

    AR.push()
    a_xh4b = AR.bf16(2048)
    a_sel = AR.bf16(2 * 2176)
    a_rpt = AR.bf16(2048)
    gl = None
    gmm = None
    for tc in range(16):
        gl = p.dma("sync", a_xh4b, xh4_h[tc * 128:(tc + 1) * 128, :], "LX",
                   waits=(gl, gmm))
        for rh in range(2):
            for dsl in range(4):
                bank = rh * 4 + dsl
                w = ()
                if dsl == 0:
                    w = (gl, tok_ptm) if (rh == 0 and tc == 0) else (gl,)
                gmm = mm(ps(bank)[:, 0:512],
                         a_ptm[:, tc * 256 + rh * 128:
                               tc * 256 + rh * 128 + 128],
                         a_xh4b[:, dsl * 512:(dsl + 1) * 512],
                         tc == 0, tc == 15, waits=w,
                         inc=("TE", 1) if (rh == 1 and dsl == 3) else None)
    sel_evs = []
    for rh in range(2):
        for dsl in range(4):
            bank = rh * 4 + dsl
            sel_evs.append(vcopy(
                a_sel[:, rh * 2176 + dsl * 512:rh * 2176 + dsl * 512 + 512],
                ps(bank)[:, 0:512], waits=(gmm,)))
    grl = p.dma("sync", sb3(a_rpt), rr(ropetab, "(c p) m -> p c m", p=128),
                "LD", waits=())
    grm = gsm = None
    for tc in range(16):
        for rh in range(2):
            w1 = (grl, sel_evs[-1]) if tc == 0 else ()
            grm = mm(ps(rh)[:, 0:128],
                     a_ptm[:, tc * 256 + rh * 128:tc * 256 + rh * 128 + 128],
                     a_rpt[:, tc * 128:(tc + 1) * 128],
                     tc == 0, tc == 15, waits=w1)
            w2 = (dss, sel_evs[-1]) if tc == 0 else ()
            gsm = mm(ps(2 + rh)[:, 0:1],
                     a_ptm[:, tc * 256 + rh * 128:tc * 256 + rh * 128 + 128],
                     sccb[:, tc:tc + 1], tc == 0, tc == 15, waits=w2)
    rp_evs = []
    for rh in range(2):
        rp_evs.append(vcopy(a_sel[:, rh * 2176 + 2048:rh * 2176 + 2176],
                            ps(rh)[:, 0:128], waits=(grm, gsm)))
        rp_evs.append(vcopy(a_cnb[:, 20 + rh:21 + rh],
                            ps(2 + rh)[:, 0:1], waits=(grm, gsm)))
    dgs = p.dma("sync", srow.ap()[0:1, 256:384], a_cnb[:, 20:21], "SS",
                waits=tuple(rp_evs))
    dgs = p.dma("sync", srow.ap()[0:1, 384:512], a_cnb[:, 21:22], "SS",
                waits=(dgs,))
    sgt = sact(srow.ap()[0:1, 256:512], srow.ap()[0:1, 256:512], AF.Sigmoid,
               waits=(dgs,))
    vgr2 = vcopy(srow.ap().bitcast(BF16)[0:1, 0:256],
                 srow.ap()[0:1, 256:512], waits=(sgt,))
    bgb = mm(ps(6)[:, 0:256], c_orb.ap(),
             srow.ap().bitcast(BF16)[0:1, 0:256], True, True,
             waits=(vgr2,))
    vgb2 = vcopy(a_gb, ps(6)[:, 0:256], waits=(bgb,))
    dag = p.dma("sync", rr(ag1_i, "(h p) m -> p h m", p=128),
                rr(a_sel, "p (h m) -> p h m", h=2), "SS",
                waits=tuple(sel_evs) + tuple(rp_evs))
    p.wait("gpsimd", dag)
    cc1 = p.op("gpsimd", lambda e: e.collective_compute(
        "AllGather", ALU.bypass, replica_groups=[list(range(NCORES))],
        ins=[ag1_i.opt()], outs=[ag1_o.opt()]), inc=("CC", 1))
    AR.pop()
    p.barrier()

    AR.push()
    a_cs4 = AR.bf16(16 * 128)
    a_xsl = AR.bf16(2176)
    a_xff = AR.bf16(2048)
    prevx = None
    prevst = None
    for sc in range(16):
        xl4 = p.dma("sync", a_xsl, ag1_o[sc * 128:(sc + 1) * 128, :], "LD",
                    waits=(cc1, prevx))
        sq4 = sact(a_xff, a_xsl[:, 0:2048], AF.Square,
                   accum=a_cnb[:, 24:25], waits=(xl4, prevst))
        vr4 = vts(a_cnb[:, 25:26], a_cnb[:, 24:25], 1.0 / D, 1e-6,
                  ALU.mult, ALU.add, waits=(sq4,))
        vr4 = vop(lambda e: e.reciprocal(a_cnb[:, 25:26], a_cnb[:, 25:26]),
                  waits=(vr4,))
        sr4 = sact(a_rsg[:, sc:sc + 1], a_cnb[:, 25:26], AF.Sqrt,
                   waits=(vr4,))
        vcs = vcopy(a_cs4[:, sc * 128:(sc + 1) * 128], a_xsl[:, 2048:2176],
                    waits=(xl4,))
        evf = []
        for g in range(4):
            tt = None
            for j in range(4):
                dc = g * 4 + j
                w = []
                if j == 0:
                    w = [sq4, vcs, evf[-1] if evf else prevx]
                tt = tr(psb(7)[:, j * 128:(j + 1) * 128],
                        a_xsl[:, dc * 128:(dc + 1) * 128], c_idb.ap(),
                        waits=w, inc=("TE", 1) if j == 3 else None)
            evf.append(vcopy(a_xff[:, g * 512:(g + 1) * 512],
                             psb(7)[:, 0:512], waits=(tt,)))
        prevx = evf[-1]
        p.wait("sync", *evf)
        prevst = p.dma("sync",
                       rr(xsf_h, "(c p) t -> p c t", p=128)
                       [:, :, sc * 128:(sc + 1) * 128],
                       sb3(a_xff), "SS")
    p.barrier()

    a_wq = AR.bf16(16 * 256)
    a_wk = AR.bf16(16 * 128)
    a_wv = AR.bf16(16 * 128)
    a_bq = AR.bf16(256)
    a_bk = AR.bf16(128)
    a_bv = AR.bf16(128)
    a_qf = AR.bf16(2 * 2048)
    a_kf = AR.bf16(2048)
    a_vt = AR.bf16(2048)
    a_q1 = AR.f32(256)
    a_q2 = AR.f32(256)
    a_q3 = AR.f32(256)
    a_xfb = AR.bf16(2048)
    wq_l = p.dma("gpsimd", rr(a_wq, "p (c m) -> p c m", c=16),
                 rr(wq_my, "(c p) m -> p c m", p=128), "LW")
    wk_l = p.dma("gpsimd", rr(a_wk, "p (c m) -> p c m", c=16),
                 rr(wk_my, "(c p) m -> p c m", p=128), "LW")
    wv_l = p.dma("gpsimd", rr(a_wv, "p (c m) -> p c m", c=16),
                 rr(wv_my, "(c p) m -> p c m", p=128), "LW")
    bq_l = p.dma("gpsimd", a_bq, bq_my.broadcast_to([128, 256]), "LG")
    bk_l = p.dma("gpsimd", a_bk, bk_my.broadcast_to([128, 128]), "LG")
    bv_l = p.dma("gpsimd", a_bv, bv_my.broadcast_to([128, 128]), "LG")
    a_l1c = a_cnb[:, 26:27]
    wtoks4 = []
    for dc in range(16):
        dl = p.dma("sync", a_l1c, ln1_c[dc * 128:(dc + 1) * 128, :], "LD",
                   waits=(wtoks4[-1] if wtoks4 else None,))
        t1 = vts(a_wq[:, dc * 256:(dc + 1) * 256],
                 a_wq[:, dc * 256:(dc + 1) * 256], a_l1c, None, ALU.mult,
                 waits=(dl, wq_l))
        t2 = vts(a_wk[:, dc * 128:(dc + 1) * 128],
                 a_wk[:, dc * 128:(dc + 1) * 128], a_l1c, None, ALU.mult,
                 waits=(dl, wk_l, t1))
        t3 = vts(a_wv[:, dc * 128:(dc + 1) * 128],
                 a_wv[:, dc * 128:(dc + 1) * 128], a_l1c, None, ALU.mult,
                 waits=(dl, wv_l, t2))
        wtoks4.append(t3)
    tok_w4 = wtoks4[-1]
    prevq = None
    xl5 = None
    lvm = None
    for sc in range(16):
        xl5 = p.dma("sync", sb3(a_xfb),
                    rr(xsf_h, "(c p) t -> p c t", p=128)
                    [:, :, sc * 128:(sc + 1) * 128], "LX", waits=(xl5, lvm))
        lqm = lkm = None
        for dc in range(16):
            w = (xl5, tok_w4, prevq) if dc == 0 else ()
            lqm = mm(ps(0)[:, 0:256], a_xfb[:, dc * 128:(dc + 1) * 128],
                     a_wq[:, dc * 256:(dc + 1) * 256], dc == 0, dc == 15,
                     waits=w)
            lkm = mm(ps(1)[:, 0:128], a_xfb[:, dc * 128:(dc + 1) * 128],
                     a_wk[:, dc * 128:(dc + 1) * 128], dc == 0, dc == 15)
            lvm = mm(ps(2)[:, 0:128], a_xfb[:, dc * 128:(dc + 1) * 128],
                     a_wv[:, dc * 128:(dc + 1) * 128], dc == 0, dc == 15)
        vv2 = vop(lambda e, sc=sc: e.scalar_tensor_tensor(
            a_vt[:, sc * 128:(sc + 1) * 128], ps(2)[:, 0:128],
            a_rsg[:, sc:sc + 1], a_bv, op0=ALU.mult, op1=ALU.add),
            waits=(lvm, bv_l))
        vq = vop(lambda e, sc=sc: e.scalar_tensor_tensor(
            a_q1, ps(0)[:, 0:256], a_rsg[:, sc:sc + 1], a_bq,
            op0=ALU.mult, op1=ALU.add), waits=(lqm, bq_l, prevq))
        vr1 = None
        for hh in range(2):
            vr1 = vts(a_q2[:, hh * 128:hh * 128 + 64],
                      a_q1[:, hh * 128 + 64:hh * 128 + 128], -1.0, None,
                      ALU.mult, waits=(vq,))
            vr1 = vcopy(a_q2[:, hh * 128 + 64:hh * 128 + 128],
                        a_q1[:, hh * 128:hh * 128 + 64], waits=(vr1,))
        csl = a_cs4[:, sc * 128:sc * 128 + 64]
        snl = a_cs4[:, sc * 128 + 64:sc * 128 + 128]
        cq = csl.unsqueeze(1).to_broadcast([128, 4, 64])
        sq_ = snl.unsqueeze(1).to_broadcast([128, 4, 64])
        vq3 = vtt(rr(a_q3, "p (a b) -> p a b", a=4),
                  rr(a_q1, "p (a b) -> p a b", a=4), cq, ALU.mult,
                  waits=(vq, vr1))
        vq3 = vtt(rr(a_q2, "p (a b) -> p a b", a=4),
                  rr(a_q2, "p (a b) -> p a b", a=4), sq_, ALU.mult,
                  waits=(vq3,))
        vq3 = vtt(a_q3, a_q3, a_q2, ALU.add, waits=(vq3,))
        vk = vop(lambda e, sc=sc: e.scalar_tensor_tensor(
            a_q1[:, 0:128], ps(1)[:, 0:128], a_rsg[:, sc:sc + 1], a_bk,
            op0=ALU.mult, op1=ALU.add), waits=(lkm, bk_l, vq3))
        vr2 = vts(a_q2[:, 0:64], a_q1[:, 64:128], -1.0, None, ALU.mult,
                  waits=(vk,))
        vr2 = vcopy(a_q2[:, 64:128], a_q1[:, 0:64], waits=(vr2,))
        ck = csl.unsqueeze(1).to_broadcast([128, 2, 64])
        sk = snl.unsqueeze(1).to_broadcast([128, 2, 64])
        vk3 = vtt(rr(a_q2[:, 128:256], "p (a b) -> p a b", a=2),
                  rr(a_q1[:, 0:128], "p (a b) -> p a b", a=2), ck, ALU.mult,
                  waits=(vr2,))
        vk3 = vtt(rr(a_q2[:, 0:128], "p (a b) -> p a b", a=2),
                  rr(a_q2[:, 0:128], "p (a b) -> p a b", a=2), sk, ALU.mult,
                  waits=(vk3,))
        vk3 = vtt(a_q2[:, 128:256], a_q2[:, 128:256], a_q2[:, 0:128],
                  ALU.add, waits=(vk3,))
        vcq = vcopy(a_q1.bitcast(BF16)[:, 0:256], a_q3, waits=(vq3, vk3))
        vck = vcopy(a_q1.bitcast(BF16)[:, 256:384], a_q2[:, 128:256],
                    waits=(vcq,))
        tq1 = tr(psb(6)[:, 0:128], a_q1.bitcast(BF16)[:, 0:128], c_idb.ap(),
                 waits=(vcq, prevq))
        tq2 = tr(psb(6)[:, 128:256], a_q1.bitcast(BF16)[:, 128:256],
                 c_idb.ap())
        tq3 = tr(psb(6)[:, 256:384], a_q1.bitcast(BF16)[:, 256:384],
                 c_idb.ap(), waits=(vck,), inc=("TE", 1))
        e1 = vcopy(a_qf[:, sc * 128:(sc + 1) * 128], psb(6)[:, 0:128],
                   waits=(tq3,))
        e2 = vcopy(a_qf[:, 2048 + sc * 128:2048 + (sc + 1) * 128],
                   psb(6)[:, 128:256], waits=(tq3,))
        prevq = vcopy(a_kf[:, sc * 128:(sc + 1) * 128], psb(6)[:, 256:384],
                      waits=(tq3, e1, e2))
    p.barrier()

    a_sr = AR.f32(2048)
    a_se = AR.bf16(2048)
    a_at = AR.bf16(128)
    prevat = None
    for qh in range(2):
        for scq in range(16):
            nv = (scq + 1) * 128
            lsm = None
            for tg in range((nv + 511) // 512):
                n = min(512, nv - tg * 512)
                lsm = mm(ps(tg)[:, 0:n],
                         a_qf[:, qh * 2048 + scq * 128:
                              qh * 2048 + scq * 128 + 128],
                         a_kf[:, tg * 512:tg * 512 + n], True, True,
                         waits=(prevat,) if tg == 0 else ())
            evs = []
            for tg in range((nv + 511) // 512):
                n = min(512, nv - tg * 512)
                evs.append(vcopy(a_sr[:, tg * 512:tg * 512 + n],
                                 ps(tg)[:, 0:n], waits=(lsm,)))
            vmx4 = vop(lambda e, nv=nv: e.reduce_max(
                a_cnb[:, 27:28], a_sr[:, 0:nv], axis=mybir.AxisListType.X),
                waits=tuple(evs))
            vnm4 = vts(a_cnb[:, 28:29], a_cnb[:, 27:28],
                       -float(128) ** -0.5, None, ALU.mult, waits=(vmx4,))
            sex4 = sact(a_se[:, 0:nv], a_sr[:, 0:nv], AF.Exp,
                        bias=a_cnb[:, 28:29], scale=float(128) ** -0.5,
                        waits=(vnm4,))
            vzm = vtt(a_se[:, scq * 128:nv], a_se[:, scq * 128:nv],
                      c_tri.ap(), ALU.mult, waits=(sex4,))
            vsm4 = vop(lambda e, nv=nv: e.reduce_sum(
                a_cnb[:, 29:30], a_se[:, 0:nv], axis=mybir.AxisListType.X),
                waits=(vzm,))
            vrc4 = vop(lambda e: e.reciprocal(a_cnb[:, 30:31],
                                              a_cnb[:, 29:30]),
                       waits=(vsm4,))
            oacc = None
            prevev = None
            for tcq in range(scq + 1):
                ta = tr(psb(6)[:, 0:128],
                        a_se[:, tcq * 128:(tcq + 1) * 128], c_idb.ap(),
                        waits=(vzm, prevev), inc=("TE", 1))
                eva = vcopy(a_at, psb(6)[:, 0:128], waits=(ta,))
                p.wait("tensor", eva)
                if tcq == 0:
                    p.wait("tensor", prevat)
                oacc = mm(ps(7)[:, 0:128], a_at,
                          a_vt[:, tcq * 128:(tcq + 1) * 128],
                          tcq == 0, tcq == scq)
                prevev = oacc
            vo4 = vop(lambda e, qh=qh, scq=scq: e.tensor_scalar(
                a_am[:, scq * 256 + qh * 128:scq * 256 + qh * 128 + 128],
                ps(7)[:, 0:128], a_cnb[:, 30:31], None, op0=ALU.mult),
                waits=(oacc, vrc4))
            prevat = vo4
    p.barrier()
    AR.pop()
    da2 = p.dma("sync", rr(a2a_i, "(c p) m -> p c m", p=128),
                sb3(a_am), "SS", waits=(prevat,))
    p.wait("gpsimd", da2)
    cc2 = p.op("gpsimd", lambda e: e.collective_compute(
        "AllToAll", ALU.bypass, replica_groups=[list(range(NCORES))],
        ins=[a2a_i.opt()], outs=[a2a_o.opt()]), inc=("CC", 1))
    p.barrier()

    a_xm = AR.bf16(16 * 256)
    a_x1 = AR.bf16(16 * 256)
    a_dl = a_xm
    AR.push()
    a_ax = AR.bf16(16 * 256)
    a_blk = AR.bf16(512)
    prevbe = None
    bl = None
    for blk in range(8):
        bl = p.dma("sync", rr(a_blk, "p (h m) -> p h m", h=2),
                   rr(a2a_o[blk * 256:(blk + 1) * 256, :],
                      "(h p) m -> p h m", p=128), "LD",
                   waits=(cc2, bl, prevbe))
        tts = []
        for h2 in range(2):
            for m2 in range(2):
                tts.append(tr(psb(6)[:, (h2 * 2 + m2) * 128:
                                     (h2 * 2 + m2) * 128 + 128],
                              a_blk[:, h2 * 256 + m2 * 128:
                                    h2 * 256 + m2 * 128 + 128],
                              c_idb.ap(),
                              waits=(bl, prevbe) if h2 + m2 == 0 else (),
                              inc=("TE", 1) if h2 * 2 + m2 == 3 else None))
        prevbe = vcopy(
            rr(a_ax[:, blk * 512:(blk + 1) * 512], "p (c q) -> p c q", c=2),
            rr(psb(6)[:, 0:512], "p (h m q) -> p m (h q)", h=2, m=2)
            if False else
            rr(psb(6)[:, 0:512], "p (h m q) -> p m h q", h=2, m=2),
            waits=(tts[-1],))
    a_slm = AR.bf16(2176)
    prevxm = None
    for rh in range(2):
        ml = p.dma("sync", a_slm, ag1_i[rh * 128:(rh + 1) * 128, :], "LD",
                   waits=(prevxm,))
        for g in range(4):
            tt = None
            for j in range(4):
                dc = g * 4 + j
                w = [ml, prevxm] if j == 0 else []
                tt = tr(psb(7)[:, j * 128:(j + 1) * 128],
                        a_slm[:, dc * 128:(dc + 1) * 128], c_idb.ap(),
                        waits=w, inc=("TE", 1) if j == 3 else None)
            prevxm = vcopy(
                rr(a_xm, "p (c q) -> p c q", c=16)
                [:, g * 4:(g + 1) * 4, rh * 128:(rh + 1) * 128],
                sb3(psb(7)[:, 0:512], c=4), waits=(tt,))
    p.barrier()
    wtk4 = [None, None]
    wls4 = [None, None]
    pvx1 = [None, None, None, None]
    for si in range(8):
        buf = si % 2
        wtk4[buf] = p.dma("gpsimd",
                          rr(wst[buf].ap()[:, 0:16 * 256],
                             "p (c m) -> p c m", c=16),
                          rr(wo1, "(c p) m -> p c m", p=128)
                          [:, :, si * 256:(si + 1) * 256],
                          "LW" if buf == 0 else "LZ",
                          waits=(wtk4[buf], wls4[buf]))
        for msub in range(2):
            dm = si * 2 + msub
            bank = dm % 4
            ltw = None
            for cc in range(16):
                w = (wtk4[buf], prevbe, prevxm, pvx1[bank])                     if cc == 0 else ()
                ltw = mm(ps(bank)[:, 0:256],
                         wst[buf].ap()[:, cc * 256 + msub * 128:
                                       cc * 256 + msub * 128 + 128],
                         a_ax[:, cc * 256:(cc + 1) * 256], cc == 0, cc == 15,
                         waits=w)
            wls4[buf] = ltw
            vx1 = vtt(a_x1[:, dm * 256:(dm + 1) * 256],
                      a_xm[:, dm * 256:(dm + 1) * 256],
                      ps(bank)[:, 0:256], ALU.add, waits=(ltw,))
            pvx1[bank] = vx1
    p.barrier()
    AR.pop()
    AR.push()
    a_h2 = AR.bf16(16 * 256)
    a_sq4 = AR.bf16(256)
    msq = None
    sq5 = None
    for dm in range(16):
        sq5 = sact(a_sq4, a_x1[:, dm * 256:(dm + 1) * 256], AF.Square,
                   waits=(msq,))
        msq = mm(ps(4)[0:1, 0:256], c_ocb.ap(), a_sq4, dm == 0, dm == 15,
                 waits=(sq5,), inc=("TE", 1))
    vq5 = vts(srow.ap()[0:1, 128:384], ps(4)[0:1, 0:256], 1.0 / D, 1e-6,
              ALU.mult, ALU.add, waits=(msq,))
    vq5 = vop(lambda e: e.reciprocal(srow.ap()[0:1, 128:384],
                                     srow.ap()[0:1, 128:384]), waits=(vq5,))
    sq6 = sact(srow.ap()[0:1, 128:384], srow.ap()[0:1, 128:384], AF.Sqrt,
               waits=(vq5,))
    vq6 = vcopy(srow.ap().bitcast(BF16)[0:1, 768:1024],
                srow.ap()[0:1, 128:384], waits=(sq6,))
    brs = mm(ps(5)[:, 0:256], c_orb.ap(),
             srow.ap().bitcast(BF16)[0:1, 768:1024], True, True,
             waits=(vq6,))
    h2toks = []
    for dm in range(16):
        dl2 = p.dma("sync", a_cnb[:, 26:27],
                    ln2_c[dm * 128:(dm + 1) * 128, :], "LD",
                    waits=(h2toks[-1] if h2toks else None,))
        h2toks.append(vop(lambda e, dm=dm: e.scalar_tensor_tensor(
            a_h2[:, dm * 256:(dm + 1) * 256],
            a_x1[:, dm * 256:(dm + 1) * 256], a_cnb[:, 26:27],
            ps(5)[:, 0:256], op0=ALU.mult, op1=ALU.mult),
            waits=(dl2, brs)))
    tok_h2 = h2toks[-1]
    p.barrier()
    a_gu = AR.bf16(64 * 256)
    gu_toks = []
    wg_l = [None, None]
    wu_l = [None, None]
    wg_last = [None, None]
    wu_last = [None, None]
    for ic2 in range(32):
        half = ic2 % 2
        hof = half * 4096
        wg_l[half] = p.dma("gpsimd",
                           rr(wst[0].ap()[:, hof:hof + 4096],
                              "p (c m) -> p c m", c=16),
                           rr(w_gate, "(c p) m -> p c m", p=128)
                           [:, :, ic2 * 256:(ic2 + 1) * 256], "LW",
                           waits=(wg_l[half], wg_last[half]))
        wu_l[half] = p.dma("gpsimd",
                           rr(wst[1].ap()[:, hof:hof + 4096],
                              "p (c m) -> p c m", c=16),
                           rr(w_up, "(c p) m -> p c m", p=128)
                           [:, :, ic2 * 256:(ic2 + 1) * 256], "LZ",
                           waits=(wu_l[half], wu_last[half]))
        for isub in range(2):
            ii = ic2 * 2 + isub
            bg, bu = (0, 1) if ii % 2 == 0 else (2, 3)
            lg = lu = None
            for dc in range(16):
                w = ()
                if dc == 0:
                    w = (wg_l[half], wu_l[half], tok_h2,
                         gu_toks[-2] if len(gu_toks) >= 2 else None)
                lg = mm(ps(bg)[:, 0:256],
                        wst[0].ap()[:, hof + dc * 256 + isub * 128:
                                    hof + dc * 256 + isub * 128 + 128],
                        a_h2[:, dc * 256:(dc + 1) * 256], dc == 0, dc == 15,
                        waits=w)
                lu = mm(ps(bu)[:, 0:256],
                        wst[1].ap()[:, hof + dc * 256 + isub * 128:
                                    hof + dc * 256 + isub * 128 + 128],
                        a_h2[:, dc * 256:(dc + 1) * 256], dc == 0, dc == 15)
            wg_last[half] = lg
            wu_last[half] = lu
            sg5 = sact(a_gu[:, ii * 256:(ii + 1) * 256], ps(bg)[:, 0:256],
                       AF.Silu, waits=(lg, gu_toks[-4] if len(gu_toks) >= 4
                                       else None))
            gu_toks.append(vop(lambda e, ii=ii, bu=bu: e.tensor_tensor(
                a_gu[:, ii * 256:(ii + 1) * 256],
                a_gu[:, ii * 256:(ii + 1) * 256], ps(bu)[:, 0:256],
                op=ALU.mult), waits=(sg5, lu)))
    tok_gu = gu_toks[-1]
    p.barrier()
    wd_l = [None, None]
    wdl_last = [None, None]
    dl_toks = []
    for si in range(16):
        buf = si % 2
        wd_l[buf] = p.dma("gpsimd",
                          rr(wst[buf].ap(), "p (c m) -> p c m", c=64),
                          rr(w_down, "(c p) m -> p c m", p=128)
                          [:, :, si * 128:(si + 1) * 128],
                          "LW" if buf == 0 else "LZ",
                          waits=(wd_l[buf], wdl_last[buf]))
        dm = si
        bank = dm % 4
        ltd = None
        for icc in range(64):
            w = ()
            if icc == 0:
                w = (wd_l[buf], tok_gu,
                     dl_toks[-4] if len(dl_toks) >= 4 else None)
            ltd = mm(ps(bank)[:, 0:256],
                     wst[buf].ap()[:, icc * 128:(icc + 1) * 128],
                     a_gu[:, icc * 256:(icc + 1) * 256], icc == 0, icc == 63,
                     waits=w)
        wdl_last[buf] = ltd
        vpr = vtt(a_x1[:, dm * 256:(dm + 1) * 256],
                  a_x1[:, dm * 256:(dm + 1) * 256], ps(bank)[:, 0:256],
                  ALU.add, waits=(ltd, dl_toks[-1] if dl_toks else None))
        vd1 = vtt(a_x1[:, dm * 256:(dm + 1) * 256],
                  a_x1[:, dm * 256:(dm + 1) * 256],
                  a_xm[:, dm * 256:(dm + 1) * 256], ALU.subtract,
                  waits=(vpr,))
        dl_toks.append(vtt(a_dl[:, dm * 256:(dm + 1) * 256],
                           a_x1[:, dm * 256:(dm + 1) * 256], a_gb,
                           ALU.mult, waits=(vd1, vgb2)))
    tok_dl = dl_toks[-1]
    p.barrier()
    AR.pop()
    a_dt = AR.bf16(2 * 2048)
    a_pr = AR.bf16(2 * 2048)
    prevg = None
    for dm in range(16):
        tt = tr(psb(6)[:, 0:128], a_dl[:, dm * 256:dm * 256 + 128],
                c_idb.ap(), waits=(tok_dl, prevg))
        tt = tr(psb(6)[:, 128:256], a_dl[:, dm * 256 + 128:dm * 256 + 256],
                c_idb.ap(), inc=("TE", 1))
        prevg = vcopy(rr(a_dt, "p (h q) -> p h q", h=2)
                      [:, :, dm * 128:(dm + 1) * 128],
                      sb3(psb(6)[:, 0:256], c=2), waits=(tt,))
    for tc in range(16):
        tt = tr(psb(6)[:, 0:128], a_ptm[:, tc * 256:tc * 256 + 128],
                c_idb.ap(), waits=(prevg,) if tc == 0 else (prevg,))
        tt = tr(psb(6)[:, 128:256], a_ptm[:, tc * 256 + 128:tc * 256 + 256],
                c_idb.ap(), inc=("TE", 1))
        prevg = vcopy(rr(a_pr, "p (h q) -> p h q", h=2)
                      [:, :, tc * 128:(tc + 1) * 128],
                      sb3(psb(6)[:, 0:256], c=2), waits=(tt,))
    a_oo = AR.f32(2048)
    last_tt5 = None
    prevo = None
    prev_v5 = [None, None, None, None]
    for tc in range(16):
        tl = p.dma("sync", xin.ap(), xfm_v(tc), "LD", waits=(last_tt5,))
        evs = []
        for g in range(4):
            tt = None
            for j in range(4):
                dc = g * 4 + j
                w = [tl, prev_v5[g]]
                if g == 0 and j == 0:
                    w += [prevg, prevo]
                if j != 0:
                    w = []
                tt = tr(ps(2 + g)[:, j * 128:(j + 1) * 128],
                        sb3(xin.ap())[:, dc, :], c_idf.ap(),
                        waits=w, inc=("TE", 1) if j == 3 else None)
            evs.append(tt)
        last_tt5 = tt
        for dsl in range(4):
            sm = None
            for rh in range(2):
                w = []
                if rh == 0:
                    w = [prev_v5[dsl - 2] if dsl >= 2 else prev_v5[dsl],
                         prevg, prevo]
                sm = mm(ps(0 if dsl % 2 == 0 else 1)[:, 0:512],
                        a_pr[:, rh * 2048 + tc * 128:
                             rh * 2048 + tc * 128 + 128],
                        a_dt[:, rh * 2048 + dsl * 512:
                             rh * 2048 + dsl * 512 + 512],
                        rh == 0, rh == 1, waits=w)
            v5 = vop(lambda e, tc=tc, g=dsl: e.tensor_scalar(
                a_oo[:, g * 512:(g + 1) * 512], ps(2 + g)[:, 0:512],
                sg1[:, tc:tc + 1], mgc[:, tc:tc + 1], op0=ALU.mult,
                op1=ALU.subtract), waits=(evs[dsl], vsg1, dsg, prevo))
            v5 = vop(lambda e, g=dsl: e.tensor_tensor(
                a_oo[:, g * 512:(g + 1) * 512],
                a_oo[:, g * 512:(g + 1) * 512],
                ps(0 if g % 2 == 0 else 1)[:, 0:512], op=ALU.add),
                waits=(v5, sm))
            prev_v5[dsl] = v5
            prevo2 = v5
        p.wait("sync", prevo2)
        prevo = p.dma("sync", out[tc * 128:(tc + 1) * 128, :], a_oo, "SS")
    AR.pop()

    p.wait("sync", ("SS", p.semv["SS"]))
    p.wait("gpsimd", ("SG", p.semv["SG"]), ("LG", p.semv["LG"]),
           ("LW", p.semv["LW"]))

    p.playback()
    return nc, sorted(tap_out.keys())


def host_inputs(inputs, core):
    f = np.float32
    import ml_dtypes
    bf = ml_dtypes.bfloat16

    def c(a, dt=f):
        return np.ascontiguousarray(a, dtype=dt)

    i = inputs
    m = {}
    m["u"] = c(i["L0_attn_u"][core])
    m["w_in"] = c(i["L0_attn_W_in"])
    m["conv_wt"] = c(i["L0_attn_conv_w"].T)
    m["conv_bc"] = c(i["L0_attn_conv_b"].reshape(CONVD, 1))
    m["dtb_c"] = c(i["L0_attn_dt_bias"].reshape(NH, 1))
    m["alog_c"] = c(i["L0_attn_A_log"].reshape(NH, 1))
    m["d_row"] = c(i["L0_attn_D"].reshape(1, NH))
    m["ns_row"] = c(i["L0_attn_norm_scale"].reshape(1, DIN))
    m["w_out0"] = c(i["L0_attn_W_out"])
    m["a_all"] = c(np.transpose(i["L0_ffn_A"], (2, 0, 1)).reshape(D, 512))
    m["b_all"] = c(np.transpose(i["L0_ffn_B"], (0, 2, 1)).reshape(512, D))
    m["wr0"] = c(i["L0_ffn_Wr"].T)
    m["w_qkv_t"] = c(i["L1_attn_W_qkv"].T)
    m["w_out1_t"] = c(i["L1_attn_W_out"].T)
    ow = np.ones((D, 2), f)
    ow[:, 1] = i["L1_ffn_w_router"]
    m["onesw1"] = c(ow)
    m["wq_my"] = c(i["L1_ffn_wq"][:, core * 256:(core + 1) * 256])
    m["wk_my"] = c(i["L1_ffn_wk"][:, core * 128:(core + 1) * 128])
    m["wv_my"] = c(i["L1_ffn_wv"][:, core * 128:(core + 1) * 128])
    m["bq_my"] = c(i["L1_ffn_bq"][core * 256:(core + 1) * 256].reshape(1, 256))
    m["bk_my"] = c(i["L1_ffn_bk"][core * 128:(core + 1) * 128].reshape(1, 128))
    m["bv_my"] = c(i["L1_ffn_bv"][core * 128:(core + 1) * 128].reshape(1, 128))
    m["wo1"] = c(i["L1_ffn_wo"])
    m["w_gate"] = c(i["L1_ffn_w_gate"])
    m["w_up"] = c(i["L1_ffn_w_up"])
    m["w_down"] = c(i["L1_ffn_w_down"])
    m["ln1_c"] = c(i["L1_ffn_ln1_w"].reshape(D, 1))
    m["ln2_c"] = c(i["L1_ffn_ln2_w"].reshape(D, 1))
    m["ident_f"] = c(np.eye(128))
    m["ident_b"] = c(np.eye(128), bf)
    m["ident_t"] = c(np.tile(np.eye(128), (1, 4)), bf)
    ngm = np.zeros((128, 128), f)
    for l in range(128):
        ngm[l, l + 1:] = -1e9
    m["negmask"] = c(ngm, bf)
    m["tril_in"] = c(np.tril(np.ones((128, 128))), bf)
    m["iota_r"] = c(np.arange(256).reshape(1, 256))
    ind = np.zeros((NH, NH * CH), f)
    for h in range(NH):
        ind[h, h * CH:(h + 1) * CH] = 1.0
    m["ind64"] = c(ind, bf)
    pos = np.arange(T, dtype=np.float64)
    invf = 1.0 / (10000.0 ** (np.arange(0, 128, 2, dtype=np.float64) / 128.0))
    ang = pos[:, None] * invf[None, :]
    m["ropetab"] = c(np.concatenate([np.cos(ang), np.sin(ang)], axis=1), bf)
    m["ones_rb"] = c(np.ones((1, 128)), bf)
    m["ones_cb"] = c(np.ones((128, 1)), bf)
    return m


_BUILD_CACHE = {}


def run(inputs, taps=()):
    key = tuple(sorted(taps))
    if key not in _BUILD_CACHE:
        with allow_ici_all_to_all():
            _BUILD_CACHE[key] = build_program(taps)
    nc, tapnames = _BUILD_CACHE[key]
    in_maps = [host_inputs(inputs, c) for c in range(NCORES)]
    res = run_bass_kernel_spmd(nc, in_maps, core_ids=list(range(NCORES)))
    return res


def kernel(**inputs):
    res = run(inputs)
    return np.stack([res.results[c]["out"] for c in range(NCORES)], axis=0)
